# version 17
# baseline (speedup 1.0000x reference)
import contextlib
import numpy as np
import ml_dtypes
import concourse.bass as bass
import concourse.mybir as mybir
from concourse.bass_utils import run_bass_kernel_spmd

F32 = mybir.dt.float32
BF16 = mybir.dt.bfloat16
U32 = mybir.dt.uint32
AF = mybir.ActivationFunctionType
ALU = mybir.AluOpType
AX = mybir.AxisListType

DM = 4096
KVD = 1024
NKVH = 8
INTER = 11008
GUC = 2 * INTER
SEQ = 4096
TOWN = 2048
TB = 512
NQB = TOWN // TB
NSB = SEQ // TB
IC = DM // 128
GJ = INTER // 128
RT128 = 1.0 / float(np.sqrt(128.0))
EPS = 1e-5

ENGS = ("sync", "scalar", "vector", "tensor", "gpsimd")


DUMPCAT = {
    "binq": ("binq_d", lambda L: L["s_binq"][0:256, 0:512], [256, 512], "bf"),
    "bino": ("bino_d", lambda L: L["s_bino"][0:256, 0:512], [256, 512], "bf"),
    "w1g": ("w1g_d", lambda L: L["s_w1g"][0:256, 0:1024], [256, 1024], "bf"),
    "w1d": ("w1d_d", lambda L: L["s_w1d"][0:256, 0:1024], [256, 1024], "bf"),
    "zpg": ("zpg_d", lambda L: L["s_zpg"][:, :], [32, GUC], "bf"),
    "zpd": ("zpd_d", lambda L: L["s_zpd"][:, :], [GJ, DM], "bf"),
    "kscr": ("kscr_d", lambda L: L["s_kscr"][:, 0:512], [KVD, 512], "bf"),
    "kscr2": ("kscr2_d", lambda L: L["s_kscr"][:, 2048:2560], [KVD, 512], "bf"),
    "vscr": ("vscr_d", lambda L: L["s_vscr"][0, 0:512, :], [512, 128], "bf"),
    "qscr": ("qscr_d", lambda L: L["s_qscr"][0:DM, :], [DM, 512], "bf"),
    "ascr": ("ascr_d", lambda L: L["s_ascr"][0:DM, :], [DM, 512], "bf"),
}


def _finish(nc, plan, es, fence, dma, emit, s_ld, s_gp, dumps, L):
    fence()
    nc._dump_handles = {}
    for name in dumps:
        (dname, get, shape, dt) = DUMPCAT[name]
        dt = BF16 if dt == "bf" else F32
        dtn = nc.dram_tensor(dname, shape, dt, kind="ExternalOutput")
        nc._dump_handles[dname] = dtn
        dma("sync", s_ld, dtn[:, :], get(L))
    plan["gpsimd"].append(lambda e, v=s_ld.n, w=s_gp.n:
                          (e.wait_ge(s_ld.h, v), e.wait_ge(s_gp.h, w)))
    with nc.Block() as block:
        @block.sync
        def _(e):
            for f in plan["sync"]:
                f(e)

        @block.scalar
        def _(e):
            for f in plan["scalar"]:
                f(e)

        @block.vector
        def _(e):
            for f in plan["vector"]:
                f(e)

        @block.tensor
        def _(e):
            for f in plan["tensor"]:
                f(e)

        @block.gpsimd
        def _(e):
            for f in plan["gpsimd"]:
                f(e)
    es.close()
    return nc


class Sem:
    def __init__(self, es, nc, name):
        self.h = es.enter_context(nc.semaphore(name))
        self.n = 0

    def inc(self, k=1):
        self.n += k
        return self.n


def build_nc(upto="p4b", dumps=(), skip=()):
    PH = {"p0": 0, "p1": 1, "p4a": 2, "p4b": 3}[upto]
    skipset = [set(skip)]
    nc = bass.Bass("TRN2", target_bir_lowering=False)
    es = contextlib.ExitStack()
    plan = {e: [] for e in ENGS}

    d_xown = nc.dram_tensor("x_own", [TOWN, DM], F32, kind="ExternalInput")
    d_xoth = nc.dram_tensor("x_oth", [TOWN, DM], F32, kind="ExternalInput")
    d_qw = nc.dram_tensor("qw", [DM, DM], F32, kind="ExternalInput")
    d_kw = nc.dram_tensor("kw", [KVD, DM], F32, kind="ExternalInput")
    d_vw = nc.dram_tensor("vw", [KVD, DM], F32, kind="ExternalInput")
    d_ow = nc.dram_tensor("ow", [DM, DM], F32, kind="ExternalInput")
    d_qs = nc.dram_tensor("qs", [DM], F32, kind="ExternalInput")
    d_ks = nc.dram_tensor("ks", [KVD], F32, kind="ExternalInput")
    d_vs = nc.dram_tensor("vs", [KVD], F32, kind="ExternalInput")
    d_os = nc.dram_tensor("os_", [DM], F32, kind="ExternalInput")
    d_gqw = nc.dram_tensor("gqw", [512, GUC], U32, kind="ExternalInput")
    d_gqz = nc.dram_tensor("gqz", [32, GUC // 8], U32, kind="ExternalInput")
    d_gsc = nc.dram_tensor("gsc", [32, GUC], F32, kind="ExternalInput")
    d_dqw = nc.dram_tensor("dqw", [INTER // 8, DM], U32, kind="ExternalInput")
    d_dqz = nc.dram_tensor("dqz", [GJ, DM // 8], U32, kind="ExternalInput")
    d_dsc = nc.dram_tensor("dsc", [GJ, DM], F32, kind="ExternalInput")
    d_identf = nc.dram_tensor("identf", [128, 128], F32, kind="ExternalInput")
    d_band32 = nc.dram_tensor("band32", [128, 63], BF16, kind="ExternalInput")
    d_band86 = nc.dram_tensor("band86", [128, 171], BF16, kind="ExternalInput")
    d_shiftv = nc.dram_tensor("shiftv", [128, 1], U32, kind="ExternalInput")

    d_out = nc.dram_tensor("out", [TOWN, DM], F32, kind="ExternalOutput")

    s_binq = nc.dram_tensor("binq", [DM, DM], BF16)
    s_bink = nc.dram_tensor("bink", [DM, KVD], BF16)
    s_binv = nc.dram_tensor("binv", [DM, KVD], BF16)
    s_bino = nc.dram_tensor("bino", [DM, DM], BF16)
    s_w1g = nc.dram_tensor("w1g", [DM, GUC], BF16)
    s_w1d = nc.dram_tensor("w1d", [INTER, DM], BF16)
    d_ghl = nc.dram_tensor("ghl", [2, 32 * GUC], BF16)
    d_dhl = nc.dram_tensor("dhl", [2, 86 * DM], BF16)
    s_zpg = nc.dram_tensor("zpg", [32, GUC], BF16)
    s_ghi = nc.dram_tensor("ghi", [32, GUC], BF16)
    s_glo = nc.dram_tensor("glo", [32, GUC], BF16)
    s_dhi = nc.dram_tensor("dhi", [GJ, DM], BF16)
    s_dlo = nc.dram_tensor("dlo", [GJ, DM], BF16)
    s_zpd = nc.dram_tensor("zpd", [GJ, DM], BF16)
    s_kscr = nc.dram_tensor("kscr", [KVD, SEQ], BF16)
    s_vscr = nc.dram_tensor("vscr", [NKVH, SEQ, 128], BF16)
    s_qscr = nc.dram_tensor("qscr", [NQB * DM, TB], BF16)
    s_ascr = nc.dram_tensor("ascr", [NQB * DM, TB], BF16)

    s_ld = Sem(es, nc, "s_ld")
    s_ld4a = Sem(es, nc, "s_ld4a")
    s_ld4b = [Sem(es, nc, f"s_ld4b{i}") for i in range(NQB)]
    s_scst = Sem(es, nc, "s_scst")
    s_gp = Sem(es, nc, "s_gp")
    s_dqld = Sem(es, nc, "s_dqld")
    s_dqld2 = Sem(es, nc, "s_dqld2")
    s_dqst = Sem(es, nc, "s_dqst")
    s_dqst2 = Sem(es, nc, "s_dqst2")
    s_ldz = Sem(es, nc, "s_ldz")
    s_ldu = Sem(es, nc, "s_ldu")
    ring8 = [Sem(es, nc, f"s_r{i}") for i in range(8)]
    xb2 = [Sem(es, nc, f"s_xb{i}") for i in range(2)]
    wb4 = [Sem(es, nc, f"s_wb{i}") for i in range(4)]
    zs3 = [Sem(es, nc, f"s_z{i}") for i in range(3)]
    stk4 = [Sem(es, nc, f"s_stk{i}") for i in range(4)]
    stv2 = [Sem(es, nc, f"s_stv{i}") for i in range(2)]
    stq1 = Sem(es, nc, "s_stq")
    gpc4 = [Sem(es, nc, f"s_gpc{i}") for i in range(4)]
    gpa4 = [Sem(es, nc, f"s_gpa{i}") for i in range(4)]
    gpo2 = [Sem(es, nc, f"s_gpo{i}") for i in range(2)]
    s_te = Sem(es, nc, "s_te")
    s_ve = Sem(es, nc, "s_ve")
    s_sc = Sem(es, nc, "s_sc")
    ALL_SEMS = ([s_ld, s_ld4a, s_scst, s_gp, s_dqld, s_dqst, s_ldz, s_ldu, s_te, s_ve, s_sc]
                + s_ld4b + ring8 + xb2 + wb4 + zs3 + stk4 + stv2 + [stq1] + gpc4 + gpa4 + gpo2
                + [s_dqld2, s_dqst2])

    t_identf = es.enter_context(nc.sbuf_tensor("c_identf", [128, 128], F32))
    t_identb = es.enter_context(nc.sbuf_tensor("c_identb", [128, 128], BF16))
    t_onesr_f = es.enter_context(nc.sbuf_tensor("c_onesr_f", [1, 128], F32))
    t_onesc_b = es.enter_context(nc.sbuf_tensor("c_onesc_b", [128, 1], BF16))
    t_onesm = es.enter_context(nc.sbuf_tensor("c_onesm", [128, 128], BF16))
    t_onesr_b = es.enter_context(nc.sbuf_tensor("c_onesr_b", [1, 128], BF16))
    t_band32 = es.enter_context(nc.sbuf_tensor("c_band32", [128, 63], BF16))
    t_band86 = es.enter_context(nc.sbuf_tensor("c_band86", [128, 171], BF16))
    t_shiftv = es.enter_context(nc.sbuf_tensor("c_shiftv", [128, 1], U32))
    t_svec = es.enter_context(nc.sbuf_tensor("c_svec", [128, 80], F32))
    t_qsin = es.enter_context(nc.sbuf_tensor("c_qsin", [128, 80], F32))
    t_ebias = es.enter_context(nc.sbuf_tensor("c_ebias", [128, 4], F32))
    t_mk2 = es.enter_context(nc.sbuf_tensor("c_mk2", [1, 8], F32))
    t_mq2 = es.enter_context(nc.sbuf_tensor("c_mq2", [1, 4], F32))
    t_crow = es.enter_context(nc.sbuf_tensor("c_crow", [1, 4], F32))
    t_stat = es.enter_context(nc.sbuf_tensor("c_stat", [128, 24], F32))
    t_rd = es.enter_context(nc.sbuf_tensor("c_rd", [1, 512], F32))
    t_rd2 = es.enter_context(nc.sbuf_tensor("c_rd2", [1, 512], F32))
    t_eps = es.enter_context(nc.sbuf_tensor("c_eps", [128, 1], F32))

    AR = 190 * 1024
    t_ar = es.enter_context(nc.sbuf_tensor("arena", [128, AR], mybir.dt.uint8))
    h_f32 = t_ar.bitcast(F32)
    h_bf = t_ar.bitcast(BF16)
    h_u32 = t_ar.bitcast(U32)

    def vf(off, cols, rows=128):
        assert off % 4 == 0 and off + cols * 4 <= AR, (off, cols)
        return h_f32[0:rows, off // 4: off // 4 + cols]

    def vb(off, cols, rows=128):
        assert off % 2 == 0 and off + cols * 2 <= AR, (off, cols)
        return h_bf[0:rows, off // 2: off // 2 + cols]

    def vu(off, cols, rows=128):
        assert off % 4 == 0 and off + cols * 4 <= AR, (off, cols)
        return h_u32[0:rows, off // 4: off // 4 + cols]

    psh = [es.enter_context(nc.psum_tensor(f"ps{i}", [128, 512], F32)) for i in range(8)]
    psf = [h.ap() for h in psh]
    psb = [h.bitcast(BF16).ap() for h in psh]
    bankg = [None] * 8

    def emit(eng, fn, waits=(), sem=None, inc=0):
        n = sem.inc(inc) if (sem is not None and inc) else None
        wt = tuple((s, v) for (s, v) in waits if v)

        def f(e, fn=fn, wt=wt, sem=sem, inc=inc):
            for (sh, v) in wt:
                e.wait_ge(sh.h, v)
            ins = fn(e)
            if sem is not None and inc:
                ins.then_inc(sem.h, inc)
        plan[eng].append(f)
        return n

    def dma(eng, sem, out_ap, in_ap, waits=(), slow=False):
        return emit(eng, lambda e, o=out_ap, i=in_ap, sl=slow:
                    e.dma_start(out=o, in_=i, allow_slow_non_contiguous=sl),
                    waits=waits, sem=sem, inc=16)

    def bw(b):
        gclaim = bankg[b]
        bankg[b] = None
        return [gclaim] if gclaim is not None else []

    def ser(semq):
        if "serload" in skipset[0]:
            return [(semq, semq.n)]
        return []

    def fence():
        tg = tuple((s, s.n) for s in ALL_SEMS if s.n > 0)
        for eng in ENGS:
            def f(e, tg=tg):
                for (sh, v) in tg:
                    e.wait_ge(sh.h, v)
            plan[eng].append(f)
        for i in range(8):
            bankg[i] = None

    dbg_stores = []

    def dbgstore(name, ap, shape, dtype):
        t = nc.dram_tensor("dbg_" + name, shape, dtype, kind="ExternalOutput")
        n = dma("scalar", s_scst, t[:, :], ap,
                waits=[(s_ve, s_ve.n), (s_sc, s_sc.n), (s_te, s_te.n)])
        dbg_stores.append(n)

    dma("sync", s_ldu, t_identf.ap(), d_identf[:, :])
    dma("sync", s_ldu, t_band32.ap(), d_band32[:, :])
    dma("sync", s_ldu, t_band86.ap(), d_band86[:, :])
    dma("sync", s_ldu, t_shiftv.ap(), d_shiftv[:, :])
    dma("sync", s_ldu, t_qsin[:, 0:32], d_qs.ap().rearrange("(c p) -> p c", p=128), slow=True)
    dma("sync", s_ldu, t_qsin[:, 32:40], d_ks.ap().rearrange("(c p) -> p c", p=128), slow=True)
    dma("sync", s_ldu, t_qsin[:, 40:48], d_vs.ap().rearrange("(c p) -> p c", p=128), slow=True)
    n_consts = dma("sync", s_ldu, t_qsin[:, 48:80], d_os.ap().rearrange("(c p) -> p c", p=128), slow=True)
    emit("vector", lambda e: e.tensor_copy(t_identb.ap(), t_identf.ap()),
         waits=[(s_ldu, n_consts)], sem=s_ve, inc=1)
    emit("vector", lambda e: e.memset(t_onesr_f.ap(), 1.0), sem=s_ve, inc=1)
    emit("vector", lambda e: e.memset(t_onesc_b.ap(), 1.0), sem=s_ve, inc=1)
    emit("vector", lambda e: e.memset(t_onesm.ap(), 1.0), sem=s_ve, inc=1)
    emit("vector", lambda e: e.memset(t_onesr_b.ap(), 1.0), sem=s_ve, inc=1)
    n_base_ve = emit("vector", lambda e: e.memset(t_eps.ap(), EPS), sem=s_ve, inc=1)

    DQ0 = 160 * 1024
    O_DQZ = 131072
    O_DQU = 132480
    O_DQR = 143488
    O_DQW = 154496
    ZON = [133120, 133120 + 13760]
    dqc = {"ve": 0, "scst": 0}
    dqp = {"g": 0, "guard": [None, None], "mmg": [None, None]}
    dq_mode = ["pe"]
    dqz = [{"ve": None, "st": None}, {"ve": None, "st": None}]
    dq_sems = [(s_dqld, s_dqst), (s_dqld2, s_dqst2)]
    dq_zi = [0]

    dequant_tasks = []

    def zp_task(osl, gu):
        def gen():
            if gu:
                pw, width = 344, 2752
                src_qz = d_gqz.reshape([128, 688])[:, osl * pw:(osl + 1) * pw]
                src_sc = d_gsc.reshape([128, 5504])[:, osl * width:(osl + 1) * width]
                dst = s_zpg.reshape([128, 5504])[:, osl * width:(osl + 1) * width]
            else:
                pw, width = 344, 2752
                src_qz = d_dqz.reshape([128, 344])[:, :]
                src_sc = d_dsc.reshape([128, 2752])[:, :]
                dst = s_zpd.reshape([128, 2752])[:, :]
            qz = vu(O_DQZ, pw)
            znib = vu(O_DQU, width)
            znib3 = znib.rearrange("p (c k) -> p c k", k=8)
            scv = vf(O_DQR, width)
            outb = vb(O_DQW, width)
            wz = [(s_ve, dqc["ve"]), (s_dqst, dqc["scst"])]
            n1 = dma("gpsimd", s_dqld, qz, src_qz, waits=wz)
            n2 = dma("gpsimd", s_dqld, scv, src_sc)
            for k in range(8):
                emit("vector",
                     lambda e, o=znib3[:, :, k:k + 1], i=qz.unsqueeze(2), sh=4 * k:
                     e.tensor_scalar(o, i, float(sh), 15.0,
                                     ALU.logical_shift_right, ALU.bitwise_and),
                     waits=[(s_dqld, n2), (s_ve, s_ve.n)], sem=s_ve, inc=1)
            nv = emit("vector",
                      lambda e, o=outb, i=znib, s=scv:
                      e.scalar_tensor_tensor(o, i, -7.0, s, ALU.add, ALU.mult),
                      waits=[(s_dqld, n2), (s_ve, s_ve.n), (s_dqst, dqc["scst"])],
                      sem=s_ve, inc=1)
            ns = dma("scalar", s_dqst, dst, outb, waits=[(s_ve, nv)])
            dqc["ve"] = nv
            dqc["scst"] = ns
            for z in range(2):
                dqz[z]["ve"] = (s_ve, nv)
                dqz[z]["st"] = (s_dqst, ns)
        return gen

    def hl_task(h, gu):
        def gen():
            if gu:
                src = d_gsc.reshape([128, 5504])[:, h * 2752:(h + 1) * 2752]
                dsth = d_ghl.reshape([2, 128, 5504])[0, :, h * 2752:(h + 1) * 2752]
                dstl = d_ghl.reshape([2, 128, 5504])[1, :, h * 2752:(h + 1) * 2752]
            else:
                src = d_dsc.reshape([128, 2752])[:, :]
                dsth = d_dhl.reshape([2, 128, 2752])[0, :, :]
                dstl = d_dhl.reshape([2, 128, 2752])[1, :, :]
            scf = vf(O_DQU, 2752)
            hib = vb(O_DQR, 2752)
            lob = vb(O_DQW, 2752)
            wz = [(s_ve, dqc["ve"]), (s_dqst, dqc["scst"])]
            n1 = dma("gpsimd", s_dqld, scf, src, waits=wz)
            nh = emit("vector", lambda e: e.tensor_copy(hib, scf),
                      waits=[(s_dqld, n1), (s_ve, s_ve.n), (s_dqst, dqc["scst"])],
                      sem=s_ve, inc=1)
            nl = emit("vector",
                      lambda e: e.tensor_tensor(lob, scf, hib, ALU.subtract),
                      waits=[(s_ve, s_ve.n)], sem=s_ve, inc=1)
            dma("scalar", s_dqst, dsth, hib, waits=[(s_ve, nh)])
            ns2 = dma("scalar", s_dqst, dstl, lob, waits=[(s_ve, nl)])
            dqc["ve"] = nl
            dqc["scst"] = ns2
            for z in range(2):
                dqz[z]["ve"] = (s_ve, nl)
                dqz[z]["st"] = (s_dqst, ns2)
        return gen

    def w1_task(it, osl, gu):
        def gen():
            if gu:
                width = 1376
                src_w = d_gqw[it * 16:(it + 1) * 16, osl * width:(osl + 1) * width]
                src_sc = d_gsc[it:it + 1, osl * width:(osl + 1) * width]
                dst = s_w1g[it * 128:(it + 1) * 128, osl * width:(osl + 1) * width]
                hs = it * GUC + osl * width
                hl_src = d_ghl[:, hs:hs + width]
            else:
                width = 1024
                src_w = d_dqw[it * 16:(it + 1) * 16, osl * width:(osl + 1) * width]
                src_sc = d_dsc[it:it + 1, osl * width:(osl + 1) * width]
                dst = s_w1d[it * 128:(it + 1) * 128, osl * width:(osl + 1) * width]
                hs = it * DM + osl * width
                hl_src = d_dhl[:, hs:hs + width]
            z = dq_zi[0] % 2
            dq_zi[0] += 1
            sld, sst = dq_sems[z]
            qbuf = vu(ZON[z], width)
            w1 = vb(ZON[z] + 5504, width)
            hib = vb(ZON[z] + 8256, width, 1)
            lob = vb(ZON[z] + 11008, width, 1)
            wz = [w for w in (dqz[z]["ve"], dqz[z]["st"]) if w is not None]
            prev_store = dqz[z]["st"]
            dma("gpsimd", sld, qbuf,
                src_w.unsqueeze(1).broadcast_to([16, 8, width]), waits=wz)
            dma("gpsimd", sld, hib, hl_src[0:1, :])
            n2 = dma("gpsimd", sld, lob, hl_src[1:2, :])
            emit("vector",
                 lambda e, o=qbuf, i=qbuf:
                 e.tensor_scalar(o, i, t_shiftv.ap(), 15.0,
                                 ALU.logical_shift_right, ALU.bitwise_and),
                 waits=[(sld, n2), (s_ldu, n_consts)], sem=s_ve, inc=1)
            last_ve = None
            c0 = 0
            ch = 0
            while c0 < width:
                cw = min(512, width - c0)
                sl = dqp["g"] % 2
                b = 6 + sl
                dqp["g"] += 1
                gg = dqp["guard"][sl]
                mmw = [(sld, n2), (s_ve, n_base_ve)] + ([gg] if gg else [])
                emit("tensor",
                     lambda e, b=b, c0=c0, cw=cw:
                     e.matmul(psf[b][:, 0:cw], t_onesr_b.ap(),
                              hib[0:1, c0:c0 + cw], start=True, stop=False),
                     waits=mmw, sem=s_te, inc=0)
                n_mm = emit("tensor",
                            lambda e, b=b, c0=c0, cw=cw:
                            e.matmul(psf[b][:, 0:cw], t_onesr_b.ap(),
                                     lob[0:1, c0:c0 + cw], start=False, stop=True),
                            sem=s_te, inc=1)
                stw = [(s_te, n_mm), (s_ve, s_ve.n)]
                if ch == 0 and prev_store:
                    stw.append(prev_store)
                last_ve = emit("vector",
                               lambda e, c0=c0, cw=cw, b=b:
                               e.scalar_tensor_tensor(w1[:, c0:c0 + cw],
                                                      qbuf[:, c0:c0 + cw], -8.0,
                                                      psf[b][:, 0:cw],
                                                      ALU.add, ALU.mult),
                               waits=stw, sem=s_ve, inc=1)
                dqp["guard"][sl] = (s_ve, last_ve)
                c0 += cw
                ch += 1
            ns = dma("scalar", sst, dst, w1, waits=[(s_ve, last_ve)])
            dqz[z]["ve"] = (s_ve, last_ve)
            dqz[z]["st"] = (sst, ns)
        return gen

    for i in range(2):
        dequant_tasks.append(zp_task(i, True))
    dequant_tasks.append(zp_task(0, False))
    dequant_tasks.append(hl_task(0, True))
    dequant_tasks.append(hl_task(1, True))
    dequant_tasks.append(hl_task(0, False))
    n_gu = 16
    n_dn = 4
    for it in range(IC):
        for osl in range(n_gu):
            dequant_tasks.append(w1_task(it, osl, True))
    for it in range(GJ):
        for osl in range(n_dn):
            dequant_tasks.append(w1_task(it, osl, False))
    if "dq" in skip:
        dequant_tasks.clear()
    for tok in skip:
        if tok.startswith("dqn"):
            del dequant_tasks[int(tok[3:]):]
    dq_idx = [0]

    def drain_dq(k):
        while k > 0 and dq_idx[0] < len(dequant_tasks):
            dequant_tasks[dq_idx[0]]()
            dq_idx[0] += 1
            k -= 1

    O_WBIN = 0
    O_SGN = 65536
    O_BSTG = 98304
    stg = vb(O_BSTG, IC * 512)
    stg3 = stg.rearrange("p (c n) -> p c n", c=IC)
    wspec = [(d_qw, s_binq, 32, 0, True),
             (d_kw, s_bink, 8, 32, False),
             (d_vw, s_binv, 8, 40, False),
             (d_ow, s_bino, 32, 48, False)]
    stg_guard = [None]
    sgn_guard = [None]
    n_bin_stores = 0
    if "bin" in skip:
        wspec = []
    for (dsrc, ddst, noc, svbase, isq) in wspec:
        for ocg in range(noc // 4):
            lw = []
            for k in range(4):
                oc = ocg * 4 + k
                wv = vf(O_WBIN + k * 16384, DM)
                lw.append((wb4[k], dma(["sync", "gpsimd"][k % 2], wb4[k], wv,
                                       dsrc[oc * 128:(oc + 1) * 128, :],
                                       waits=[(s_sc, s_sc.n), (s_ldu, n_consts)])))
            nsg = []
            for k in range(4):
                oc = ocg * 4 + k
                wv = vf(O_WBIN + k * 16384, DM)
                sg = vb(O_SGN + k * 8192, DM)
                sw = [lw[k]]
                if k == 0 and sgn_guard[0] is not None:
                    sw.append(sgn_guard[0])
                    sgn_guard[0] = None
                nsg.append(emit("scalar", lambda e, o=sg, i=wv: e.activation(o, i, AF.Sign),
                                waits=sw, sem=s_sc, inc=1))
                emit("vector",
                     lambda e, i=wv: e.reduce_sum(t_stat[:, 8:9], i, AX.X,
                                                  apply_absolute_value=True),
                     waits=[lw[k], (s_ve, s_ve.n)], sem=s_ve, inc=1)
                emit("vector",
                     lambda e: e.tensor_scalar(t_stat[:, 9:10], t_stat[:, 8:9], 1.0 / DM,
                                               1e-8, ALU.mult, ALU.max),
                     waits=[(s_ve, s_ve.n)], sem=s_ve, inc=1)
                emit("vector",
                     lambda e, col=svbase + oc, m=(RT128 if isq else 1.0):
                     e.tensor_scalar(t_svec[:, col:col + 1], t_stat[:, 9:10],
                                     t_qsin[:, col:col + 1], m, ALU.mult, ALU.mult),
                     waits=[(s_ve, s_ve.n)], sem=s_ve, inc=1)
            nev = None
            for icb in range(IC):
                if icb % 3 == 0:
                    drain_dq(0 if "nodq0" in skip else 1)
                b = 4 + (icb % 2)
                tws = bw(b) + [(s_sc, nsg[3]), (s_ve, n_base_ve)]
                if icb == 0 and stg_guard[0] is not None:
                    tws.append(stg_guard[0])
                    stg_guard[0] = None
                ntp = None
                for k in range(4):
                    sg = vb(O_SGN + k * 8192, DM)
                    ntp = emit("tensor",
                               lambda e, b=b, k=k, i=sg, ic=icb:
                               e.transpose(psb[b][:, k * 128:(k + 1) * 128],
                                           i[:, ic * 128:(ic + 1) * 128], t_identb.ap()),
                               waits=(tws if k == 0 else []), sem=s_te,
                               inc=(1 if k == 3 else 0))
                nev = emit("vector",
                           lambda e, b=b, ic=icb:
                           e.tensor_copy(stg[:, ic * 512:(ic + 1) * 512], psb[b][:, 0:512]),
                           waits=[(s_te, ntp)], sem=s_ve, inc=1)
                bankg[b] = (s_ve, nev)
            sgn_guard[0] = (s_te, s_te.n)
            dst = ddst[:, ocg * 512:(ocg + 1) * 512].rearrange("(c p) n -> p c n", p=128)
            n_bin_stores = dma("scalar", s_scst, dst, stg3, waits=[(s_ve, nev)])
            stg_guard[0] = (s_scst, n_bin_stores)

    if PH == 0:
        drain_dq(10 ** 9)
        return _finish(nc, plan, es, fence, dma, emit, s_ld, s_gp, dumps, locals())
    fence()
    O_XLNT = 0
    O_XBUF = 32768
    O_XLNB = 65536
    O_XSQ = 116736
    O_WRNG = 90112
    O_KSTG = 106496
    O_VSTG = 110592
    O_QSTG = 112640

    XLNT_O = [O_XLNT, 161792]
    def xlnt_t(ic, h):
        return vb(XLNT_O[h] + ic * 1024, 512)
    def xlnt3h(h):
        return vb(XLNT_O[h], IC * 512).rearrange("p (c n) -> p c n", c=IC)
    xlnt_guard = [None, None]
    wring = [vb(O_WRNG + i * 1024, 512) for i in range(8)]
    wring_g = [None] * 8
    wring_i = [0]
    kstg = [vb(O_KSTG + i * 1024, 512) for i in range(4)]
    kstg_g = [None] * 4
    vstg = [vb(O_VSTG + i * 1024, 512) for i in range(2)]
    vstg_g = [None] * 2
    qstg = [vb(O_QSTG + i * 1024, 512) for i in range(4)]
    qstg_g = [None] * 4
    xbuf_g = [None, None]
    xlnb_g = [None, None]
    n_q_store = [0]

    XLNB_O = [O_XLNB, O_XLNB + 8192, 81920, 98304]
    xlnb_g4 = [None] * 4

    def emit_ln_ve(sb):
        own = sb < NQB
        xsrc = d_xown if own else d_xoth
        r0 = (sb % NQB) * TB
        naps = []
        for tt in range(4):
            xv = vf(O_XBUF + (tt % 2) * 16384, DM)
            xlv = vb(XLNB_O[tt], DM)
            xq = vf(O_XSQ, DM)
            wts = [(s_ldu, n_consts)]
            if xbuf_g[tt % 2] is not None:
                wts.append(xbuf_g[tt % 2])
                xbuf_g[tt % 2] = None
            nldx = dma("sync", xb2[tt % 2], xv,
                       xsrc[r0 + tt * 128: r0 + (tt + 1) * 128, :], waits=wts)
            emit("scalar", lambda e, i=xv, o=xq:
                 e.activation(o, i, AF.Copy, accum_out=t_stat[:, 0:1]),
                 waits=[(xb2[tt % 2], nldx), (s_sc, s_sc.n)], sem=s_sc, inc=1)
            n_sqs = emit("scalar", lambda e, i=xv, o=xq:
                         e.activation(o, i, AF.Square, accum_out=t_stat[:, 1:2]),
                         waits=[(s_sc, s_sc.n)], sem=s_sc, inc=1)
            emit("vector", lambda e: e.tensor_scalar(t_stat[:, 2:3], t_stat[:, 0:1],
                                                     1.0 / DM, None, ALU.mult),
                 waits=[(s_sc, n_sqs), (s_ve, s_ve.n)], sem=s_ve, inc=1)
            emit("vector", lambda e: e.tensor_scalar(t_stat[:, 3:4], t_stat[:, 1:2],
                                                     1.0 / DM, None, ALU.mult),
                 waits=[(s_ve, s_ve.n)], sem=s_ve, inc=1)
            emit("vector", lambda e: e.tensor_tensor(t_stat[:, 4:5], t_stat[:, 2:3],
                                                     t_stat[:, 2:3], ALU.mult),
                 waits=[(s_ve, s_ve.n)], sem=s_ve, inc=1)
            emit("vector", lambda e: e.tensor_tensor(t_stat[:, 5:6], t_stat[:, 3:4],
                                                     t_stat[:, 4:5], ALU.subtract),
                 waits=[(s_ve, s_ve.n)], sem=s_ve, inc=1)
            nvar = emit("vector", lambda e: e.tensor_scalar(t_stat[:, 5:6], t_stat[:, 5:6],
                                                            0.0, None, ALU.max),
                        waits=[(s_ve, s_ve.n)], sem=s_ve, inc=1)
            nsq = emit("scalar", lambda e: e.activation(t_stat[:, 6:7], t_stat[:, 5:6],
                                                        AF.Sqrt, bias=t_eps[:, 0:1]),
                       waits=[(s_ve, nvar)], sem=s_sc, inc=1)
            emit("vector", lambda e: e.reciprocal(t_stat[:, 7:8], t_stat[:, 6:7]),
                 waits=[(s_sc, nsq), (s_ve, s_ve.n)], sem=s_ve, inc=1)
            emit("vector", lambda e: e.tensor_tensor(t_stat[:, 8:9], t_stat[:, 2:3],
                                                     t_stat[:, 7:8], ALU.mult),
                 waits=[(s_ve, s_ve.n)], sem=s_ve, inc=1)
            emit("vector", lambda e: e.tensor_scalar(t_stat[:, 9:10], t_stat[:, 8:9],
                                                     -1.0, None, ALU.mult),
                 waits=[(s_ve, s_ve.n)], sem=s_ve, inc=1)
            wla = []
            if xlnb_g4[tt] is not None:
                wla.append(xlnb_g4[tt])
                xlnb_g4[tt] = None
            nap = emit("vector", lambda e, i=xv, o=xlv:
                       e.tensor_scalar(o, i, t_stat[:, 7:8], t_stat[:, 9:10],
                                       ALU.mult, ALU.add),
                       waits=wla + [(s_ve, s_ve.n)], sem=s_ve, inc=1)
            xbuf_g[tt % 2] = (s_ve, nap)
            naps.append(nap)
        return naps

    def emit_ln_pe(half, naps):
        n_xlnt_ready = None
        for tt in range(4):
            xlv = vb(XLNB_O[tt], DM)
            for q4 in range(8):
                b = 4 + (q4 % 2)
                tws = bw(b) + [(s_ve, naps[tt])]
                if tt == 0 and q4 == 0:
                    tws.append((s_te, s_te.n))
                    if xlnt_guard[half] is not None:
                        tws.append(xlnt_guard[half])
                        xlnt_guard[half] = None
                ntp = None
                for j in range(4):
                    icb = q4 * 4 + j
                    ntp = emit("tensor",
                               lambda e, b=b, j=j, ic=icb, i=xlv:
                               e.transpose(psb[b][:, j * 128:(j + 1) * 128],
                                           i[:, ic * 128:(ic + 1) * 128], t_identb.ap()),
                               waits=(tws if j == 0 else []), sem=s_te,
                               inc=(1 if j == 3 else 0))
                nev = emit("vector",
                           lambda e, b=b, q4=q4, tt=tt, h=half:
                           e.tensor_copy(xlnt3h(h)[:, q4 * 4:(q4 + 1) * 4,
                                                   tt * 128:(tt + 1) * 128],
                                         psb[b][:, 0:512]),
                           waits=[(s_te, ntp)], sem=s_ve, inc=1)
                bankg[b] = (s_ve, nev)
                n_xlnt_ready = nev
            xlnb_g4[tt] = (s_te, s_te.n)
        return n_xlnt_ready

    nxr_cur = emit_ln_pe(0, emit_ln_ve(0))
    for sb in range(1 if "p1one" in skip else NSB):
        own = sb < NQB
        half = sb % 2
        drain_dq(2)

        def proj(bsrc, noc, svbase, store_fn, nxr=nxr_cur, h=half):
            for og in range(noc // 4):
                nmm = [None] * 4
                for icb0 in range(0, IC, 2):
                    if icb0 % 4 == 0:
                        drain_dq(1)
                    nlds = []
                    slots = []
                    for icb in (icb0, icb0 + 1):
                        slot = wring_i[0] % 8
                        wring_i[0] += 1
                        slots.append(slot)
                        wwt = []
                        if wring_g[slot] is not None:
                            wwt.append(wring_g[slot])
                            wring_g[slot] = None
                        nlds.append((ring8[slot],
                                     dma(["sync", "gpsimd"][(icb0 // 2) % 2],
                                         ring8[slot], wring[slot],
                                         bsrc[icb * 128:(icb + 1) * 128,
                                              og * 512:(og + 1) * 512],
                                         waits=wwt)))
                    mi = 0
                    for icb in (icb0, icb0 + 1):
                        slot = slots[icb - icb0]
                        for k in range(4):
                            tws = list(nlds) if mi == 0 else []
                            if icb0 == 0:
                                tws = tws + bw(k) + ([(s_ve, nxr)] if mi == 0 else [])
                            nmm[k] = emit("tensor",
                                          lambda e, k=k, s=slot, ic=icb, h=h:
                                          e.matmul(psf[k][:, 0:512],
                                                   wring[s][:, k * 128:(k + 1) * 128],
                                                   xlnt_t(ic, h),
                                                   start=(ic == 0),
                                                   stop=(ic == IC - 1)),
                                          waits=tws, sem=s_te,
                                          inc=(1 if mi == 7 else 0))
                            mi += 1
                    for slot in slots:
                        wring_g[slot] = (s_te, nmm[3])
                for k in range(4):
                    store_fn(og * 4 + k, k, nmm[3])

        def kvstore_mk(which):
            def f(oc, k, nmmlast):
                sslot = oc % 4
                swt = [(s_te, nmmlast)]
                if kstg_g[sslot] is not None:
                    swt.append(kstg_g[sslot])
                    kstg_g[sslot] = None
                sv = 32 if which == "k" else 40
                nev = emit("scalar",
                           lambda e, k=k, oc=oc, o=kstg[sslot], sv=sv:
                           e.activation(o, psf[k][:, 0:512], AF.Copy,
                                        scale=t_svec[:, sv + oc: sv + oc + 1]),
                           waits=swt, sem=s_sc, inc=1)
                bankg[k] = (s_sc, nev)
                if which == "k":
                    n = dma("scalar", stk4[sslot],
                            s_kscr[oc * 128:(oc + 1) * 128, sb * 512:(sb + 1) * 512],
                            kstg[sslot])
                    kstg_g[sslot] = (stk4[sslot], n)
                else:
                    b = 4 + (oc % 2)
                    tws = bw(b) + [(s_sc, nev)]
                    ntp = None
                    for j in range(4):
                        ntp = emit("tensor",
                                   lambda e, b=b, j=j, i=kstg[sslot]:
                                   e.transpose(psb[b][:, j * 128:(j + 1) * 128],
                                               i[:, j * 128:(j + 1) * 128], t_identb.ap()),
                                   waits=(tws if j == 0 else []), sem=s_te,
                                   inc=(1 if j == 3 else 0))
                    kstg_g[sslot] = (s_te, ntp)
                    vslot = oc % 2
                    vwt = [(s_te, ntp)]
                    if vstg_g[vslot] is not None:
                        vwt.append(vstg_g[vslot])
                        vstg_g[vslot] = None
                    nev2 = emit("vector",
                                lambda e, b=b, o=vstg[vslot]:
                                e.tensor_copy(o, psb[b][:, 0:512]),
                                waits=vwt, sem=s_ve, inc=1)
                    bankg[b] = (s_ve, nev2)
                    dstv = s_vscr[oc, sb * 512:(sb + 1) * 512, :]\
                        .rearrange("(j p) d -> p j d", p=128)
                    srcv = vstg[vslot].rearrange("p (j d) -> p j d", j=4)
                    n = dma("scalar", stv2[vslot], dstv, srcv, waits=[(s_ve, nev2)])
                    vstg_g[vslot] = (stv2[vslot], n)
            return f

        proj(s_bink, 8, 32, kvstore_mk("k"))
        naps_next = None
        if "p1one" not in skip and sb + 1 < NSB:
            naps_next = emit_ln_ve(sb + 1)
        proj(s_binv, 8, 40, kvstore_mk("v"))

        if own:
            qb = sb
            pend_q = []

            def qstore(oc, k, nmmlast, qb=qb, pend=pend_q):
                sslot = oc % 4
                swt = [(s_te, nmmlast)]
                if qstg_g[sslot] is not None:
                    swt.append(qstg_g[sslot])
                    qstg_g[sslot] = None
                nev = emit("scalar",
                           lambda e, k=k, oc=oc, o=qstg[sslot]:
                           e.activation(o, psf[k][:, 0:512], AF.Copy,
                                        scale=t_svec[:, oc: oc + 1]),
                           waits=swt, sem=s_sc, inc=1)
                bankg[k] = (s_sc, nev)
                pend.append((oc, sslot, nev))
                if len(pend) == 4:
                    og = pend[0][0] // 4
                    dstq = s_qscr[qb * DM + og * 512: qb * DM + (og + 1) * 512, :]\
                        .rearrange("(c p) n -> p c n", p=128)
                    srcq = vb(O_QSTG, 4 * 512).rearrange("p (c n) -> p c n", c=4)
                    n = dma("scalar", stq1, dstq, srcq, waits=[(s_sc, pend[-1][2])])
                    for (_, ss, _) in pend:
                        qstg_g[ss] = (stq1, n)
                    n_q_store[0] = n
                    pend.clear()
            proj(s_binq, 32, 0, qstore)
        if sb == 0 and "xlndbg" in skip:
            dbgstore("xlnt", vb(O_XLNT, IC * 512), [128, IC * 512], BF16)
            dbgstore("stat", t_stat[:, 0:24], [128, 24], F32)
        xlnt_guard[half] = (s_te, s_te.n)
        if naps_next is not None:
            nxr_cur = emit_ln_pe((sb + 1) % 2, naps_next)


    if PH == 1:
        drain_dq(10 ** 9)
        return _finish(nc, plan, es, fence, dma, emit, s_ld, s_gp, dumps, locals())
    drain_dq(10 ** 9)
    fence()
    O_KBUF = 0
    O_VBUF = 65536
    O_QRNG = 131072
    O_ABUF = 139264
    O_PBUF = 143360
    O_RBUF = 147456
    O_SQS = 151552
    O_KNS = 152576

    def kb_t(h):
        return vb(O_KBUF + h * 8192, SEQ)

    def vb_t(h, c):
        return vb(O_VBUF + (h * IC + c) * 256, 128)

    def qr_t(slot, hh):
        return vb(O_QRNG + slot * 4096 + hh * 1024, 512)
    qrng_g = [None] * 2
    abuf = [vb(O_ABUF + i * 1024, 512) for i in range(4)]
    abuf_g = [None] * 4
    pbuf = [vb(O_PBUF + i * 1024, 512) for i in range(4)]
    pbuf_g = [None] * 4
    rbuf = [vf(O_RBUF + i * 2048, 512) for i in range(2)]
    rbuf_g = [None] * 2
    sqs = vb(O_SQS, 512)
    kns = vb(O_KNS, DM)

    dma("sync", s_ld4a, vb(O_KBUF, NKVH * SEQ).rearrange("p (h n) -> p h n", h=NKVH),
        s_kscr.ap().rearrange("(h p) n -> p h n", p=128))
    n_vload = dma("sync", s_ld4a,
                  vb(O_VBUF, NKVH * IC * 128).rearrange("p (h c d) -> p h c d",
                                                        h=NKVH, c=IC),
                  s_vscr.ap().rearrange("h (c p) d -> p h c d", p=128))

    kns_guard = [None]
    if "nonorms" in skip:
        emit("vector", lambda e: e.memset(t_ebias.ap(), -8.0), sem=s_ve, inc=1)
    for kvh in range(NKVH) if "nonorms" not in skip else []:
        sw = [(s_ld4a, n_vload)]
        if kns_guard[0] is not None:
            sw.append(kns_guard[0])
            kns_guard[0] = None
        nqq = emit("scalar", lambda e, h=kvh: e.activation(kns, kb_t(h), AF.Square),
                   waits=sw, sem=s_sc, inc=1)
        nt = None
        for sl in range(8):
            tws = [(s_sc, nqq)] + bw(6)
            nt = emit("tensor",
                      lambda e, sl=sl: e.matmul(psf[6][:, 0:512], t_onesm.ap(),
                                                kns[:, sl * 512:(sl + 1) * 512],
                                                start=True, stop=True),
                      waits=tws, sem=s_te, inc=1)
            if sl == 0:
                nv = emit("vector",
                          lambda e, h=kvh:
                          e.reduce_max(t_mk2[0:1, h:h + 1], psf[6][0:1, 0:512], AX.X),
                          waits=[(s_te, nt), (s_ve, s_ve.n)], sem=s_ve, inc=1)
            else:
                emit("vector",
                     lambda e: e.reduce_max(t_stat[0:1, 16:17], psf[6][0:1, 0:512], AX.X),
                     waits=[(s_te, nt), (s_ve, s_ve.n)], sem=s_ve, inc=1)
                nv = emit("vector",
                          lambda e, h=kvh:
                          e.tensor_tensor(t_mk2[0:1, h:h + 1], t_mk2[0:1, h:h + 1],
                                          t_stat[0:1, 16:17], ALU.max),
                          waits=[(s_ve, s_ve.n)], sem=s_ve, inc=1)
            bankg[6] = (s_ve, nv)
        kns_guard[0] = (s_te, nt)

    n_ascr_store = [0]
    sqs_guard = [None]
    crow_guard = [None]
    ebias_guard = [None]
    for qb in range(1 if ("p4aone" in skip or "p4aqb1" in skip) else NQB):
        for kvh in range(1 if "p4aone" in skip else (2 if "p4akv2" in skip else NKVH)):
            qslot = (qb * NKVH + kvh) % 2
            qwt = [(stq1, n_q_store[0])]
            if qrng_g[qslot] is not None:
                qwt.append(qrng_g[qslot])
                qrng_g[qslot] = None
            nq = dma("sync", ring8[qslot],
                     vb(O_QRNG + qslot * 4096, 4 * 512)
                     .rearrange("p (h n) -> p h n", h=4),
                     s_qscr[qb * DM + kvh * 512: qb * DM + (kvh + 1) * 512, :]
                     .rearrange("(h p) n -> p h n", p=128), waits=qwt)
            for hh in range(4) if "nonorms" not in skip else []:
                sw = [(ring8[qslot], nq)]
                if sqs_guard[0] is not None:
                    sw.append(sqs_guard[0])
                    sqs_guard[0] = None
                nsq2 = emit("scalar",
                            lambda e, hh=hh, s=qslot:
                            e.activation(sqs, qr_t(s, hh), AF.Square),
                            waits=sw, sem=s_sc, inc=1)
                tws = [(s_sc, nsq2)] + bw(6)
                nt = emit("tensor",
                          lambda e: e.matmul(psf[6][:, 0:512], t_onesm.ap(), sqs,
                                             start=True, stop=True),
                          waits=tws, sem=s_te, inc=1)
                sqs_guard[0] = (s_te, nt)
                nv = emit("vector",
                          lambda e, hh=hh:
                          e.reduce_max(t_mq2[0:1, hh:hh + 1], psf[6][0:1, 0:512], AX.X),
                          waits=[(s_te, nt), (s_ve, s_ve.n)], sem=s_ve, inc=1)
                bankg[6] = (s_ve, nv)
            cw = []
            if crow_guard[0] is not None:
                cw.append(crow_guard[0])
                crow_guard[0] = None
            if "nonorms" in skip:
                ew = [ebias_guard[0]] if ebias_guard[0] is not None else []
                ebias_guard[0] = None
                nveb = emit("vector", lambda e: e.memset(t_ebias.ap(), -8.0),
                            waits=ew, sem=s_ve, inc=1)
            else:
                nv = emit("vector",
                          lambda e, h=kvh:
                          e.tensor_single_scalar(t_crow.ap(), t_mq2.ap(),
                                                 t_mk2[0:1, h:h + 1], ALU.mult),
                          waits=cw + [(s_ve, s_ve.n)], sem=s_ve, inc=1)
                nsq3 = emit("scalar",
                            lambda e: e.activation(t_crow.ap(), t_crow.ap(), AF.Sqrt),
                            waits=[(s_ve, nv)], sem=s_sc, inc=1)
                nv2 = emit("vector",
                           lambda e: e.tensor_scalar(t_crow.ap(), t_crow.ap(), -1.0, -1.0,
                                                     ALU.mult, ALU.add),
                           waits=[(s_sc, nsq3), (s_ve, s_ve.n)], sem=s_ve, inc=1)
                tws = [(s_ve, nv2)] + bw(6)
                if ebias_guard[0] is not None:
                    tws.append(ebias_guard[0])
                    ebias_guard[0] = None
                ntb = emit("tensor",
                           lambda e: e.matmul(psf[6][:, 0:4], t_onesr_f.ap(), t_crow.ap(),
                                              start=True, stop=True),
                           waits=tws, sem=s_te, inc=1)
                crow_guard[0] = (s_te, ntb)
                nveb = emit("vector", lambda e: e.tensor_copy(t_ebias.ap(), psf[6][:, 0:4]),
                            waits=[(s_te, ntb)], sem=s_ve, inc=1)
                bankg[6] = (s_ve, nveb)

            last_exp = None
            heads = [] if "noheads" in skip else list(range(4))
            for hh in heads:
                h = kvh * 4 + hh
                so = 2 + (h % 2)
                db = 4
                SBANKS = (0, 1, 5, 6)
                n_smm = [None] * IC
                n_exp = [None] * IC
                n_dmm = [None] * IC

                def emit_S(c):
                    sbk = SBANKS[c % 4]
                    if c >= 4:
                        tws = [(s_sc, n_exp[c - 4])]
                    else:
                        tws = bw(sbk)
                    if c == 0:
                        tws = tws + [(s_ve, nveb), (ring8[qslot], nq)]
                    n_smm[c] = emit("tensor",
                                    lambda e, c=c, sbk=sbk, h=kvh, hh=hh, s=qslot:
                                    e.matmul(psf[sbk][:, 0:512],
                                             kb_t(h)[:, c * 128:(c + 1) * 128],
                                             qr_t(s, hh),
                                             start=True, stop=True),
                                    waits=tws, sem=s_te, inc=1)

                def emit_exp(c):
                    sbk = SBANKS[c % 4]
                    tws = [(s_te, n_smm[c])]
                    pslot = c % 4
                    if pbuf_g[pslot] is not None:
                        tws.append(pbuf_g[pslot])
                        pbuf_g[pslot] = None
                    n_exp[c] = emit("scalar",
                                    lambda e, sbk=sbk, hh=hh, p=pslot:
                                    e.activation(pbuf[p], psf[sbk][:, 0:512], AF.Exp,
                                                 bias=t_ebias[:, hh:hh + 1]),
                                    waits=tws, sem=s_sc, inc=1)
                    bankg[sbk] = (s_sc, n_exp[c])

                def emit_PVD(c):
                    pslot = c % 4
                    tws = [(s_sc, n_exp[c])]
                    if c == 0:
                        tws = tws + bw(so) + bw(db)
                    if c < IC - 1:
                        emit("tensor",
                             lambda e, c=c, h=kvh, p=pslot, so=so:
                             e.matmul(psf[so][:, 0:512], vb_t(h, c), pbuf[p],
                                      start=(c == 0), stop=False),
                             waits=tws, sem=s_te, inc=0)
                        n_dmm[c] = emit("tensor",
                                        lambda e, c=c, p=pslot, db=db:
                                        e.matmul(psf[db][:, 0:512], t_onesm.ap(),
                                                 pbuf[p], start=(c == 0), stop=False),
                                        waits=[], sem=s_te, inc=1)
                    else:
                        emit("tensor",
                             lambda e, c=c, p=pslot, db=db:
                             e.matmul(psf[db][:, 0:512], t_onesm.ap(), pbuf[p],
                                      start=False, stop=True),
                             waits=tws, sem=s_te, inc=0)
                        n_dmm[c] = emit("tensor",
                                        lambda e, c=c, h=kvh, p=pslot, so=so:
                                        e.matmul(psf[so][:, 0:512], vb_t(h, c), pbuf[p],
                                                 start=False, stop=True),
                                        waits=[], sem=s_te, inc=1)
                    pbuf_g[pslot] = (s_te, n_dmm[c])

                emit_S(0)
                emit_exp(0)
                emit_S(1)
                emit_exp(1)
                if "sonly" in skip:
                    for c in range(IC):
                        if c + 2 < IC:
                            emit_S(c + 2)
                            emit_exp(c + 2)
                        pbuf_g[c % 4] = (s_sc, n_exp[c])
                    last_exp = n_exp[IC - 1]
                    aslot = h % 4
                    awt = [(s_sc, last_exp)]
                    if abuf_g[aslot] is not None:
                        awt.append(abuf_g[aslot])
                        abuf_g[aslot] = None
                    nev = emit("vector",
                               lambda e, a=aslot: e.tensor_copy(abuf[a], pbuf[3]),
                               waits=awt, sem=s_ve, inc=1)
                    pbuf_g[3] = (s_ve, nev)
                    na = dma("gpsimd", s_gp,
                             s_ascr[qb * DM + h * 128: qb * DM + (h + 1) * 128, :],
                             abuf[aslot], waits=[(s_ve, nev)])
                    abuf_g[aslot] = (s_gp, na)
                    n_ascr_store[0] = na
                    continue
                for c in range(IC):
                    emit_PVD(c)
                    if c + 2 < IC:
                        emit_S(c + 2)
                        emit_exp(c + 2)
                last_exp = n_exp[IC - 1]
                if "nodiv" in skip:
                    aslot = h % 4
                    awt = [(s_te, n_dmm[IC - 1])] + bw(db)
                    if abuf_g[aslot] is not None:
                        awt.append(abuf_g[aslot])
                        abuf_g[aslot] = None
                    nev = emit("vector",
                               lambda e, a=aslot, so=so:
                               e.tensor_copy(abuf[a], psf[so][:, 0:512]),
                               waits=awt, sem=s_ve, inc=1)
                    bankg[so] = (s_ve, nev)
                    bankg[db] = (s_ve, nev)
                    na = dma("gpsimd", s_gp,
                             s_ascr[qb * DM + h * 128: qb * DM + (h + 1) * 128, :],
                             abuf[aslot], waits=[(s_ve, nev)])
                    abuf_g[aslot] = (s_gp, na)
                    n_ascr_store[0] = na
                    continue
                rslot = h % 2
                rwt = [(s_te, n_dmm[IC - 1]), (s_ve, s_ve.n)]
                if rbuf_g[rslot] is not None:
                    rwt.append(rbuf_g[rslot])
                    rbuf_g[rslot] = None
                nrb = emit("vector",
                           lambda e, r=rslot, db=db: e.reciprocal(rbuf[r], psf[db][:, 0:512]),
                           waits=rwt, sem=s_ve, inc=1)
                bankg[db] = (s_ve, nrb)
                aslot = h % 4
                awt = [(s_ve, nrb)]
                if abuf_g[aslot] is not None:
                    awt.append(abuf_g[aslot])
                    abuf_g[aslot] = None
                nev = emit("vector",
                           lambda e, a=aslot, r=rslot, so=so:
                           e.tensor_tensor(abuf[a], psf[so][:, 0:512], rbuf[r], ALU.mult),
                           waits=awt, sem=s_ve, inc=1)
                bankg[so] = (s_ve, nev)
                rbuf_g[rslot] = (s_ve, nev)
                na = dma("gpsimd", gpa4[aslot],
                         s_ascr[qb * DM + h * 128: qb * DM + (h + 1) * 128, :],
                         abuf[aslot], waits=[(s_ve, nev)])
                abuf_g[aslot] = (gpa4[aslot], na)
                n_ascr_store[0] = na
            qrng_g[qslot] = (s_te, s_te.n)
            if last_exp is not None:
                ebias_guard[0] = (s_sc, last_exp)
            drain_dq(8 if "oldpace" in skip else 16)

    drain_dq(10 ** 9)

    if PH == 2:
        return _finish(nc, plan, es, fence, dma, emit, s_ld, s_gp, dumps, locals())
    fence()
    O_X2B = 0
    O_X2LT = 32768
    O_HBUF = 65536
    O_WR4 = 153600
    O_OBF = 161792
    O_SGT = 163840
    O_DSTG = 165888
    O_OSTG = 169984
    O_NXG = 174080
    O_NXH = 175104
    O_ZG = 176128
    O_ZD = 178176
    O_X2LN = 179200

    x2t = [vb(O_X2B + j * 8192, DM) for j in range(4)]

    def x2lt_t(ic):
        return vb(O_X2LT + ic * 1024, 512)
    hbuf = [vb(O_HBUF + j * 1024, 512) for j in range(GJ)]
    wr4 = [vb(O_WR4 + i * 1024, 512) for i in range(8)]
    wr4_g = [None] * 8
    obf = vb(O_OBF, 512)
    sgt = vf(O_SGT, 512)
    dstg = [vf(O_DSTG + i * 2048, 512) for i in range(2)]
    dstg_g = [None] * 2
    ostg = [vf(O_OSTG + i * 2048, 512) for i in range(2)]
    ostg_g = [None] * 2
    nxg = vb(O_NXG, 512, 32)
    nxh = vb(O_NXH, 512, GJ)
    zgt = vb(O_ZG, 512, 32)
    zut = vb(O_ZG + 1024, 512, 32)
    zdt = vb(O_ZD, 512, GJ)
    x2ln = vb(O_X2LN, DM)

    n_out_store = [0]
    obf_g = [None]
    sgt_g = [None]
    zg_guard = [None]
    zd_guard = [None]
    x2ln_guard = [None]
    for qb in range(1 if "p4bone" in skip else NQB):
        ldq = s_ld4b[qb]
        n_x2i = []
        for j in range(4):
            n_x2i.append((gpc4[j], dma("gpsimd", gpc4[j], x2t[j],
                          d_xown[qb * 512 + j * 128: qb * 512 + (j + 1) * 128, :],
                          waits=[(s_ve, s_ve.n)] if j == 0 else [])))
        for og in range(8):
            nmm = [None] * 4
            for icb in range(IC):
                slot = icb % 4
                wwt = []
                if wr4_g[slot] is not None:
                    wwt.append(wr4_g[slot])
                    wr4_g[slot] = None
                nldw = dma("sync", ring8[slot], wr4[slot],
                           s_bino[icb * 128:(icb + 1) * 128, og * 512:(og + 1) * 512],
                           waits=wwt)
                aslot = 4 + icb % 4
                awt = []
                if wr4_g[aslot] is not None:
                    awt.append(wr4_g[aslot])
                    wr4_g[aslot] = None
                nlda = dma("sync", ring8[aslot], wr4[aslot],
                           s_ascr[qb * DM + icb * 128: qb * DM + (icb + 1) * 128, :],
                           waits=awt)
                for k in range(4):
                    tws = [(ring8[slot], nldw), (ring8[aslot], nlda)] if k == 0 else []
                    if icb == 0:
                        tws = tws + bw(k)
                    nmm[k] = emit("tensor",
                                  lambda e, k=k, s=slot, a=aslot, ic=icb:
                                  e.matmul(psf[k][:, 0:512],
                                           wr4[s][:, k * 128:(k + 1) * 128],
                                           wr4[a], start=(ic == 0), stop=(ic == IC - 1)),
                                  waits=tws, sem=s_te, inc=(1 if k == 3 else 0))
                wr4_g[slot] = (s_te, nmm[3])
                wr4_g[aslot] = (s_te, nmm[3])
            ntpl = None
            for k in range(4):
                oc = og * 4 + k
                swt = [(s_te, nmm[3])]
                if obf_g[0] is not None:
                    swt.append(obf_g[0])
                    obf_g[0] = None
                nevo = emit("scalar",
                            lambda e, k=k, oc=oc:
                            e.activation(obf, psf[k][:, 0:512], AF.Copy,
                                         scale=t_svec[:, 48 + oc: 48 + oc + 1]),
                            waits=swt, sem=s_sc, inc=1)
                bankg[k] = (s_sc, nevo)
                for j in range(4):
                    b = 4 + j
                    tws = [(s_sc, nevo)]
                    if k == 0:
                        tws = tws + bw(b)
                    ntpl = emit("tensor",
                                lambda e, b=b, j=j, k=k:
                                e.transpose(psb[b][:, k * 128:(k + 1) * 128],
                                            obf[:, j * 128:(j + 1) * 128], t_identb.ap()),
                                waits=tws, sem=s_te, inc=(1 if j == 3 else 0))
                obf_g[0] = (s_te, ntpl)
            for j in range(4):
                b = 4 + j
                awt = [(s_te, ntpl), n_x2i[j]]
                nadd = emit("vector",
                            lambda e, b=b, j=j, og=og:
                            e.tensor_tensor(x2t[j][:, og * 512:(og + 1) * 512],
                                            psb[b][:, 0:512],
                                            x2t[j][:, og * 512:(og + 1) * 512], ALU.add),
                            waits=awt, sem=s_ve, inc=1)
                bankg[b] = (s_ve, nadd)
        if qb == 0 and "x2dbg" in skip:
            for j in range(4):
                dbgstore(f"x2_{j}", x2t[j], [128, DM], BF16)
        n_x2lt_ready = None
        for j in range(4):
            lw = []
            if x2ln_guard[0] is not None:
                lw.append(x2ln_guard[0])
                x2ln_guard[0] = None
            emit("vector", lambda e, j=j: e.reduce_sum(t_stat[:, 0:1], x2t[j], AX.X),
                 waits=[(s_ve, s_ve.n)], sem=s_ve, inc=1)
            emit("vector", lambda e, j=j:
                 e.scalar_tensor_tensor(x2ln, x2t[j], 1.0, x2t[j], ALU.mult, ALU.mult,
                                        accum_out=t_stat[:, 1:2]),
                 waits=lw + [(s_ve, s_ve.n)], sem=s_ve, inc=1)
            emit("vector", lambda e: e.tensor_scalar(t_stat[:, 2:3], t_stat[:, 0:1],
                                                     1.0 / DM, None, ALU.mult),
                 waits=[(s_ve, s_ve.n)], sem=s_ve, inc=1)
            emit("vector", lambda e: e.tensor_scalar(t_stat[:, 3:4], t_stat[:, 1:2],
                                                     1.0 / DM, None, ALU.mult),
                 waits=[(s_ve, s_ve.n)], sem=s_ve, inc=1)
            emit("vector", lambda e: e.tensor_tensor(t_stat[:, 4:5], t_stat[:, 2:3],
                                                     t_stat[:, 2:3], ALU.mult),
                 waits=[(s_ve, s_ve.n)], sem=s_ve, inc=1)
            emit("vector", lambda e: e.tensor_tensor(t_stat[:, 5:6], t_stat[:, 3:4],
                                                     t_stat[:, 4:5], ALU.subtract),
                 waits=[(s_ve, s_ve.n)], sem=s_ve, inc=1)
            nvar = emit("vector", lambda e: e.tensor_scalar(t_stat[:, 5:6], t_stat[:, 5:6],
                                                            0.0, None, ALU.max),
                        waits=[(s_ve, s_ve.n)], sem=s_ve, inc=1)
            nsq = emit("scalar", lambda e: e.activation(t_stat[:, 6:7], t_stat[:, 5:6],
                                                        AF.Sqrt, bias=t_eps[:, 0:1]),
                       waits=[(s_ve, nvar)], sem=s_sc, inc=1)
            emit("vector", lambda e: e.reciprocal(t_stat[:, 7:8], t_stat[:, 6:7]),
                 waits=[(s_sc, nsq), (s_ve, s_ve.n)], sem=s_ve, inc=1)
            emit("vector", lambda e: e.tensor_tensor(t_stat[:, 8:9], t_stat[:, 2:3],
                                                     t_stat[:, 7:8], ALU.mult),
                 waits=[(s_ve, s_ve.n)], sem=s_ve, inc=1)
            emit("vector", lambda e: e.tensor_scalar(t_stat[:, 9:10], t_stat[:, 8:9],
                                                     -1.0, None, ALU.mult),
                 waits=[(s_ve, s_ve.n)], sem=s_ve, inc=1)
            nap = emit("vector", lambda e, j=j:
                       e.tensor_scalar(x2ln, x2t[j], t_stat[:, 7:8], t_stat[:, 9:10],
                                       ALU.mult, ALU.add),
                       waits=[(s_ve, s_ve.n)], sem=s_ve, inc=1)
            ntp = None
            for q4 in range(8):
                b = 4 + (q4 % 2)
                tws = bw(b) + [(s_ve, nap)]
                for jj in range(4):
                    icb = q4 * 4 + jj
                    ntp = emit("tensor",
                               lambda e, b=b, jj=jj, ic=icb:
                               e.transpose(psb[b][:, jj * 128:(jj + 1) * 128],
                                           x2ln[:, ic * 128:(ic + 1) * 128], t_identb.ap()),
                               waits=(tws if jj == 0 else []), sem=s_te,
                               inc=(1 if jj == 3 else 0))
                nev = emit("vector",
                           lambda e, b=b, q4=q4, j=j:
                           e.tensor_copy(vb(O_X2LT, IC * 512)
                                         .rearrange("p (c n) -> p c n", c=IC)
                                         [:, q4 * 4:(q4 + 1) * 4, j * 128:(j + 1) * 128],
                                         psb[b][:, 0:512]),
                           waits=[(s_te, ntp)], sem=s_ve, inc=1)
                bankg[b] = (s_ve, nev)
                n_x2lt_ready = nev
            x2ln_guard[0] = (s_te, ntp)
        if qb == 0 and "x2dbg" in skip:
            dbgstore("x2lt", vb(O_X2LT, IC * 512), [128, IC * 512], BF16)
        tws = bw(4) + [(s_ve, n_x2lt_ready), (s_ldu, n_consts)]
        ntx = None
        for icb in range(IC):
            ntx = emit("tensor",
                       lambda e, ic=icb:
                       e.matmul(psf[4][0:32, 0:512], t_band32[:, 31 - ic: 63 - ic],
                                x2lt_t(ic), start=(ic == 0), stop=(ic == IC - 1)),
                       waits=(tws if icb == 0 else []), sem=s_te,
                       inc=(1 if icb == IC - 1 else 0))
        nxg_n = emit("vector",
                     lambda e: e.tensor_scalar(nxg, psf[4][0:32, 0:512], -1.0, None,
                                               ALU.mult),
                     waits=[(s_te, ntx)], sem=s_ve, inc=1)
        bankg[4] = (s_ve, nxg_n)
        for j0 in range(0, GJ, 4):
            jn = min(4, GJ - j0)
            zw = []
            if zg_guard[0] is not None:
                zw.append(zg_guard[0])
                zg_guard[0] = None
            nzg = dma("sync", zs3[0], zgt[0:32, 0:jn * 128],
                      s_zpg[:, j0 * 128:(j0 + jn) * 128], waits=zw)
            nzu = dma("sync", zs3[1], zut[0:32, 0:jn * 128],
                      s_zpg[:, INTER + j0 * 128: INTER + (j0 + jn) * 128])
            nmm = [None] * 8
            for icb in range(IC):
                slot = (icb * 2) % 8
                slot2 = (icb * 2 + 1) % 8
                wwt = []
                if wr4_g[slot] is not None:
                    wwt.append(wr4_g[slot])
                    wr4_g[slot] = None
                nldg = dma("sync", ring8[slot], wr4[slot][:, 0:jn * 128],
                           s_w1g[icb * 128:(icb + 1) * 128, j0 * 128:(j0 + jn) * 128],
                           waits=wwt)
                wwt2 = []
                if wr4_g[slot2] is not None:
                    wwt2.append(wr4_g[slot2])
                    wr4_g[slot2] = None
                nldu = dma("sync", ring8[slot2], wr4[slot2][:, 0:jn * 128],
                           s_w1g[icb * 128:(icb + 1) * 128,
                                 INTER + j0 * 128: INTER + (j0 + jn) * 128],
                           waits=wwt2)
                nu_last = None
                for kk in range(2 * jn):
                    g_half = kk < jn
                    k = kk if g_half else kk - jn
                    sl_ = slot if g_half else slot2
                    bk = k if g_half else 4 + k
                    tws = [(ring8[slot], nldg), (ring8[slot2], nldu)] if kk == 0 else []
                    if icb == 0:
                        tws = tws + bw(bk)
                        if kk == 0:
                            tws = tws + [(s_ve, n_x2lt_ready)]
                    nu_last = emit("tensor",
                                   lambda e, bk=bk, k=k, s=sl_, ic=icb:
                                   e.matmul(psf[bk][:, 0:512],
                                            wr4[s][:, k * 128:(k + 1) * 128],
                                            x2lt_t(ic), start=(ic == 0), stop=False),
                                   waits=tws, sem=s_te,
                                   inc=(1 if kk == 2 * jn - 1 else 0))
                wr4_g[slot] = (s_te, nu_last)
                wr4_g[slot2] = (s_te, nu_last)
            for k in range(jn):
                tws = [(zs3[0], nzg), (s_ve, nxg_n)] if k == 0 else []
                nmm[k] = emit("tensor",
                              lambda e, k=k:
                              e.matmul(psf[k][:, 0:512], zgt[0:32, k * 128:(k + 1) * 128],
                                       nxg, start=False, stop=True),
                              waits=tws, sem=s_te, inc=1)
            for k in range(jn):
                tws = [(zs3[1], nzu)] if k == 0 else []
                nmm[4 + k] = emit("tensor",
                                  lambda e, k=k:
                                  e.matmul(psf[4 + k][:, 0:512],
                                           zut[0:32, k * 128:(k + 1) * 128],
                                           nxg, start=False, stop=True),
                                  waits=tws, sem=s_te, inc=1)
            zg_guard[0] = (s_te, nmm[4 + jn - 1])
            for k in range(jn):
                j = j0 + k
                sw = [(s_te, nmm[k])]
                if sgt_g[0] is not None:
                    sw.append(sgt_g[0])
                    sgt_g[0] = None
                nsg = emit("scalar",
                           lambda e, k=k: e.activation(sgt, psf[k][:, 0:512], AF.Silu),
                           waits=sw, sem=s_sc, inc=1)
                bankg[k] = (s_sc, nsg)
                nh = emit("vector",
                          lambda e, k=k, j=j:
                          e.tensor_tensor(hbuf[j], psf[4 + k][:, 0:512], sgt, ALU.mult),
                          waits=[(s_sc, nsg), (s_te, nmm[4 + k])], sem=s_ve, inc=1)
                bankg[4 + k] = (s_ve, nh)
                sgt_g[0] = (s_ve, nh)
        n_h_ready = s_ve.n
        if qb == 0 and "x2dbg" in skip:
            dbgstore("hb", vb(O_HBUF, GJ * 512), [128, GJ * 512], BF16)
            dbgstore("nxg", nxg, [32, 512], BF16)
        tws = bw(4) + [(s_ve, n_h_ready), (s_ldu, n_consts)]
        ntx = None
        for kc in range(GJ):
            ntx = emit("tensor",
                       lambda e, kc=kc:
                       e.matmul(psf[4][0:GJ, 0:512], t_band86[:, 85 - kc: 171 - kc],
                                hbuf[kc], start=(kc == 0), stop=(kc == GJ - 1)),
                       waits=(tws if kc == 0 else []), sem=s_te,
                       inc=(1 if kc == GJ - 1 else 0))
        nxh_n = emit("vector",
                     lambda e: e.tensor_scalar(nxh, psf[4][0:GJ, 0:512], -1.0, None,
                                               ALU.mult),
                     waits=[(s_te, ntx)], sem=s_ve, inc=1)
        bankg[4] = (s_ve, nxh_n)
        for og in range(8):
            zw = []
            if zd_guard[0] is not None:
                zw.append(zd_guard[0])
                zd_guard[0] = None
            nzd = dma("sync", zs3[2], zdt[0:GJ, :], s_zpd[:, og * 512:(og + 1) * 512],
                      waits=zw)
            nmm = [None] * 4
            for kc0 in range(0, GJ, 2):
                pair = [kc0] + ([kc0 + 1] if kc0 + 1 < GJ else [])
                nlds = []
                for kc in pair:
                    slot = kc % 8
                    wwt = []
                    if wr4_g[slot] is not None:
                        wwt.append(wr4_g[slot])
                        wr4_g[slot] = None
                    nlds.append((ring8[slot],
                                 dma("sync", ring8[slot], wr4[slot],
                                     s_w1d[kc * 128:(kc + 1) * 128,
                                           og * 512:(og + 1) * 512],
                                     waits=wwt)))
                nd_last = None
                nmms = 4 * len(pair)
                mi = 0
                for kc in pair:
                    slot = kc % 8
                    for k in range(4):
                        tws = list(nlds) if mi == 0 else []
                        if kc0 == 0:
                            tws = tws + bw(k) + ([(s_ve, nxh_n)] if mi == 0 else [])
                        nd_last = emit("tensor",
                                       lambda e, k=k, s=slot, kc=kc:
                                       e.matmul(psf[k][:, 0:512],
                                                wr4[s][:, k * 128:(k + 1) * 128],
                                                hbuf[kc], start=(kc == 0), stop=False),
                                       waits=tws, sem=s_te,
                                       inc=(1 if mi == nmms - 1 else 0))
                        mi += 1
                for kc in pair:
                    wr4_g[kc % 8] = (s_te, nd_last)
            for k in range(4):
                tws = [(zs3[2], nzd)] if k == 0 else []
                nmm[k] = emit("tensor",
                              lambda e, k=k:
                              e.matmul(psf[k][:, 0:512], zdt[0:GJ, k * 128:(k + 1) * 128],
                                       nxh, start=False, stop=True),
                              waits=tws, sem=s_te, inc=1)
            zd_guard[0] = (s_te, nmm[3])
            ntp_fin = None
            for k in range(4):
                dslot = k % 2
                dwt = [(s_te, nmm[k])]
                if dstg_g[dslot] is not None:
                    dwt.append(dstg_g[dslot])
                    dstg_g[dslot] = None
                nevd = emit("vector",
                            lambda e, k=k, d=dslot: e.tensor_copy(dstg[d], psf[k][:, 0:512]),
                            waits=dwt, sem=s_ve, inc=1)
                bankg[k] = (s_ve, nevd)
                ntp = None
                for j in range(4):
                    b = 4 + j
                    tws = [(s_ve, nevd)]
                    if k == 0:
                        tws = tws + bw(b)
                    ntp = emit("tensor",
                               lambda e, b=b, j=j, k=k, d=dslot:
                               e.transpose(psf[b][:, k * 128:(k + 1) * 128],
                                           dstg[d][:, j * 128:(j + 1) * 128],
                                           t_identf.ap()),
                               waits=tws, sem=s_te, inc=(1 if j == 3 else 0))
                dstg_g[dslot] = (s_te, ntp)
                ntp_fin = ntp
            for j in range(4):
                b = 4 + j
                oslot = j % 2
                owt = [(s_te, ntp_fin)]
                if ostg_g[oslot] is not None:
                    owt.append(ostg_g[oslot])
                    ostg_g[oslot] = None
                nout = emit("vector",
                            lambda e, b=b, j=j, og=og, o=oslot:
                            e.tensor_tensor(ostg[o], psf[b][:, 0:512],
                                            x2t[j][:, og * 512:(og + 1) * 512], ALU.add),
                            waits=owt, sem=s_ve, inc=1)
                bankg[b] = (s_ve, nout)
                ng = dma("gpsimd", gpo2[oslot],
                         d_out[qb * 512 + j * 128: qb * 512 + (j + 1) * 128,
                               og * 512:(og + 1) * 512],
                         ostg[oslot], waits=[(s_ve, nout)])
                ostg_g[oslot] = (gpo2[oslot], ng)
                n_out_store[0] = ng

    g0, g1 = gpo2[0].n, gpo2[1].n
    plan["gpsimd"].append(lambda e, a=g0, b=g1:
                          (e.wait_ge(gpo2[0].h, a), e.wait_ge(gpo2[1].h, b)))
    if dumps:
        return _finish(nc, plan, es, fence, dma, emit, s_ld, s_gp, dumps, locals())

    with nc.Block() as block:
        @block.sync
        def _(e):
            for f in plan["sync"]:
                f(e)

        @block.scalar
        def _(e):
            for f in plan["scalar"]:
                f(e)

        @block.vector
        def _(e):
            for f in plan["vector"]:
                f(e)

        @block.tensor
        def _(e):
            for f in plan["tensor"]:
                f(e)

        @block.gpsimd
        def _(e):
            for f in plan["gpsimd"]:
                f(e)

    es.close()
    return nc


def _band(w, col):
    a = np.zeros((128, w), dtype=np.float32)
    a[:, col] = 1.0
    return a.astype(ml_dtypes.bfloat16)


_NC_CACHE = [None]


def kernel(**inputs):
    hs = np.asarray(inputs["L0_attn_hidden_states"])
    ins = {
        "qw": np.ascontiguousarray(inputs["L0_attn_q_weight"]),
        "kw": np.ascontiguousarray(inputs["L0_attn_k_weight"]),
        "vw": np.ascontiguousarray(inputs["L0_attn_v_weight"]),
        "ow": np.ascontiguousarray(inputs["L0_attn_o_weight"]),
        "qs": np.ascontiguousarray(inputs["L0_attn_q_scale"]),
        "ks": np.ascontiguousarray(inputs["L0_attn_k_scale"]),
        "vs": np.ascontiguousarray(inputs["L0_attn_v_scale"]),
        "os_": np.ascontiguousarray(inputs["L0_attn_o_scale"]),
        "gqw": np.ascontiguousarray(inputs["L0_ffn_qweight_gu"]),
        "gqz": np.ascontiguousarray(inputs["L0_ffn_qzeros_gu"]),
        "gsc": np.ascontiguousarray(inputs["L0_ffn_scales_gu"]),
        "dqw": np.ascontiguousarray(inputs["L0_ffn_qweight_down"]),
        "dqz": np.ascontiguousarray(inputs["L0_ffn_qzeros_down"]),
        "dsc": np.ascontiguousarray(inputs["L0_ffn_scales_down"]),
        "identf": np.eye(128, dtype=np.float32),
        "band32": _band(63, 31),
        "band86": _band(171, 85),
        "shiftv": ((np.arange(128, dtype=np.uint32) % 8) * 4).reshape(128, 1),
    }
    in_maps = []
    for c in range(8):
        b, h = c // 2, c % 2
        m = dict(ins)
        m["x_own"] = np.ascontiguousarray(hs[b, h * TOWN:(h + 1) * TOWN])
        m["x_oth"] = np.ascontiguousarray(hs[b, (1 - h) * TOWN:(2 - h) * TOWN])
        in_maps.append(m)
    if _NC_CACHE[0] is None:
        _NC_CACHE[0] = build_nc()
    res = run_bass_kernel_spmd(_NC_CACHE[0], in_maps, core_ids=list(range(8)))
    out = np.empty((4, SEQ, DM), dtype=np.float32)
    for c in range(8):
        b, h = c // 2, c % 2
        out[b, h * TOWN:(h + 1) * TOWN] = res.results[c]["out"]
    return out


# version 18
# speedup vs baseline: 1.0225x; 1.0225x over previous
import contextlib
import numpy as np
import ml_dtypes
import concourse.bass as bass
import concourse.mybir as mybir
from concourse.bass_utils import run_bass_kernel_spmd

F32 = mybir.dt.float32
BF16 = mybir.dt.bfloat16
U32 = mybir.dt.uint32
AF = mybir.ActivationFunctionType
ALU = mybir.AluOpType
AX = mybir.AxisListType

DM = 4096
KVD = 1024
NKVH = 8
INTER = 11008
GUC = 2 * INTER
SEQ = 4096
TOWN = 2048
TB = 512
NQB = TOWN // TB
NSB = SEQ // TB
IC = DM // 128
GJ = INTER // 128
RT128 = 1.0 / float(np.sqrt(128.0))
EPS = 1e-5

ENGS = ("sync", "scalar", "vector", "tensor", "gpsimd")


DUMPCAT = {
    "binq": ("binq_d", lambda L: L["s_binq"][0:256, 0:512], [256, 512], "bf"),
    "bino": ("bino_d", lambda L: L["s_bino"][0:256, 0:512], [256, 512], "bf"),
    "w1g": ("w1g_d", lambda L: L["s_w1g"][0:256, 0:1024], [256, 1024], "bf"),
    "w1d": ("w1d_d", lambda L: L["s_w1d"][0:256, 0:1024], [256, 1024], "bf"),
    "zpg": ("zpg_d", lambda L: L["s_zpg"][:, :], [32, GUC], "bf"),
    "zpd": ("zpd_d", lambda L: L["s_zpd"][:, :], [GJ, DM], "bf"),
    "kscr": ("kscr_d", lambda L: L["s_kscr"][:, 0:512], [KVD, 512], "bf"),
    "kscr2": ("kscr2_d", lambda L: L["s_kscr"][:, 2048:2560], [KVD, 512], "bf"),
    "vscr": ("vscr_d", lambda L: L["s_vscr"][0, 0:512, :], [512, 128], "bf"),
    "qscr": ("qscr_d", lambda L: L["s_qscr"][0:DM, :], [DM, 512], "bf"),
    "ascr": ("ascr_d", lambda L: L["s_ascr"][0:DM, :], [DM, 512], "bf"),
}


def _finish(nc, plan, es, fence, dma, emit, s_ld, s_gp, dumps, L):
    fence()
    nc._dump_handles = {}
    for name in dumps:
        (dname, get, shape, dt) = DUMPCAT[name]
        dt = BF16 if dt == "bf" else F32
        dtn = nc.dram_tensor(dname, shape, dt, kind="ExternalOutput")
        nc._dump_handles[dname] = dtn
        dma("sync", s_ld, dtn[:, :], get(L))
    plan["gpsimd"].append(lambda e, v=s_ld.n, w=s_gp.n:
                          (e.wait_ge(s_ld.h, v), e.wait_ge(s_gp.h, w)))
    with nc.Block() as block:
        @block.sync
        def _(e):
            for f in plan["sync"]:
                f(e)

        @block.scalar
        def _(e):
            for f in plan["scalar"]:
                f(e)

        @block.vector
        def _(e):
            for f in plan["vector"]:
                f(e)

        @block.tensor
        def _(e):
            for f in plan["tensor"]:
                f(e)

        @block.gpsimd
        def _(e):
            for f in plan["gpsimd"]:
                f(e)
    es.close()
    return nc


class Sem:
    def __init__(self, es, nc, name):
        self.h = es.enter_context(nc.semaphore(name))
        self.n = 0

    def inc(self, k=1):
        self.n += k
        return self.n


def build_nc(upto="p4b", dumps=(), skip=()):
    PH = {"p0": 0, "p1": 1, "p4a": 2, "p4b": 3}[upto]
    skipset = [set(skip)]
    nc = bass.Bass("TRN2", target_bir_lowering=False)
    es = contextlib.ExitStack()
    plan = {e: [] for e in ENGS}

    d_xown = nc.dram_tensor("x_own", [TOWN, DM], F32, kind="ExternalInput")
    d_xoth = nc.dram_tensor("x_oth", [TOWN, DM], F32, kind="ExternalInput")
    d_qw = nc.dram_tensor("qw", [DM, DM], F32, kind="ExternalInput")
    d_kw = nc.dram_tensor("kw", [KVD, DM], F32, kind="ExternalInput")
    d_vw = nc.dram_tensor("vw", [KVD, DM], F32, kind="ExternalInput")
    d_ow = nc.dram_tensor("ow", [DM, DM], F32, kind="ExternalInput")
    d_qs = nc.dram_tensor("qs", [DM], F32, kind="ExternalInput")
    d_ks = nc.dram_tensor("ks", [KVD], F32, kind="ExternalInput")
    d_vs = nc.dram_tensor("vs", [KVD], F32, kind="ExternalInput")
    d_os = nc.dram_tensor("os_", [DM], F32, kind="ExternalInput")
    d_gqw = nc.dram_tensor("gqw", [512, GUC], U32, kind="ExternalInput")
    d_gqz = nc.dram_tensor("gqz", [32, GUC // 8], U32, kind="ExternalInput")
    d_gsc = nc.dram_tensor("gsc", [32, GUC], F32, kind="ExternalInput")
    d_dqw = nc.dram_tensor("dqw", [INTER // 8, DM], U32, kind="ExternalInput")
    d_dqz = nc.dram_tensor("dqz", [GJ, DM // 8], U32, kind="ExternalInput")
    d_dsc = nc.dram_tensor("dsc", [GJ, DM], F32, kind="ExternalInput")
    d_identf = nc.dram_tensor("identf", [128, 128], F32, kind="ExternalInput")
    d_band32 = nc.dram_tensor("band32", [128, 63], BF16, kind="ExternalInput")
    d_band86 = nc.dram_tensor("band86", [128, 171], BF16, kind="ExternalInput")
    d_shiftv = nc.dram_tensor("shiftv", [128, 1], U32, kind="ExternalInput")

    d_out = nc.dram_tensor("out", [TOWN, DM], F32, kind="ExternalOutput")

    s_binq = nc.dram_tensor("binq", [DM, DM], BF16)
    s_bink = nc.dram_tensor("bink", [DM, KVD], BF16)
    s_binv = nc.dram_tensor("binv", [DM, KVD], BF16)
    s_bino = nc.dram_tensor("bino", [DM, DM], BF16)
    s_w1g = nc.dram_tensor("w1g", [DM, GUC], BF16)
    s_w1d = nc.dram_tensor("w1d", [INTER, DM], BF16)
    d_ghl = nc.dram_tensor("ghl", [2, 32 * GUC], BF16)
    d_dhl = nc.dram_tensor("dhl", [2, 86 * DM], BF16)
    s_zpg = nc.dram_tensor("zpg", [32, GUC], BF16)
    s_ghi = nc.dram_tensor("ghi", [32, GUC], BF16)
    s_glo = nc.dram_tensor("glo", [32, GUC], BF16)
    s_dhi = nc.dram_tensor("dhi", [GJ, DM], BF16)
    s_dlo = nc.dram_tensor("dlo", [GJ, DM], BF16)
    s_zpd = nc.dram_tensor("zpd", [GJ, DM], BF16)
    s_kscr = nc.dram_tensor("kscr", [KVD, SEQ], BF16)
    s_vscr = nc.dram_tensor("vscr", [NKVH, SEQ, 128], BF16)
    s_qscr = nc.dram_tensor("qscr", [NQB * DM, TB], BF16)
    s_ascr = nc.dram_tensor("ascr", [NQB * DM, TB], BF16)

    s_ld = Sem(es, nc, "s_ld")
    s_ld4a = Sem(es, nc, "s_ld4a")
    s_ld4b = [Sem(es, nc, f"s_ld4b{i}") for i in range(NQB)]
    s_scst = Sem(es, nc, "s_scst")
    s_gp = Sem(es, nc, "s_gp")
    s_dqld = Sem(es, nc, "s_dqld")
    s_dqld2 = Sem(es, nc, "s_dqld2")
    s_dqst = Sem(es, nc, "s_dqst")
    s_dqst2 = Sem(es, nc, "s_dqst2")
    s_ldz = Sem(es, nc, "s_ldz")
    s_ldu = Sem(es, nc, "s_ldu")
    ring8 = [Sem(es, nc, f"s_r{i}") for i in range(8)]
    xb2 = [Sem(es, nc, f"s_xb{i}") for i in range(2)]
    wb4 = [Sem(es, nc, f"s_wb{i}") for i in range(4)]
    zs3 = [Sem(es, nc, f"s_z{i}") for i in range(3)]
    stk4 = [Sem(es, nc, f"s_stk{i}") for i in range(4)]
    stv2 = [Sem(es, nc, f"s_stv{i}") for i in range(2)]
    stq1 = Sem(es, nc, "s_stq")
    gpc4 = [Sem(es, nc, f"s_gpc{i}") for i in range(4)]
    gpa4 = [Sem(es, nc, f"s_gpa{i}") for i in range(4)]
    gpo2 = [Sem(es, nc, f"s_gpo{i}") for i in range(2)]
    s_te = Sem(es, nc, "s_te")
    s_ve = Sem(es, nc, "s_ve")
    s_sc = Sem(es, nc, "s_sc")
    ALL_SEMS = ([s_ld, s_ld4a, s_scst, s_gp, s_dqld, s_dqst, s_ldz, s_ldu, s_te, s_ve, s_sc]
                + s_ld4b + ring8 + xb2 + wb4 + zs3 + stk4 + stv2 + [stq1] + gpc4 + gpa4 + gpo2
                + [s_dqld2, s_dqst2])

    t_identf = es.enter_context(nc.sbuf_tensor("c_identf", [128, 128], F32))
    t_identb = es.enter_context(nc.sbuf_tensor("c_identb", [128, 128], BF16))
    t_onesr_f = es.enter_context(nc.sbuf_tensor("c_onesr_f", [1, 128], F32))
    t_onesc_b = es.enter_context(nc.sbuf_tensor("c_onesc_b", [128, 1], BF16))
    t_onesm = es.enter_context(nc.sbuf_tensor("c_onesm", [128, 128], BF16))
    t_onesr_b = es.enter_context(nc.sbuf_tensor("c_onesr_b", [1, 128], BF16))
    t_band32 = es.enter_context(nc.sbuf_tensor("c_band32", [128, 63], BF16))
    t_band86 = es.enter_context(nc.sbuf_tensor("c_band86", [128, 171], BF16))
    t_shiftv = es.enter_context(nc.sbuf_tensor("c_shiftv", [128, 1], U32))
    t_svec = es.enter_context(nc.sbuf_tensor("c_svec", [128, 80], F32))
    t_qsin = es.enter_context(nc.sbuf_tensor("c_qsin", [128, 80], F32))
    t_ebias = es.enter_context(nc.sbuf_tensor("c_ebias", [128, 4], F32))
    t_mk2 = es.enter_context(nc.sbuf_tensor("c_mk2", [1, 8], F32))
    t_mq2 = es.enter_context(nc.sbuf_tensor("c_mq2", [1, 4], F32))
    t_crow = es.enter_context(nc.sbuf_tensor("c_crow", [1, 4], F32))
    t_stat = es.enter_context(nc.sbuf_tensor("c_stat", [128, 24], F32))
    t_rd = es.enter_context(nc.sbuf_tensor("c_rd", [1, 512], F32))
    t_rd2 = es.enter_context(nc.sbuf_tensor("c_rd2", [1, 512], F32))
    t_eps = es.enter_context(nc.sbuf_tensor("c_eps", [128, 1], F32))

    AR = 190 * 1024
    t_ar = es.enter_context(nc.sbuf_tensor("arena", [128, AR], mybir.dt.uint8))
    h_f32 = t_ar.bitcast(F32)
    h_bf = t_ar.bitcast(BF16)
    h_u32 = t_ar.bitcast(U32)

    def vf(off, cols, rows=128):
        assert off % 4 == 0 and off + cols * 4 <= AR, (off, cols)
        return h_f32[0:rows, off // 4: off // 4 + cols]

    def vb(off, cols, rows=128):
        assert off % 2 == 0 and off + cols * 2 <= AR, (off, cols)
        return h_bf[0:rows, off // 2: off // 2 + cols]

    def vu(off, cols, rows=128):
        assert off % 4 == 0 and off + cols * 4 <= AR, (off, cols)
        return h_u32[0:rows, off // 4: off // 4 + cols]

    psh = [es.enter_context(nc.psum_tensor(f"ps{i}", [128, 512], F32)) for i in range(8)]
    psf = [h.ap() for h in psh]
    psb = [h.bitcast(BF16).ap() for h in psh]
    bankg = [None] * 8

    def emit(eng, fn, waits=(), sem=None, inc=0):
        n = sem.inc(inc) if (sem is not None and inc) else None
        wt = tuple((s, v) for (s, v) in waits if v)

        def f(e, fn=fn, wt=wt, sem=sem, inc=inc):
            for (sh, v) in wt:
                e.wait_ge(sh.h, v)
            ins = fn(e)
            if sem is not None and inc:
                ins.then_inc(sem.h, inc)
        plan[eng].append(f)
        return n

    def dma(eng, sem, out_ap, in_ap, waits=(), slow=False):
        return emit(eng, lambda e, o=out_ap, i=in_ap, sl=slow:
                    e.dma_start(out=o, in_=i, allow_slow_non_contiguous=sl),
                    waits=waits, sem=sem, inc=16)

    def bw(b):
        gclaim = bankg[b]
        bankg[b] = None
        return [gclaim] if gclaim is not None else []

    def ser(semq):
        if "serload" in skipset[0]:
            return [(semq, semq.n)]
        return []

    def fence():
        tg = tuple((s, s.n) for s in ALL_SEMS if s.n > 0)
        for eng in ENGS:
            def f(e, tg=tg):
                for (sh, v) in tg:
                    e.wait_ge(sh.h, v)
            plan[eng].append(f)
        for i in range(8):
            bankg[i] = None

    dbg_stores = []

    def dbgstore(name, ap, shape, dtype):
        t = nc.dram_tensor("dbg_" + name, shape, dtype, kind="ExternalOutput")
        n = dma("scalar", s_scst, t[:, :], ap,
                waits=[(s_ve, s_ve.n), (s_sc, s_sc.n), (s_te, s_te.n)])
        dbg_stores.append(n)

    dma("sync", s_ldu, t_identf.ap(), d_identf[:, :])
    dma("sync", s_ldu, t_band32.ap(), d_band32[:, :])
    dma("sync", s_ldu, t_band86.ap(), d_band86[:, :])
    dma("sync", s_ldu, t_shiftv.ap(), d_shiftv[:, :])
    dma("sync", s_ldu, t_qsin[:, 0:32], d_qs.ap().rearrange("(c p) -> p c", p=128), slow=True)
    dma("sync", s_ldu, t_qsin[:, 32:40], d_ks.ap().rearrange("(c p) -> p c", p=128), slow=True)
    dma("sync", s_ldu, t_qsin[:, 40:48], d_vs.ap().rearrange("(c p) -> p c", p=128), slow=True)
    n_consts = dma("sync", s_ldu, t_qsin[:, 48:80], d_os.ap().rearrange("(c p) -> p c", p=128), slow=True)
    emit("vector", lambda e: e.tensor_copy(t_identb.ap(), t_identf.ap()),
         waits=[(s_ldu, n_consts)], sem=s_ve, inc=1)
    emit("vector", lambda e: e.memset(t_onesr_f.ap(), 1.0), sem=s_ve, inc=1)
    emit("vector", lambda e: e.memset(t_onesc_b.ap(), 1.0), sem=s_ve, inc=1)
    emit("vector", lambda e: e.memset(t_onesm.ap(), 1.0), sem=s_ve, inc=1)
    emit("vector", lambda e: e.memset(t_onesr_b.ap(), 1.0), sem=s_ve, inc=1)
    n_base_ve = emit("vector", lambda e: e.memset(t_eps.ap(), EPS), sem=s_ve, inc=1)

    DQ0 = 160 * 1024
    O_DQZ = 131072
    O_DQU = 132480
    O_DQR = 143488
    O_DQW = 154496
    ZON = [133120, 133120 + 13760]
    dqc = {"ve": 0, "scst": 0}
    dqp = {"g": 0, "guard": [None, None], "mmg": [None, None]}
    dq_mode = ["pe"]
    dqz = [{"ve": None, "st": None}, {"ve": None, "st": None}]
    dq_sems = [(s_dqld, s_dqst), (s_dqld2, s_dqst2)]
    dq_zi = [0]

    dequant_tasks = []

    def zp_task(osl, gu):
        def gen():
            if gu:
                pw, width = 344, 2752
                src_qz = d_gqz.reshape([128, 688])[:, osl * pw:(osl + 1) * pw]
                src_sc = d_gsc.reshape([128, 5504])[:, osl * width:(osl + 1) * width]
                dst = s_zpg.reshape([128, 5504])[:, osl * width:(osl + 1) * width]
            else:
                pw, width = 344, 2752
                src_qz = d_dqz.reshape([128, 344])[:, :]
                src_sc = d_dsc.reshape([128, 2752])[:, :]
                dst = s_zpd.reshape([128, 2752])[:, :]
            qz = vu(O_DQZ, pw)
            znib = vu(O_DQU, width)
            znib3 = znib.rearrange("p (c k) -> p c k", k=8)
            scv = vf(O_DQR, width)
            outb = vb(O_DQW, width)
            wz = [(s_ve, dqc["ve"]), (s_dqst, dqc["scst"])]
            n1 = dma("gpsimd", s_dqld, qz, src_qz, waits=wz)
            n2 = dma("gpsimd", s_dqld, scv, src_sc)
            for k in range(8):
                emit("vector",
                     lambda e, o=znib3[:, :, k:k + 1], i=qz.unsqueeze(2), sh=4 * k:
                     e.tensor_scalar(o, i, float(sh), 15.0,
                                     ALU.logical_shift_right, ALU.bitwise_and),
                     waits=[(s_dqld, n2), (s_ve, s_ve.n)], sem=s_ve, inc=1)
            nv = emit("vector",
                      lambda e, o=outb, i=znib, s=scv:
                      e.scalar_tensor_tensor(o, i, -7.0, s, ALU.add, ALU.mult),
                      waits=[(s_dqld, n2), (s_ve, s_ve.n), (s_dqst, dqc["scst"])],
                      sem=s_ve, inc=1)
            ns = dma("scalar", s_dqst, dst, outb, waits=[(s_ve, nv)])
            dqc["ve"] = nv
            dqc["scst"] = ns
            for z in range(2):
                dqz[z]["ve"] = (s_ve, nv)
                dqz[z]["st"] = (s_dqst, ns)
        return gen

    def hl_task(h, gu):
        def gen():
            if gu:
                src = d_gsc.reshape([128, 5504])[:, h * 2752:(h + 1) * 2752]
                dsth = d_ghl.reshape([2, 128, 5504])[0, :, h * 2752:(h + 1) * 2752]
                dstl = d_ghl.reshape([2, 128, 5504])[1, :, h * 2752:(h + 1) * 2752]
            else:
                src = d_dsc.reshape([128, 2752])[:, :]
                dsth = d_dhl.reshape([2, 128, 2752])[0, :, :]
                dstl = d_dhl.reshape([2, 128, 2752])[1, :, :]
            scf = vf(O_DQU, 2752)
            hib = vb(O_DQR, 2752)
            lob = vb(O_DQW, 2752)
            wz = [(s_ve, dqc["ve"]), (s_dqst, dqc["scst"])]
            n1 = dma("gpsimd", s_dqld, scf, src, waits=wz)
            nh = emit("vector", lambda e: e.tensor_copy(hib, scf),
                      waits=[(s_dqld, n1), (s_ve, s_ve.n), (s_dqst, dqc["scst"])],
                      sem=s_ve, inc=1)
            nl = emit("vector",
                      lambda e: e.tensor_tensor(lob, scf, hib, ALU.subtract),
                      waits=[(s_ve, s_ve.n)], sem=s_ve, inc=1)
            dma("scalar", s_dqst, dsth, hib, waits=[(s_ve, nh)])
            ns2 = dma("scalar", s_dqst, dstl, lob, waits=[(s_ve, nl)])
            dqc["ve"] = nl
            dqc["scst"] = ns2
            for z in range(2):
                dqz[z]["ve"] = (s_ve, nl)
                dqz[z]["st"] = (s_dqst, ns2)
        return gen

    def w1_task(it, osl, gu):
        def gen():
            if gu:
                width = 1376
                src_w = d_gqw[it * 16:(it + 1) * 16, osl * width:(osl + 1) * width]
                src_sc = d_gsc[it:it + 1, osl * width:(osl + 1) * width]
                dst = s_w1g[it * 128:(it + 1) * 128, osl * width:(osl + 1) * width]
                hs = it * GUC + osl * width
                hl_src = d_ghl[:, hs:hs + width]
            else:
                width = 1024
                src_w = d_dqw[it * 16:(it + 1) * 16, osl * width:(osl + 1) * width]
                src_sc = d_dsc[it:it + 1, osl * width:(osl + 1) * width]
                dst = s_w1d[it * 128:(it + 1) * 128, osl * width:(osl + 1) * width]
                hs = it * DM + osl * width
                hl_src = d_dhl[:, hs:hs + width]
            z = dq_zi[0] % 2
            dq_zi[0] += 1
            sld, sst = dq_sems[z]
            qbuf = vu(ZON[z], width)
            w1 = vb(ZON[z] + 5504, width)
            hib = vb(ZON[z] + 8256, width, 1)
            lob = vb(ZON[z] + 11008, width, 1)
            wz = [w for w in (dqz[z]["ve"], dqz[z]["st"]) if w is not None]
            prev_store = dqz[z]["st"]
            dma("gpsimd", sld, qbuf,
                src_w.unsqueeze(1).broadcast_to([16, 8, width]), waits=wz)
            dma("gpsimd", sld, hib, hl_src[0:1, :])
            n2 = dma("gpsimd", sld, lob, hl_src[1:2, :])
            emit("vector",
                 lambda e, o=qbuf, i=qbuf:
                 e.tensor_scalar(o, i, t_shiftv.ap(), 15.0,
                                 ALU.logical_shift_right, ALU.bitwise_and),
                 waits=[(sld, n2), (s_ldu, n_consts)], sem=s_ve, inc=1)
            last_ve = None
            c0 = 0
            ch = 0
            while c0 < width:
                cw = min(512, width - c0)
                sl = dqp["g"] % 2
                b = 6 + sl
                dqp["g"] += 1
                gg = dqp["guard"][sl]
                mmw = [(sld, n2), (s_ve, n_base_ve)] + ([gg] if gg else [])
                emit("tensor",
                     lambda e, b=b, c0=c0, cw=cw:
                     e.matmul(psf[b][:, 0:cw], t_onesr_b.ap(),
                              hib[0:1, c0:c0 + cw], start=True, stop=False),
                     waits=mmw, sem=s_te, inc=0)
                n_mm = emit("tensor",
                            lambda e, b=b, c0=c0, cw=cw:
                            e.matmul(psf[b][:, 0:cw], t_onesr_b.ap(),
                                     lob[0:1, c0:c0 + cw], start=False, stop=True),
                            sem=s_te, inc=1)
                stw = [(s_te, n_mm), (s_ve, s_ve.n)]
                if ch == 0 and prev_store:
                    stw.append(prev_store)
                last_ve = emit("vector",
                               lambda e, c0=c0, cw=cw, b=b:
                               e.scalar_tensor_tensor(w1[:, c0:c0 + cw],
                                                      qbuf[:, c0:c0 + cw], -8.0,
                                                      psf[b][:, 0:cw],
                                                      ALU.add, ALU.mult),
                               waits=stw, sem=s_ve, inc=1)
                dqp["guard"][sl] = (s_ve, last_ve)
                c0 += cw
                ch += 1
            ns = dma("scalar", sst, dst, w1, waits=[(s_ve, last_ve)])
            dqz[z]["ve"] = (s_ve, last_ve)
            dqz[z]["st"] = (sst, ns)
        return gen

    for i in range(2):
        dequant_tasks.append(zp_task(i, True))
    dequant_tasks.append(zp_task(0, False))
    dequant_tasks.append(hl_task(0, True))
    dequant_tasks.append(hl_task(1, True))
    dequant_tasks.append(hl_task(0, False))
    n_gu = 16
    n_dn = 4
    for it in range(IC):
        for osl in range(n_gu):
            dequant_tasks.append(w1_task(it, osl, True))
    for it in range(GJ):
        for osl in range(n_dn):
            dequant_tasks.append(w1_task(it, osl, False))
    if "dq" in skip:
        dequant_tasks.clear()
    for tok in skip:
        if tok.startswith("dqn"):
            del dequant_tasks[int(tok[3:]):]
    dq_idx = [0]

    def drain_dq(k):
        while k > 0 and dq_idx[0] < len(dequant_tasks):
            dequant_tasks[dq_idx[0]]()
            dq_idx[0] += 1
            k -= 1

    O_WBIN = 0
    O_SGN = 65536
    O_BSTG = 98304
    stg = vb(O_BSTG, IC * 512)
    stg3 = stg.rearrange("p (c n) -> p c n", c=IC)
    wspec = [(d_qw, s_binq, 32, 0, True),
             (d_kw, s_bink, 8, 32, False),
             (d_vw, s_binv, 8, 40, False),
             (d_ow, s_bino, 32, 48, False)]
    stg_guard = [None]
    sgn_guard = [None]
    n_bin_stores = 0
    if "bin" in skip:
        wspec = []
    for (dsrc, ddst, noc, svbase, isq) in wspec:
        for ocg in range(noc // 4):
            lw = []
            for k in range(4):
                oc = ocg * 4 + k
                wv = vf(O_WBIN + k * 16384, DM)
                lw.append((wb4[k], dma(["sync", "gpsimd"][k % 2], wb4[k], wv,
                                       dsrc[oc * 128:(oc + 1) * 128, :],
                                       waits=[(s_sc, s_sc.n), (s_ldu, n_consts)])))
            nsg = []
            for k in range(4):
                oc = ocg * 4 + k
                wv = vf(O_WBIN + k * 16384, DM)
                sg = vb(O_SGN + k * 8192, DM)
                sw = [lw[k]]
                if k == 0 and sgn_guard[0] is not None:
                    sw.append(sgn_guard[0])
                    sgn_guard[0] = None
                nsg.append(emit("scalar", lambda e, o=sg, i=wv: e.activation(o, i, AF.Sign),
                                waits=sw, sem=s_sc, inc=1))
                emit("vector",
                     lambda e, i=wv: e.reduce_sum(t_stat[:, 8:9], i, AX.X,
                                                  apply_absolute_value=True),
                     waits=[lw[k], (s_ve, s_ve.n)], sem=s_ve, inc=1)
                emit("vector",
                     lambda e: e.tensor_scalar(t_stat[:, 9:10], t_stat[:, 8:9], 1.0 / DM,
                                               1e-8, ALU.mult, ALU.max),
                     waits=[(s_ve, s_ve.n)], sem=s_ve, inc=1)
                emit("vector",
                     lambda e, col=svbase + oc, m=(RT128 if isq else 1.0):
                     e.tensor_scalar(t_svec[:, col:col + 1], t_stat[:, 9:10],
                                     t_qsin[:, col:col + 1], m, ALU.mult, ALU.mult),
                     waits=[(s_ve, s_ve.n)], sem=s_ve, inc=1)
            nev = None
            for icb in range(IC):
                if icb % 3 == 0:
                    drain_dq(0 if "nodq0" in skip else 1)
                b = 4 + (icb % 2)
                tws = bw(b) + [(s_sc, nsg[3]), (s_ve, n_base_ve)]
                if icb == 0 and stg_guard[0] is not None:
                    tws.append(stg_guard[0])
                    stg_guard[0] = None
                ntp = None
                for k in range(4):
                    sg = vb(O_SGN + k * 8192, DM)
                    ntp = emit("tensor",
                               lambda e, b=b, k=k, i=sg, ic=icb:
                               e.transpose(psb[b][:, k * 128:(k + 1) * 128],
                                           i[:, ic * 128:(ic + 1) * 128], t_identb.ap()),
                               waits=(tws if k == 0 else []), sem=s_te,
                               inc=(1 if k == 3 else 0))
                nev = emit("vector",
                           lambda e, b=b, ic=icb:
                           e.tensor_copy(stg[:, ic * 512:(ic + 1) * 512], psb[b][:, 0:512]),
                           waits=[(s_te, ntp)], sem=s_ve, inc=1)
                bankg[b] = (s_ve, nev)
            sgn_guard[0] = (s_te, s_te.n)
            dst = ddst[:, ocg * 512:(ocg + 1) * 512].rearrange("(c p) n -> p c n", p=128)
            n_bin_stores = dma("scalar", s_scst, dst, stg3, waits=[(s_ve, nev)])
            stg_guard[0] = (s_scst, n_bin_stores)

    if PH == 0:
        drain_dq(10 ** 9)
        return _finish(nc, plan, es, fence, dma, emit, s_ld, s_gp, dumps, locals())
    fence()
    O_XLNT = 0
    O_XBUF = 32768
    O_XLNB = 65536
    O_XSQ = 116736
    O_WRNG = 90112
    O_KSTG = 106496
    O_VSTG = 110592
    O_QSTG = 112640

    XLNT_O = [O_XLNT, 161792]
    def xlnt_t(ic, h):
        return vb(XLNT_O[h] + ic * 1024, 512)
    def xlnt3h(h):
        return vb(XLNT_O[h], IC * 512).rearrange("p (c n) -> p c n", c=IC)
    xlnt_guard = [None, None]
    wring = [vb(O_WRNG + i * 1024, 512) for i in range(8)]
    wring_g = [None] * 8
    wring_i = [0]
    kstg = [vb(O_KSTG + i * 1024, 512) for i in range(4)]
    kstg_g = [None] * 4
    vstg = [vb(O_VSTG + i * 1024, 512) for i in range(2)]
    vstg_g = [None] * 2
    qstg = [vb(O_QSTG + i * 1024, 512) for i in range(4)]
    qstg_g = [None] * 4
    xbuf_g = [None, None]
    xlnb_g = [None, None]
    n_q_store = [0]

    XLNB_O = [O_XLNB, O_XLNB + 8192, 81920, 98304]
    xlnb_g4 = [None] * 4

    def emit_ln_ve(sb):
        own = sb < NQB
        xsrc = d_xown if own else d_xoth
        r0 = (sb % NQB) * TB
        naps = []
        for tt in range(4):
            xv = vf(O_XBUF + (tt % 2) * 16384, DM)
            xlv = vb(XLNB_O[tt], DM)
            xq = vf(O_XSQ, DM)
            wts = [(s_ldu, n_consts)]
            if xbuf_g[tt % 2] is not None:
                wts.append(xbuf_g[tt % 2])
                xbuf_g[tt % 2] = None
            nldx = dma("sync", xb2[tt % 2], xv,
                       xsrc[r0 + tt * 128: r0 + (tt + 1) * 128, :], waits=wts)
            emit("scalar", lambda e, i=xv, o=xq:
                 e.activation(o, i, AF.Copy, accum_out=t_stat[:, 0:1]),
                 waits=[(xb2[tt % 2], nldx), (s_sc, s_sc.n)], sem=s_sc, inc=1)
            n_sqs = emit("scalar", lambda e, i=xv, o=xq:
                         e.activation(o, i, AF.Square, accum_out=t_stat[:, 1:2]),
                         waits=[(s_sc, s_sc.n)], sem=s_sc, inc=1)
            emit("vector", lambda e: e.tensor_scalar(t_stat[:, 2:3], t_stat[:, 0:1],
                                                     1.0 / DM, None, ALU.mult),
                 waits=[(s_sc, n_sqs), (s_ve, s_ve.n)], sem=s_ve, inc=1)
            emit("vector", lambda e: e.tensor_scalar(t_stat[:, 3:4], t_stat[:, 1:2],
                                                     1.0 / DM, None, ALU.mult),
                 waits=[(s_ve, s_ve.n)], sem=s_ve, inc=1)
            emit("vector", lambda e: e.tensor_tensor(t_stat[:, 4:5], t_stat[:, 2:3],
                                                     t_stat[:, 2:3], ALU.mult),
                 waits=[(s_ve, s_ve.n)], sem=s_ve, inc=1)
            emit("vector", lambda e: e.tensor_tensor(t_stat[:, 5:6], t_stat[:, 3:4],
                                                     t_stat[:, 4:5], ALU.subtract),
                 waits=[(s_ve, s_ve.n)], sem=s_ve, inc=1)
            nvar = emit("vector", lambda e: e.tensor_scalar(t_stat[:, 5:6], t_stat[:, 5:6],
                                                            0.0, None, ALU.max),
                        waits=[(s_ve, s_ve.n)], sem=s_ve, inc=1)
            nsq = emit("scalar", lambda e: e.activation(t_stat[:, 6:7], t_stat[:, 5:6],
                                                        AF.Sqrt, bias=t_eps[:, 0:1]),
                       waits=[(s_ve, nvar)], sem=s_sc, inc=1)
            emit("vector", lambda e: e.reciprocal(t_stat[:, 7:8], t_stat[:, 6:7]),
                 waits=[(s_sc, nsq), (s_ve, s_ve.n)], sem=s_ve, inc=1)
            emit("vector", lambda e: e.tensor_tensor(t_stat[:, 8:9], t_stat[:, 2:3],
                                                     t_stat[:, 7:8], ALU.mult),
                 waits=[(s_ve, s_ve.n)], sem=s_ve, inc=1)
            emit("vector", lambda e: e.tensor_scalar(t_stat[:, 9:10], t_stat[:, 8:9],
                                                     -1.0, None, ALU.mult),
                 waits=[(s_ve, s_ve.n)], sem=s_ve, inc=1)
            wla = []
            if xlnb_g4[tt] is not None:
                wla.append(xlnb_g4[tt])
                xlnb_g4[tt] = None
            nap = emit("vector", lambda e, i=xv, o=xlv:
                       e.tensor_scalar(o, i, t_stat[:, 7:8], t_stat[:, 9:10],
                                       ALU.mult, ALU.add),
                       waits=wla + [(s_ve, s_ve.n)], sem=s_ve, inc=1)
            xbuf_g[tt % 2] = (s_ve, nap)
            naps.append(nap)
        return naps

    def emit_ln_pe(half, naps):
        n_xlnt_ready = None
        for tt in range(4):
            xlv = vb(XLNB_O[tt], DM)
            for q4 in range(8):
                b = 4 + (q4 % 2)
                tws = bw(b) + [(s_ve, naps[tt])]
                if tt == 0 and q4 == 0:
                    tws.append((s_te, s_te.n))
                    if xlnt_guard[half] is not None:
                        tws.append(xlnt_guard[half])
                        xlnt_guard[half] = None
                ntp = None
                for j in range(4):
                    icb = q4 * 4 + j
                    ntp = emit("tensor",
                               lambda e, b=b, j=j, ic=icb, i=xlv:
                               e.transpose(psb[b][:, j * 128:(j + 1) * 128],
                                           i[:, ic * 128:(ic + 1) * 128], t_identb.ap()),
                               waits=(tws if j == 0 else []), sem=s_te,
                               inc=(1 if j == 3 else 0))
                nev = emit("vector",
                           lambda e, b=b, q4=q4, tt=tt, h=half:
                           e.tensor_copy(xlnt3h(h)[:, q4 * 4:(q4 + 1) * 4,
                                                   tt * 128:(tt + 1) * 128],
                                         psb[b][:, 0:512]),
                           waits=[(s_te, ntp)], sem=s_ve, inc=1)
                bankg[b] = (s_ve, nev)
                n_xlnt_ready = nev
            xlnb_g4[tt] = (s_te, s_te.n)
        return n_xlnt_ready

    nxr_cur = emit_ln_pe(0, emit_ln_ve(0))
    for sb in range(1 if "p1one" in skip else NSB):
        own = sb < NQB
        half = sb % 2
        drain_dq(2)

        def proj(bsrc, noc, svbase, store_fn, nxr=nxr_cur, h=half):
            for og in range(noc // 4):
                nmm = [None] * 4
                for icb in range(IC):
                    if icb % 4 == 0:
                        drain_dq(1)
                    slot = wring_i[0] % 8
                    wring_i[0] += 1
                    wv = wring[slot]
                    wwt = []
                    if wring_g[slot] is not None:
                        wwt.append(wring_g[slot])
                        wring_g[slot] = None
                    nld = dma(["sync", "gpsimd"][slot % 2], ring8[slot], wv,
                              bsrc[icb * 128:(icb + 1) * 128, og * 512:(og + 1) * 512],
                              waits=wwt)
                    for k in range(4):
                        tws = [(ring8[slot], nld)] if k == 0 else []
                        if icb == 0:
                            tws = tws + bw(k) + [(s_ve, nxr)]
                        nmm[k] = emit("tensor",
                                      lambda e, k=k, s=slot, ic=icb, h=h:
                                      e.matmul(psf[k][:, 0:512],
                                               wring[s][:, k * 128:(k + 1) * 128],
                                               xlnt_t(ic, h),
                                               start=(ic == 0), stop=(ic == IC - 1)),
                                      waits=tws, sem=s_te, inc=(1 if k == 3 else 0))
                    wring_g[slot] = (s_te, nmm[3])
                for k in range(4):
                    store_fn(og * 4 + k, k, nmm[3])

        def kvstore_mk(which):
            def f(oc, k, nmmlast):
                sslot = oc % 4
                swt = [(s_te, nmmlast)]
                if kstg_g[sslot] is not None:
                    swt.append(kstg_g[sslot])
                    kstg_g[sslot] = None
                sv = 32 if which == "k" else 40
                nev = emit("scalar",
                           lambda e, k=k, oc=oc, o=kstg[sslot], sv=sv:
                           e.activation(o, psf[k][:, 0:512], AF.Copy,
                                        scale=t_svec[:, sv + oc: sv + oc + 1]),
                           waits=swt, sem=s_sc, inc=1)
                bankg[k] = (s_sc, nev)
                if which == "k":
                    n = dma("scalar", stk4[sslot],
                            s_kscr[oc * 128:(oc + 1) * 128, sb * 512:(sb + 1) * 512],
                            kstg[sslot])
                    kstg_g[sslot] = (stk4[sslot], n)
                else:
                    b = 4 + (oc % 2)
                    tws = bw(b) + [(s_sc, nev)]
                    ntp = None
                    for j in range(4):
                        ntp = emit("tensor",
                                   lambda e, b=b, j=j, i=kstg[sslot]:
                                   e.transpose(psb[b][:, j * 128:(j + 1) * 128],
                                               i[:, j * 128:(j + 1) * 128], t_identb.ap()),
                                   waits=(tws if j == 0 else []), sem=s_te,
                                   inc=(1 if j == 3 else 0))
                    kstg_g[sslot] = (s_te, ntp)
                    vslot = oc % 2
                    vwt = [(s_te, ntp)]
                    if vstg_g[vslot] is not None:
                        vwt.append(vstg_g[vslot])
                        vstg_g[vslot] = None
                    nev2 = emit("vector",
                                lambda e, b=b, o=vstg[vslot]:
                                e.tensor_copy(o, psb[b][:, 0:512]),
                                waits=vwt, sem=s_ve, inc=1)
                    bankg[b] = (s_ve, nev2)
                    dstv = s_vscr[oc, sb * 512:(sb + 1) * 512, :]\
                        .rearrange("(j p) d -> p j d", p=128)
                    srcv = vstg[vslot].rearrange("p (j d) -> p j d", j=4)
                    n = dma("scalar", stv2[vslot], dstv, srcv, waits=[(s_ve, nev2)])
                    vstg_g[vslot] = (stv2[vslot], n)
            return f

        proj(s_bink, 8, 32, kvstore_mk("k"))
        naps_next = None
        if "p1one" not in skip and sb + 1 < NSB:
            naps_next = emit_ln_ve(sb + 1)
        proj(s_binv, 8, 40, kvstore_mk("v"))

        if own:
            qb = sb
            pend_q = []

            def qstore(oc, k, nmmlast, qb=qb, pend=pend_q):
                sslot = oc % 4
                swt = [(s_te, nmmlast)]
                if qstg_g[sslot] is not None:
                    swt.append(qstg_g[sslot])
                    qstg_g[sslot] = None
                nev = emit("scalar",
                           lambda e, k=k, oc=oc, o=qstg[sslot]:
                           e.activation(o, psf[k][:, 0:512], AF.Copy,
                                        scale=t_svec[:, oc: oc + 1]),
                           waits=swt, sem=s_sc, inc=1)
                bankg[k] = (s_sc, nev)
                pend.append((oc, sslot, nev))
                if len(pend) == 4:
                    og = pend[0][0] // 4
                    dstq = s_qscr[qb * DM + og * 512: qb * DM + (og + 1) * 512, :]\
                        .rearrange("(c p) n -> p c n", p=128)
                    srcq = vb(O_QSTG, 4 * 512).rearrange("p (c n) -> p c n", c=4)
                    n = dma("scalar", stq1, dstq, srcq, waits=[(s_sc, pend[-1][2])])
                    for (_, ss, _) in pend:
                        qstg_g[ss] = (stq1, n)
                    n_q_store[0] = n
                    pend.clear()
            proj(s_binq, 32, 0, qstore)
        if sb == 0 and "xlndbg" in skip:
            dbgstore("xlnt", vb(O_XLNT, IC * 512), [128, IC * 512], BF16)
            dbgstore("stat", t_stat[:, 0:24], [128, 24], F32)
        xlnt_guard[half] = (s_te, s_te.n)
        if naps_next is not None:
            nxr_cur = emit_ln_pe((sb + 1) % 2, naps_next)


    if PH == 1:
        drain_dq(10 ** 9)
        return _finish(nc, plan, es, fence, dma, emit, s_ld, s_gp, dumps, locals())
    drain_dq(10 ** 9)
    fence()
    O_KBUF = 0
    O_VBUF = 65536
    O_QRNG = 131072
    O_ABUF = 139264
    O_PBUF = 143360
    O_RBUF = 147456
    O_SQS = 151552
    O_KNS = 152576

    def kb_t(h):
        return vb(O_KBUF + h * 8192, SEQ)

    def vb_t(h, c):
        return vb(O_VBUF + (h * IC + c) * 256, 128)

    def qr_t(slot, hh):
        return vb(O_QRNG + slot * 4096 + hh * 1024, 512)
    qrng_g = [None] * 2
    abuf = [vb(O_ABUF + i * 1024, 512) for i in range(4)]
    abuf_g = [None] * 4
    pbuf = [vb(O_PBUF + i * 1024, 512) for i in range(4)]
    pbuf_g = [None] * 4
    rbuf = [vf(O_RBUF + i * 2048, 512) for i in range(2)]
    rbuf_g = [None] * 2
    sqs = vb(O_SQS, 512)
    kns = vb(O_KNS, DM)

    dma("sync", s_ld4a, vb(O_KBUF, NKVH * SEQ).rearrange("p (h n) -> p h n", h=NKVH),
        s_kscr.ap().rearrange("(h p) n -> p h n", p=128))
    n_vload = dma("sync", s_ld4a,
                  vb(O_VBUF, NKVH * IC * 128).rearrange("p (h c d) -> p h c d",
                                                        h=NKVH, c=IC),
                  s_vscr.ap().rearrange("h (c p) d -> p h c d", p=128))

    kns_guard = [None]
    if "nonorms" in skip:
        emit("vector", lambda e: e.memset(t_ebias.ap(), -8.0), sem=s_ve, inc=1)
    for kvh in range(NKVH) if "nonorms" not in skip else []:
        sw = [(s_ld4a, n_vload)]
        if kns_guard[0] is not None:
            sw.append(kns_guard[0])
            kns_guard[0] = None
        nqq = emit("scalar", lambda e, h=kvh: e.activation(kns, kb_t(h), AF.Square),
                   waits=sw, sem=s_sc, inc=1)
        nt = None
        for sl in range(8):
            tws = [(s_sc, nqq)] + bw(6)
            nt = emit("tensor",
                      lambda e, sl=sl: e.matmul(psf[6][:, 0:512], t_onesm.ap(),
                                                kns[:, sl * 512:(sl + 1) * 512],
                                                start=True, stop=True),
                      waits=tws, sem=s_te, inc=1)
            if sl == 0:
                nv = emit("vector",
                          lambda e, h=kvh:
                          e.reduce_max(t_mk2[0:1, h:h + 1], psf[6][0:1, 0:512], AX.X),
                          waits=[(s_te, nt), (s_ve, s_ve.n)], sem=s_ve, inc=1)
            else:
                emit("vector",
                     lambda e: e.reduce_max(t_stat[0:1, 16:17], psf[6][0:1, 0:512], AX.X),
                     waits=[(s_te, nt), (s_ve, s_ve.n)], sem=s_ve, inc=1)
                nv = emit("vector",
                          lambda e, h=kvh:
                          e.tensor_tensor(t_mk2[0:1, h:h + 1], t_mk2[0:1, h:h + 1],
                                          t_stat[0:1, 16:17], ALU.max),
                          waits=[(s_ve, s_ve.n)], sem=s_ve, inc=1)
            bankg[6] = (s_ve, nv)
        kns_guard[0] = (s_te, nt)

    n_ascr_store = [0]
    sqs_guard = [None]
    crow_guard = [None]
    ebias_guard = [None]
    for qb in range(1 if ("p4aone" in skip or "p4aqb1" in skip) else NQB):
        for kvh in range(1 if "p4aone" in skip else (2 if "p4akv2" in skip else NKVH)):
            qslot = (qb * NKVH + kvh) % 2
            qwt = [(stq1, n_q_store[0])]
            if qrng_g[qslot] is not None:
                qwt.append(qrng_g[qslot])
                qrng_g[qslot] = None
            nq = dma("sync", ring8[qslot],
                     vb(O_QRNG + qslot * 4096, 4 * 512)
                     .rearrange("p (h n) -> p h n", h=4),
                     s_qscr[qb * DM + kvh * 512: qb * DM + (kvh + 1) * 512, :]
                     .rearrange("(h p) n -> p h n", p=128), waits=qwt)
            for hh in range(4) if "nonorms" not in skip else []:
                sw = [(ring8[qslot], nq)]
                if sqs_guard[0] is not None:
                    sw.append(sqs_guard[0])
                    sqs_guard[0] = None
                nsq2 = emit("scalar",
                            lambda e, hh=hh, s=qslot:
                            e.activation(sqs, qr_t(s, hh), AF.Square),
                            waits=sw, sem=s_sc, inc=1)
                tws = [(s_sc, nsq2)] + bw(6)
                nt = emit("tensor",
                          lambda e: e.matmul(psf[6][:, 0:512], t_onesm.ap(), sqs,
                                             start=True, stop=True),
                          waits=tws, sem=s_te, inc=1)
                sqs_guard[0] = (s_te, nt)
                nv = emit("vector",
                          lambda e, hh=hh:
                          e.reduce_max(t_mq2[0:1, hh:hh + 1], psf[6][0:1, 0:512], AX.X),
                          waits=[(s_te, nt), (s_ve, s_ve.n)], sem=s_ve, inc=1)
                bankg[6] = (s_ve, nv)
            cw = []
            if crow_guard[0] is not None:
                cw.append(crow_guard[0])
                crow_guard[0] = None
            if "nonorms" in skip:
                ew = [ebias_guard[0]] if ebias_guard[0] is not None else []
                ebias_guard[0] = None
                nveb = emit("vector", lambda e: e.memset(t_ebias.ap(), -8.0),
                            waits=ew, sem=s_ve, inc=1)
            else:
                nv = emit("vector",
                          lambda e, h=kvh:
                          e.tensor_single_scalar(t_crow.ap(), t_mq2.ap(),
                                                 t_mk2[0:1, h:h + 1], ALU.mult),
                          waits=cw + [(s_ve, s_ve.n)], sem=s_ve, inc=1)
                nsq3 = emit("scalar",
                            lambda e: e.activation(t_crow.ap(), t_crow.ap(), AF.Sqrt),
                            waits=[(s_ve, nv)], sem=s_sc, inc=1)
                nv2 = emit("vector",
                           lambda e: e.tensor_scalar(t_crow.ap(), t_crow.ap(), -1.0, -1.0,
                                                     ALU.mult, ALU.add),
                           waits=[(s_sc, nsq3), (s_ve, s_ve.n)], sem=s_ve, inc=1)
                tws = [(s_ve, nv2)] + bw(6)
                if ebias_guard[0] is not None:
                    tws.append(ebias_guard[0])
                    ebias_guard[0] = None
                ntb = emit("tensor",
                           lambda e: e.matmul(psf[6][:, 0:4], t_onesr_f.ap(), t_crow.ap(),
                                              start=True, stop=True),
                           waits=tws, sem=s_te, inc=1)
                crow_guard[0] = (s_te, ntb)
                nveb = emit("vector", lambda e: e.tensor_copy(t_ebias.ap(), psf[6][:, 0:4]),
                            waits=[(s_te, ntb)], sem=s_ve, inc=1)
                bankg[6] = (s_ve, nveb)

            last_exp = None
            heads = [] if "noheads" in skip else list(range(4))
            for hh in heads:
                h = kvh * 4 + hh
                so = 2 + (h % 2)
                db = 4
                SBANKS = (0, 1, 5, 6)
                n_smm = [None] * IC
                n_exp = [None] * IC
                n_dmm = [None] * IC

                def emit_S(c):
                    sbk = SBANKS[c % 4]
                    if c >= 4:
                        tws = [(s_sc, n_exp[c - 4])]
                    else:
                        tws = bw(sbk)
                    if c == 0:
                        tws = tws + [(s_ve, nveb), (ring8[qslot], nq)]
                    n_smm[c] = emit("tensor",
                                    lambda e, c=c, sbk=sbk, h=kvh, hh=hh, s=qslot:
                                    e.matmul(psf[sbk][:, 0:512],
                                             kb_t(h)[:, c * 128:(c + 1) * 128],
                                             qr_t(s, hh),
                                             start=True, stop=True),
                                    waits=tws, sem=s_te, inc=1)

                def emit_exp(c):
                    sbk = SBANKS[c % 4]
                    tws = [(s_te, n_smm[c])]
                    pslot = c % 4
                    if pbuf_g[pslot] is not None:
                        tws.append(pbuf_g[pslot])
                        pbuf_g[pslot] = None
                    n_exp[c] = emit("scalar",
                                    lambda e, sbk=sbk, hh=hh, p=pslot:
                                    e.activation(pbuf[p], psf[sbk][:, 0:512], AF.Exp,
                                                 bias=t_ebias[:, hh:hh + 1]),
                                    waits=tws, sem=s_sc, inc=1)
                    bankg[sbk] = (s_sc, n_exp[c])

                def emit_PVD(c):
                    pslot = c % 4
                    tws = [(s_sc, n_exp[c])]
                    if c == 0:
                        tws = tws + bw(so) + bw(db)
                    if c < IC - 1:
                        emit("tensor",
                             lambda e, c=c, h=kvh, p=pslot, so=so:
                             e.matmul(psf[so][:, 0:512], vb_t(h, c), pbuf[p],
                                      start=(c == 0), stop=False),
                             waits=tws, sem=s_te, inc=0)
                        n_dmm[c] = emit("tensor",
                                        lambda e, c=c, p=pslot, db=db:
                                        e.matmul(psf[db][:, 0:512], t_onesm.ap(),
                                                 pbuf[p], start=(c == 0), stop=False),
                                        waits=[], sem=s_te, inc=1)
                    else:
                        emit("tensor",
                             lambda e, c=c, p=pslot, db=db:
                             e.matmul(psf[db][:, 0:512], t_onesm.ap(), pbuf[p],
                                      start=False, stop=True),
                             waits=tws, sem=s_te, inc=0)
                        n_dmm[c] = emit("tensor",
                                        lambda e, c=c, h=kvh, p=pslot, so=so:
                                        e.matmul(psf[so][:, 0:512], vb_t(h, c), pbuf[p],
                                                 start=False, stop=True),
                                        waits=[], sem=s_te, inc=1)
                    pbuf_g[pslot] = (s_te, n_dmm[c])

                emit_S(0)
                emit_exp(0)
                emit_S(1)
                emit_exp(1)
                if "sonly" in skip:
                    for c in range(IC):
                        if c + 2 < IC:
                            emit_S(c + 2)
                            emit_exp(c + 2)
                        pbuf_g[c % 4] = (s_sc, n_exp[c])
                    last_exp = n_exp[IC - 1]
                    aslot = h % 4
                    awt = [(s_sc, last_exp)]
                    if abuf_g[aslot] is not None:
                        awt.append(abuf_g[aslot])
                        abuf_g[aslot] = None
                    nev = emit("vector",
                               lambda e, a=aslot: e.tensor_copy(abuf[a], pbuf[3]),
                               waits=awt, sem=s_ve, inc=1)
                    pbuf_g[3] = (s_ve, nev)
                    na = dma("gpsimd", s_gp,
                             s_ascr[qb * DM + h * 128: qb * DM + (h + 1) * 128, :],
                             abuf[aslot], waits=[(s_ve, nev)])
                    abuf_g[aslot] = (s_gp, na)
                    n_ascr_store[0] = na
                    continue
                for c in range(IC):
                    emit_PVD(c)
                    if c + 2 < IC:
                        emit_S(c + 2)
                        emit_exp(c + 2)
                last_exp = n_exp[IC - 1]
                if "nodiv" in skip:
                    aslot = h % 4
                    awt = [(s_te, n_dmm[IC - 1])] + bw(db)
                    if abuf_g[aslot] is not None:
                        awt.append(abuf_g[aslot])
                        abuf_g[aslot] = None
                    nev = emit("vector",
                               lambda e, a=aslot, so=so:
                               e.tensor_copy(abuf[a], psf[so][:, 0:512]),
                               waits=awt, sem=s_ve, inc=1)
                    bankg[so] = (s_ve, nev)
                    bankg[db] = (s_ve, nev)
                    na = dma("gpsimd", s_gp,
                             s_ascr[qb * DM + h * 128: qb * DM + (h + 1) * 128, :],
                             abuf[aslot], waits=[(s_ve, nev)])
                    abuf_g[aslot] = (s_gp, na)
                    n_ascr_store[0] = na
                    continue
                rslot = h % 2
                rwt = [(s_te, n_dmm[IC - 1]), (s_ve, s_ve.n)]
                if rbuf_g[rslot] is not None:
                    rwt.append(rbuf_g[rslot])
                    rbuf_g[rslot] = None
                nrb = emit("vector",
                           lambda e, r=rslot, db=db: e.reciprocal(rbuf[r], psf[db][:, 0:512]),
                           waits=rwt, sem=s_ve, inc=1)
                bankg[db] = (s_ve, nrb)
                aslot = h % 4
                awt = [(s_ve, nrb)]
                if abuf_g[aslot] is not None:
                    awt.append(abuf_g[aslot])
                    abuf_g[aslot] = None
                nev = emit("vector",
                           lambda e, a=aslot, r=rslot, so=so:
                           e.tensor_tensor(abuf[a], psf[so][:, 0:512], rbuf[r], ALU.mult),
                           waits=awt, sem=s_ve, inc=1)
                bankg[so] = (s_ve, nev)
                rbuf_g[rslot] = (s_ve, nev)
                na = dma("gpsimd", gpa4[aslot],
                         s_ascr[qb * DM + h * 128: qb * DM + (h + 1) * 128, :],
                         abuf[aslot], waits=[(s_ve, nev)])
                abuf_g[aslot] = (gpa4[aslot], na)
                n_ascr_store[0] = na
            qrng_g[qslot] = (s_te, s_te.n)
            if last_exp is not None:
                ebias_guard[0] = (s_sc, last_exp)
            drain_dq(8 if "oldpace" in skip else 16)

    drain_dq(10 ** 9)

    if PH == 2:
        return _finish(nc, plan, es, fence, dma, emit, s_ld, s_gp, dumps, locals())
    fence()
    O_X2B = 0
    O_X2LT = 32768
    O_HBUF = 65536
    O_WR4 = 153600
    O_OBF = 161792
    O_SGT = 163840
    O_DSTG = 165888
    O_OSTG = 169984
    O_NXG = 174080
    O_NXH = 175104
    O_ZG = 176128
    O_ZD = 178176
    O_X2LN = 179200

    x2t = [vb(O_X2B + j * 8192, DM) for j in range(4)]

    def x2lt_t(ic):
        return vb(O_X2LT + ic * 1024, 512)
    hbuf = [vb(O_HBUF + j * 1024, 512) for j in range(GJ)]
    wr4 = [vb(O_WR4 + i * 1024, 512) for i in range(8)]
    wr4_g = [None] * 8
    obf = vb(O_OBF, 512)
    sgt = vf(O_SGT, 512)
    dstg = [vf(O_DSTG + i * 2048, 512) for i in range(2)]
    dstg_g = [None] * 2
    ostg = [vf(O_OSTG + i * 2048, 512) for i in range(2)]
    ostg_g = [None] * 2
    nxg = vb(O_NXG, 512, 32)
    nxh = vb(O_NXH, 512, GJ)
    zgt = vb(O_ZG, 512, 32)
    zut = vb(O_ZG + 1024, 512, 32)
    zdt = vb(O_ZD, 512, GJ)
    x2ln = vb(O_X2LN, DM)

    n_out_store = [0]
    obf_g = [None]
    sgt_g = [None]
    zg_guard = [None]
    zd_guard = [None]
    x2ln_guard = [None]
    for qb in range(1 if "p4bone" in skip else NQB):
        ldq = s_ld4b[qb]
        n_x2i = []
        for j in range(4):
            n_x2i.append((gpc4[j], dma("gpsimd", gpc4[j], x2t[j],
                          d_xown[qb * 512 + j * 128: qb * 512 + (j + 1) * 128, :],
                          waits=[(s_ve, s_ve.n)] if j == 0 else [])))
        for og in range(8):
            nmm = [None] * 4
            for icb in range(IC):
                slot = icb % 4
                wwt = []
                if wr4_g[slot] is not None:
                    wwt.append(wr4_g[slot])
                    wr4_g[slot] = None
                nldw = dma("sync", ring8[slot], wr4[slot],
                           s_bino[icb * 128:(icb + 1) * 128, og * 512:(og + 1) * 512],
                           waits=wwt)
                aslot = 4 + icb % 4
                awt = []
                if wr4_g[aslot] is not None:
                    awt.append(wr4_g[aslot])
                    wr4_g[aslot] = None
                nlda = dma("sync", ring8[aslot], wr4[aslot],
                           s_ascr[qb * DM + icb * 128: qb * DM + (icb + 1) * 128, :],
                           waits=awt)
                for k in range(4):
                    tws = [(ring8[slot], nldw), (ring8[aslot], nlda)] if k == 0 else []
                    if icb == 0:
                        tws = tws + bw(k)
                    nmm[k] = emit("tensor",
                                  lambda e, k=k, s=slot, a=aslot, ic=icb:
                                  e.matmul(psf[k][:, 0:512],
                                           wr4[s][:, k * 128:(k + 1) * 128],
                                           wr4[a], start=(ic == 0), stop=(ic == IC - 1)),
                                  waits=tws, sem=s_te, inc=(1 if k == 3 else 0))
                wr4_g[slot] = (s_te, nmm[3])
                wr4_g[aslot] = (s_te, nmm[3])
            ntpl = None
            for k in range(4):
                oc = og * 4 + k
                swt = [(s_te, nmm[3])]
                if obf_g[0] is not None:
                    swt.append(obf_g[0])
                    obf_g[0] = None
                nevo = emit("scalar",
                            lambda e, k=k, oc=oc:
                            e.activation(obf, psf[k][:, 0:512], AF.Copy,
                                         scale=t_svec[:, 48 + oc: 48 + oc + 1]),
                            waits=swt, sem=s_sc, inc=1)
                bankg[k] = (s_sc, nevo)
                for j in range(4):
                    b = 4 + j
                    tws = [(s_sc, nevo)]
                    if k == 0:
                        tws = tws + bw(b)
                    ntpl = emit("tensor",
                                lambda e, b=b, j=j, k=k:
                                e.transpose(psb[b][:, k * 128:(k + 1) * 128],
                                            obf[:, j * 128:(j + 1) * 128], t_identb.ap()),
                                waits=tws, sem=s_te, inc=(1 if j == 3 else 0))
                obf_g[0] = (s_te, ntpl)
            for j in range(4):
                b = 4 + j
                awt = [(s_te, ntpl), n_x2i[j]]
                nadd = emit("vector",
                            lambda e, b=b, j=j, og=og:
                            e.tensor_tensor(x2t[j][:, og * 512:(og + 1) * 512],
                                            psb[b][:, 0:512],
                                            x2t[j][:, og * 512:(og + 1) * 512], ALU.add),
                            waits=awt, sem=s_ve, inc=1)
                bankg[b] = (s_ve, nadd)
        if qb == 0 and "x2dbg" in skip:
            for j in range(4):
                dbgstore(f"x2_{j}", x2t[j], [128, DM], BF16)
        n_x2lt_ready = None
        for j in range(4):
            lw = []
            if x2ln_guard[0] is not None:
                lw.append(x2ln_guard[0])
                x2ln_guard[0] = None
            emit("scalar", lambda e, j=j:
                 e.activation(x2ln, x2t[j], AF.Copy, accum_out=t_stat[:, 0:1]),
                 waits=lw + [(s_ve, s_ve.n), (s_sc, s_sc.n)], sem=s_sc, inc=1)
            n_sq2 = emit("scalar", lambda e, j=j:
                         e.activation(x2ln, x2t[j], AF.Square,
                                      accum_out=t_stat[:, 1:2]),
                         waits=[(s_sc, s_sc.n)], sem=s_sc, inc=1)
            emit("vector", lambda e: e.tensor_scalar(t_stat[:, 2:3], t_stat[:, 0:1],
                                                     1.0 / DM, None, ALU.mult),
                 waits=[(s_sc, n_sq2), (s_ve, s_ve.n)], sem=s_ve, inc=1)
            emit("vector", lambda e: e.tensor_scalar(t_stat[:, 3:4], t_stat[:, 1:2],
                                                     1.0 / DM, None, ALU.mult),
                 waits=[(s_ve, s_ve.n)], sem=s_ve, inc=1)
            emit("vector", lambda e: e.tensor_tensor(t_stat[:, 4:5], t_stat[:, 2:3],
                                                     t_stat[:, 2:3], ALU.mult),
                 waits=[(s_ve, s_ve.n)], sem=s_ve, inc=1)
            emit("vector", lambda e: e.tensor_tensor(t_stat[:, 5:6], t_stat[:, 3:4],
                                                     t_stat[:, 4:5], ALU.subtract),
                 waits=[(s_ve, s_ve.n)], sem=s_ve, inc=1)
            nvar = emit("vector", lambda e: e.tensor_scalar(t_stat[:, 5:6], t_stat[:, 5:6],
                                                            0.0, None, ALU.max),
                        waits=[(s_ve, s_ve.n)], sem=s_ve, inc=1)
            nsq = emit("scalar", lambda e: e.activation(t_stat[:, 6:7], t_stat[:, 5:6],
                                                        AF.Sqrt, bias=t_eps[:, 0:1]),
                       waits=[(s_ve, nvar)], sem=s_sc, inc=1)
            emit("vector", lambda e: e.reciprocal(t_stat[:, 7:8], t_stat[:, 6:7]),
                 waits=[(s_sc, nsq), (s_ve, s_ve.n)], sem=s_ve, inc=1)
            emit("vector", lambda e: e.tensor_tensor(t_stat[:, 8:9], t_stat[:, 2:3],
                                                     t_stat[:, 7:8], ALU.mult),
                 waits=[(s_ve, s_ve.n)], sem=s_ve, inc=1)
            emit("vector", lambda e: e.tensor_scalar(t_stat[:, 9:10], t_stat[:, 8:9],
                                                     -1.0, None, ALU.mult),
                 waits=[(s_ve, s_ve.n)], sem=s_ve, inc=1)
            nap = emit("vector", lambda e, j=j:
                       e.tensor_scalar(x2ln, x2t[j], t_stat[:, 7:8], t_stat[:, 9:10],
                                       ALU.mult, ALU.add),
                       waits=[(s_ve, s_ve.n)], sem=s_ve, inc=1)
            ntp = None
            for q4 in range(8):
                b = 4 + (q4 % 2)
                tws = bw(b) + [(s_ve, nap)]
                for jj in range(4):
                    icb = q4 * 4 + jj
                    ntp = emit("tensor",
                               lambda e, b=b, jj=jj, ic=icb:
                               e.transpose(psb[b][:, jj * 128:(jj + 1) * 128],
                                           x2ln[:, ic * 128:(ic + 1) * 128], t_identb.ap()),
                               waits=(tws if jj == 0 else []), sem=s_te,
                               inc=(1 if jj == 3 else 0))
                nev = emit("vector",
                           lambda e, b=b, q4=q4, j=j:
                           e.tensor_copy(vb(O_X2LT, IC * 512)
                                         .rearrange("p (c n) -> p c n", c=IC)
                                         [:, q4 * 4:(q4 + 1) * 4, j * 128:(j + 1) * 128],
                                         psb[b][:, 0:512]),
                           waits=[(s_te, ntp)], sem=s_ve, inc=1)
                bankg[b] = (s_ve, nev)
                n_x2lt_ready = nev
            x2ln_guard[0] = (s_te, ntp)
        if qb == 0 and "x2dbg" in skip:
            dbgstore("x2lt", vb(O_X2LT, IC * 512), [128, IC * 512], BF16)
        tws = bw(4) + [(s_ve, n_x2lt_ready), (s_ldu, n_consts)]
        ntx = None
        for icb in range(IC):
            ntx = emit("tensor",
                       lambda e, ic=icb:
                       e.matmul(psf[4][0:32, 0:512], t_band32[:, 31 - ic: 63 - ic],
                                x2lt_t(ic), start=(ic == 0), stop=(ic == IC - 1)),
                       waits=(tws if icb == 0 else []), sem=s_te,
                       inc=(1 if icb == IC - 1 else 0))
        nxg_n = emit("vector",
                     lambda e: e.tensor_scalar(nxg, psf[4][0:32, 0:512], -1.0, None,
                                               ALU.mult),
                     waits=[(s_te, ntx)], sem=s_ve, inc=1)
        bankg[4] = (s_ve, nxg_n)
        for j0 in range(0, GJ, 4):
            jn = min(4, GJ - j0)
            zw = []
            if zg_guard[0] is not None:
                zw.append(zg_guard[0])
                zg_guard[0] = None
            nzg = dma("sync", zs3[0], zgt[0:32, 0:jn * 128],
                      s_zpg[:, j0 * 128:(j0 + jn) * 128], waits=zw)
            nzu = dma("sync", zs3[1], zut[0:32, 0:jn * 128],
                      s_zpg[:, INTER + j0 * 128: INTER + (j0 + jn) * 128])
            nmm = [None] * 8
            for icb in range(IC):
                slot = (icb * 2) % 8
                slot2 = (icb * 2 + 1) % 8
                wwt = []
                if wr4_g[slot] is not None:
                    wwt.append(wr4_g[slot])
                    wr4_g[slot] = None
                nldg = dma("sync", ring8[slot], wr4[slot][:, 0:jn * 128],
                           s_w1g[icb * 128:(icb + 1) * 128, j0 * 128:(j0 + jn) * 128],
                           waits=wwt)
                wwt2 = []
                if wr4_g[slot2] is not None:
                    wwt2.append(wr4_g[slot2])
                    wr4_g[slot2] = None
                nldu = dma("sync", ring8[slot2], wr4[slot2][:, 0:jn * 128],
                           s_w1g[icb * 128:(icb + 1) * 128,
                                 INTER + j0 * 128: INTER + (j0 + jn) * 128],
                           waits=wwt2)
                nu_last = None
                for kk in range(2 * jn):
                    g_half = kk < jn
                    k = kk if g_half else kk - jn
                    sl_ = slot if g_half else slot2
                    bk = k if g_half else 4 + k
                    tws = [(ring8[slot], nldg), (ring8[slot2], nldu)] if kk == 0 else []
                    if icb == 0:
                        tws = tws + bw(bk)
                        if kk == 0:
                            tws = tws + [(s_ve, n_x2lt_ready)]
                    nu_last = emit("tensor",
                                   lambda e, bk=bk, k=k, s=sl_, ic=icb:
                                   e.matmul(psf[bk][:, 0:512],
                                            wr4[s][:, k * 128:(k + 1) * 128],
                                            x2lt_t(ic), start=(ic == 0), stop=False),
                                   waits=tws, sem=s_te,
                                   inc=(1 if kk == 2 * jn - 1 else 0))
                wr4_g[slot] = (s_te, nu_last)
                wr4_g[slot2] = (s_te, nu_last)
            for k in range(jn):
                tws = [(zs3[0], nzg), (s_ve, nxg_n)] if k == 0 else []
                nmm[k] = emit("tensor",
                              lambda e, k=k:
                              e.matmul(psf[k][:, 0:512], zgt[0:32, k * 128:(k + 1) * 128],
                                       nxg, start=False, stop=True),
                              waits=tws, sem=s_te, inc=1)
            for k in range(jn):
                tws = [(zs3[1], nzu)] if k == 0 else []
                nmm[4 + k] = emit("tensor",
                                  lambda e, k=k:
                                  e.matmul(psf[4 + k][:, 0:512],
                                           zut[0:32, k * 128:(k + 1) * 128],
                                           nxg, start=False, stop=True),
                                  waits=tws, sem=s_te, inc=1)
            zg_guard[0] = (s_te, nmm[4 + jn - 1])
            for k in range(jn):
                j = j0 + k
                sw = [(s_te, nmm[k])]
                if sgt_g[0] is not None:
                    sw.append(sgt_g[0])
                    sgt_g[0] = None
                nsg = emit("scalar",
                           lambda e, k=k: e.activation(sgt, psf[k][:, 0:512], AF.Silu),
                           waits=sw, sem=s_sc, inc=1)
                bankg[k] = (s_sc, nsg)
                nh = emit("vector",
                          lambda e, k=k, j=j:
                          e.tensor_tensor(hbuf[j], psf[4 + k][:, 0:512], sgt, ALU.mult),
                          waits=[(s_sc, nsg), (s_te, nmm[4 + k])], sem=s_ve, inc=1)
                bankg[4 + k] = (s_ve, nh)
                sgt_g[0] = (s_ve, nh)
        n_h_ready = s_ve.n
        if qb == 0 and "x2dbg" in skip:
            dbgstore("hb", vb(O_HBUF, GJ * 512), [128, GJ * 512], BF16)
            dbgstore("nxg", nxg, [32, 512], BF16)
        tws = bw(4) + [(s_ve, n_h_ready), (s_ldu, n_consts)]
        ntx = None
        for kc in range(GJ):
            ntx = emit("tensor",
                       lambda e, kc=kc:
                       e.matmul(psf[4][0:GJ, 0:512], t_band86[:, 85 - kc: 171 - kc],
                                hbuf[kc], start=(kc == 0), stop=(kc == GJ - 1)),
                       waits=(tws if kc == 0 else []), sem=s_te,
                       inc=(1 if kc == GJ - 1 else 0))
        nxh_n = emit("vector",
                     lambda e: e.tensor_scalar(nxh, psf[4][0:GJ, 0:512], -1.0, None,
                                               ALU.mult),
                     waits=[(s_te, ntx)], sem=s_ve, inc=1)
        bankg[4] = (s_ve, nxh_n)
        for og in range(8):
            zw = []
            if zd_guard[0] is not None:
                zw.append(zd_guard[0])
                zd_guard[0] = None
            nzd = dma("sync", zs3[2], zdt[0:GJ, :], s_zpd[:, og * 512:(og + 1) * 512],
                      waits=zw)
            nmm = [None] * 4
            for kc in range(GJ):
                slot = kc % 8
                wwt = []
                if wr4_g[slot] is not None:
                    wwt.append(wr4_g[slot])
                    wr4_g[slot] = None
                nld = dma("sync", ring8[slot], wr4[slot],
                          s_w1d[kc * 128:(kc + 1) * 128, og * 512:(og + 1) * 512],
                          waits=wwt)
                nd_last = None
                for k in range(4):
                    tws = [(ring8[slot], nld)] if k == 0 else []
                    if kc == 0:
                        tws = tws + bw(k) + [(s_ve, nxh_n)]
                    nd_last = emit("tensor",
                                   lambda e, k=k, s=slot, kc=kc:
                                   e.matmul(psf[k][:, 0:512],
                                            wr4[s][:, k * 128:(k + 1) * 128],
                                            hbuf[kc], start=(kc == 0), stop=False),
                                   waits=tws, sem=s_te, inc=(1 if k == 3 else 0))
                wr4_g[slot] = (s_te, nd_last)
            for k in range(4):
                tws = [(zs3[2], nzd)] if k == 0 else []
                nmm[k] = emit("tensor",
                              lambda e, k=k:
                              e.matmul(psf[k][:, 0:512], zdt[0:GJ, k * 128:(k + 1) * 128],
                                       nxh, start=False, stop=True),
                              waits=tws, sem=s_te, inc=1)
            zd_guard[0] = (s_te, nmm[3])
            ntp_fin = None
            for k in range(4):
                dslot = k % 2
                dwt = [(s_te, nmm[k])]
                if dstg_g[dslot] is not None:
                    dwt.append(dstg_g[dslot])
                    dstg_g[dslot] = None
                nevd = emit("vector",
                            lambda e, k=k, d=dslot: e.tensor_copy(dstg[d], psf[k][:, 0:512]),
                            waits=dwt, sem=s_ve, inc=1)
                bankg[k] = (s_ve, nevd)
                ntp = None
                for j in range(4):
                    b = 4 + j
                    tws = [(s_ve, nevd)]
                    if k == 0:
                        tws = tws + bw(b)
                    ntp = emit("tensor",
                               lambda e, b=b, j=j, k=k, d=dslot:
                               e.transpose(psf[b][:, k * 128:(k + 1) * 128],
                                           dstg[d][:, j * 128:(j + 1) * 128],
                                           t_identf.ap()),
                               waits=tws, sem=s_te, inc=(1 if j == 3 else 0))
                dstg_g[dslot] = (s_te, ntp)
                ntp_fin = ntp
            for j in range(4):
                b = 4 + j
                oslot = j % 2
                owt = [(s_te, ntp_fin)]
                if ostg_g[oslot] is not None:
                    owt.append(ostg_g[oslot])
                    ostg_g[oslot] = None
                nout = emit("vector",
                            lambda e, b=b, j=j, og=og, o=oslot:
                            e.tensor_tensor(ostg[o], psf[b][:, 0:512],
                                            x2t[j][:, og * 512:(og + 1) * 512], ALU.add),
                            waits=owt, sem=s_ve, inc=1)
                bankg[b] = (s_ve, nout)
                ng = dma("gpsimd", gpo2[oslot],
                         d_out[qb * 512 + j * 128: qb * 512 + (j + 1) * 128,
                               og * 512:(og + 1) * 512],
                         ostg[oslot], waits=[(s_ve, nout)])
                ostg_g[oslot] = (gpo2[oslot], ng)
                n_out_store[0] = ng

    g0, g1 = gpo2[0].n, gpo2[1].n
    plan["gpsimd"].append(lambda e, a=g0, b=g1:
                          (e.wait_ge(gpo2[0].h, a), e.wait_ge(gpo2[1].h, b)))
    if dumps:
        return _finish(nc, plan, es, fence, dma, emit, s_ld, s_gp, dumps, locals())

    with nc.Block() as block:
        @block.sync
        def _(e):
            for f in plan["sync"]:
                f(e)

        @block.scalar
        def _(e):
            for f in plan["scalar"]:
                f(e)

        @block.vector
        def _(e):
            for f in plan["vector"]:
                f(e)

        @block.tensor
        def _(e):
            for f in plan["tensor"]:
                f(e)

        @block.gpsimd
        def _(e):
            for f in plan["gpsimd"]:
                f(e)

    es.close()
    return nc


def _band(w, col):
    a = np.zeros((128, w), dtype=np.float32)
    a[:, col] = 1.0
    return a.astype(ml_dtypes.bfloat16)


_NC_CACHE = [None]


def kernel(**inputs):
    hs = np.asarray(inputs["L0_attn_hidden_states"])
    ins = {
        "qw": np.ascontiguousarray(inputs["L0_attn_q_weight"]),
        "kw": np.ascontiguousarray(inputs["L0_attn_k_weight"]),
        "vw": np.ascontiguousarray(inputs["L0_attn_v_weight"]),
        "ow": np.ascontiguousarray(inputs["L0_attn_o_weight"]),
        "qs": np.ascontiguousarray(inputs["L0_attn_q_scale"]),
        "ks": np.ascontiguousarray(inputs["L0_attn_k_scale"]),
        "vs": np.ascontiguousarray(inputs["L0_attn_v_scale"]),
        "os_": np.ascontiguousarray(inputs["L0_attn_o_scale"]),
        "gqw": np.ascontiguousarray(inputs["L0_ffn_qweight_gu"]),
        "gqz": np.ascontiguousarray(inputs["L0_ffn_qzeros_gu"]),
        "gsc": np.ascontiguousarray(inputs["L0_ffn_scales_gu"]),
        "dqw": np.ascontiguousarray(inputs["L0_ffn_qweight_down"]),
        "dqz": np.ascontiguousarray(inputs["L0_ffn_qzeros_down"]),
        "dsc": np.ascontiguousarray(inputs["L0_ffn_scales_down"]),
        "identf": np.eye(128, dtype=np.float32),
        "band32": _band(63, 31),
        "band86": _band(171, 85),
        "shiftv": ((np.arange(128, dtype=np.uint32) % 8) * 4).reshape(128, 1),
    }
    in_maps = []
    for c in range(8):
        b, h = c // 2, c % 2
        m = dict(ins)
        m["x_own"] = np.ascontiguousarray(hs[b, h * TOWN:(h + 1) * TOWN])
        m["x_oth"] = np.ascontiguousarray(hs[b, (1 - h) * TOWN:(2 - h) * TOWN])
        in_maps.append(m)
    if _NC_CACHE[0] is None:
        _NC_CACHE[0] = build_nc()
    res = run_bass_kernel_spmd(_NC_CACHE[0], in_maps, core_ids=list(range(8)))
    out = np.empty((4, SEQ, DM), dtype=np.float32)
    for c in range(8):
        b, h = c // 2, c % 2
        out[b, h * TOWN:(h + 1) * TOWN] = res.results[c]["out"]
    return out


# version 20 (final)
# speedup vs baseline: 1.0405x; 1.0176x over previous
import contextlib
import numpy as np
import ml_dtypes
import concourse.bass as bass
import concourse.mybir as mybir
from concourse.bass_utils import run_bass_kernel_spmd

F32 = mybir.dt.float32
BF16 = mybir.dt.bfloat16
U32 = mybir.dt.uint32
AF = mybir.ActivationFunctionType
ALU = mybir.AluOpType
AX = mybir.AxisListType

DM = 4096
KVD = 1024
NKVH = 8
INTER = 11008
GUC = 2 * INTER
SEQ = 4096
TOWN = 2048
TB = 512
NQB = TOWN // TB
NSB = SEQ // TB
IC = DM // 128
GJ = INTER // 128
RT128 = 1.0 / float(np.sqrt(128.0))
EPS = 1e-5

ENGS = ("sync", "scalar", "vector", "tensor", "gpsimd")


DUMPCAT = {
    "binq": ("binq_d", lambda L: L["s_binq"][0:256, 0:512], [256, 512], "bf"),
    "bino": ("bino_d", lambda L: L["s_bino"][0:256, 0:512], [256, 512], "bf"),
    "w1g": ("w1g_d", lambda L: L["s_w1g"][0:256, 0:1024], [256, 1024], "bf"),
    "w1d": ("w1d_d", lambda L: L["s_w1d"][0:256, 0:1024], [256, 1024], "bf"),
    "zpg": ("zpg_d", lambda L: L["s_zpg"][:, :], [32, GUC], "bf"),
    "zpd": ("zpd_d", lambda L: L["s_zpd"][:, :], [GJ, DM], "bf"),
    "kscr": ("kscr_d", lambda L: L["s_kscr"][:, 0:512], [KVD, 512], "bf"),
    "kscr2": ("kscr2_d", lambda L: L["s_kscr"][:, 2048:2560], [KVD, 512], "bf"),
    "vscr": ("vscr_d", lambda L: L["s_vscr"][0, 0:512, :], [512, 128], "bf"),
    "qscr": ("qscr_d", lambda L: L["s_qscr"][0:DM, :], [DM, 512], "bf"),
    "ascr": ("ascr_d", lambda L: L["s_ascr"][0:DM, :], [DM, 512], "bf"),
}


def _finish(nc, plan, es, fence, dma, emit, s_ld, s_gp, dumps, L):
    fence()
    nc._dump_handles = {}
    for name in dumps:
        (dname, get, shape, dt) = DUMPCAT[name]
        dt = BF16 if dt == "bf" else F32
        dtn = nc.dram_tensor(dname, shape, dt, kind="ExternalOutput")
        nc._dump_handles[dname] = dtn
        dma("sync", s_ld, dtn[:, :], get(L))
    plan["gpsimd"].append(lambda e, v=s_ld.n, w=s_gp.n:
                          (e.wait_ge(s_ld.h, v), e.wait_ge(s_gp.h, w)))
    with nc.Block() as block:
        @block.sync
        def _(e):
            for f in plan["sync"]:
                f(e)

        @block.scalar
        def _(e):
            for f in plan["scalar"]:
                f(e)

        @block.vector
        def _(e):
            for f in plan["vector"]:
                f(e)

        @block.tensor
        def _(e):
            for f in plan["tensor"]:
                f(e)

        @block.gpsimd
        def _(e):
            for f in plan["gpsimd"]:
                f(e)
    es.close()
    return nc


class Sem:
    def __init__(self, es, nc, name):
        self.h = es.enter_context(nc.semaphore(name))
        self.n = 0

    def inc(self, k=1):
        self.n += k
        return self.n


def build_nc(upto="p4b", dumps=(), skip=()):
    PH = {"p0": 0, "p1": 1, "p4a": 2, "p4b": 3}[upto]
    skipset = [set(skip)]
    nc = bass.Bass("TRN2", target_bir_lowering=False)
    es = contextlib.ExitStack()
    plan = {e: [] for e in ENGS}

    d_xown = nc.dram_tensor("x_own", [TOWN, DM], F32, kind="ExternalInput")
    d_xoth = nc.dram_tensor("x_oth", [TOWN, DM], F32, kind="ExternalInput")
    d_qw = nc.dram_tensor("qw", [DM, DM], F32, kind="ExternalInput")
    d_kw = nc.dram_tensor("kw", [KVD, DM], F32, kind="ExternalInput")
    d_vw = nc.dram_tensor("vw", [KVD, DM], F32, kind="ExternalInput")
    d_ow = nc.dram_tensor("ow", [DM, DM], F32, kind="ExternalInput")
    d_qs = nc.dram_tensor("qs", [DM], F32, kind="ExternalInput")
    d_ks = nc.dram_tensor("ks", [KVD], F32, kind="ExternalInput")
    d_vs = nc.dram_tensor("vs", [KVD], F32, kind="ExternalInput")
    d_os = nc.dram_tensor("os_", [DM], F32, kind="ExternalInput")
    d_gqw = nc.dram_tensor("gqw", [512, GUC], U32, kind="ExternalInput")
    d_gqz = nc.dram_tensor("gqz", [32, GUC // 8], U32, kind="ExternalInput")
    d_gsc = nc.dram_tensor("gsc", [32, GUC], F32, kind="ExternalInput")
    d_dqw = nc.dram_tensor("dqw", [INTER // 8, DM], U32, kind="ExternalInput")
    d_dqz = nc.dram_tensor("dqz", [GJ, DM // 8], U32, kind="ExternalInput")
    d_dsc = nc.dram_tensor("dsc", [GJ, DM], F32, kind="ExternalInput")
    d_identf = nc.dram_tensor("identf", [128, 128], F32, kind="ExternalInput")
    d_band32 = nc.dram_tensor("band32", [128, 63], BF16, kind="ExternalInput")
    d_band86 = nc.dram_tensor("band86", [128, 171], BF16, kind="ExternalInput")
    d_shiftv = nc.dram_tensor("shiftv", [128, 1], U32, kind="ExternalInput")

    d_out = nc.dram_tensor("out", [TOWN, DM], F32, kind="ExternalOutput")

    s_binq = nc.dram_tensor("binq", [DM, DM], BF16)
    s_bink = nc.dram_tensor("bink", [DM, KVD], BF16)
    s_binv = nc.dram_tensor("binv", [DM, KVD], BF16)
    s_bino = nc.dram_tensor("bino", [DM, DM], BF16)
    s_w1g = nc.dram_tensor("w1g", [DM, GUC], BF16)
    s_w1d = nc.dram_tensor("w1d", [INTER, DM], BF16)
    d_ghl = nc.dram_tensor("ghl", [2, 32 * GUC], BF16)
    d_dhl = nc.dram_tensor("dhl", [2, 86 * DM], BF16)
    s_zpg = nc.dram_tensor("zpg", [32, GUC], BF16)
    s_ghi = nc.dram_tensor("ghi", [32, GUC], BF16)
    s_glo = nc.dram_tensor("glo", [32, GUC], BF16)
    s_dhi = nc.dram_tensor("dhi", [GJ, DM], BF16)
    s_dlo = nc.dram_tensor("dlo", [GJ, DM], BF16)
    s_zpd = nc.dram_tensor("zpd", [GJ, DM], BF16)
    s_kscr = nc.dram_tensor("kscr", [KVD, SEQ], BF16)
    s_vscr = nc.dram_tensor("vscr", [NKVH, SEQ, 128], BF16)
    s_qscr = nc.dram_tensor("qscr", [NQB * DM, TB], BF16)
    s_ascr = nc.dram_tensor("ascr", [NQB * DM, TB], BF16)

    s_ld = Sem(es, nc, "s_ld")
    s_ld4a = Sem(es, nc, "s_ld4a")
    s_ld4b = [Sem(es, nc, f"s_ld4b{i}") for i in range(NQB)]
    s_scst = Sem(es, nc, "s_scst")
    s_gp = Sem(es, nc, "s_gp")
    s_dqld = Sem(es, nc, "s_dqld")
    s_dqld2 = Sem(es, nc, "s_dqld2")
    s_dqst = Sem(es, nc, "s_dqst")
    s_dqst2 = Sem(es, nc, "s_dqst2")
    s_ldz = Sem(es, nc, "s_ldz")
    s_ldu = Sem(es, nc, "s_ldu")
    ring8 = [Sem(es, nc, f"s_r{i}") for i in range(8)]
    xb2 = [Sem(es, nc, f"s_xb{i}") for i in range(2)]
    wb4 = [Sem(es, nc, f"s_wb{i}") for i in range(4)]
    zs3 = [Sem(es, nc, f"s_z{i}") for i in range(3)]
    stk4 = [Sem(es, nc, f"s_stk{i}") for i in range(4)]
    stv2 = [Sem(es, nc, f"s_stv{i}") for i in range(2)]
    stq1 = Sem(es, nc, "s_stq")
    gpc4 = [Sem(es, nc, f"s_gpc{i}") for i in range(4)]
    gpa4 = [Sem(es, nc, f"s_gpa{i}") for i in range(4)]
    gpo2 = [Sem(es, nc, f"s_gpo{i}") for i in range(2)]
    s_te = Sem(es, nc, "s_te")
    s_ve = Sem(es, nc, "s_ve")
    s_sc = Sem(es, nc, "s_sc")
    ALL_SEMS = ([s_ld, s_ld4a, s_scst, s_gp, s_dqld, s_dqst, s_ldz, s_ldu, s_te, s_ve, s_sc]
                + s_ld4b + ring8 + xb2 + wb4 + zs3 + stk4 + stv2 + [stq1] + gpc4 + gpa4 + gpo2
                + [s_dqld2, s_dqst2])

    t_identf = es.enter_context(nc.sbuf_tensor("c_identf", [128, 128], F32))
    t_identb = es.enter_context(nc.sbuf_tensor("c_identb", [128, 128], BF16))
    t_onesr_f = es.enter_context(nc.sbuf_tensor("c_onesr_f", [1, 128], F32))
    t_onesc_b = es.enter_context(nc.sbuf_tensor("c_onesc_b", [128, 1], BF16))
    t_onesm = es.enter_context(nc.sbuf_tensor("c_onesm", [128, 128], BF16))
    t_onesr_b = es.enter_context(nc.sbuf_tensor("c_onesr_b", [1, 128], BF16))
    t_band32 = es.enter_context(nc.sbuf_tensor("c_band32", [128, 63], BF16))
    t_band86 = es.enter_context(nc.sbuf_tensor("c_band86", [128, 171], BF16))
    t_shiftv = es.enter_context(nc.sbuf_tensor("c_shiftv", [128, 1], U32))
    t_svec = es.enter_context(nc.sbuf_tensor("c_svec", [128, 80], F32))
    t_qsin = es.enter_context(nc.sbuf_tensor("c_qsin", [128, 80], F32))
    t_ebias = es.enter_context(nc.sbuf_tensor("c_ebias", [128, 4], F32))
    t_mk2 = es.enter_context(nc.sbuf_tensor("c_mk2", [1, 8], F32))
    t_mq2 = es.enter_context(nc.sbuf_tensor("c_mq2", [1, 4], F32))
    t_crow = es.enter_context(nc.sbuf_tensor("c_crow", [1, 4], F32))
    t_stat = es.enter_context(nc.sbuf_tensor("c_stat", [128, 24], F32))
    t_rd = es.enter_context(nc.sbuf_tensor("c_rd", [1, 512], F32))
    t_rd2 = es.enter_context(nc.sbuf_tensor("c_rd2", [1, 512], F32))
    t_eps = es.enter_context(nc.sbuf_tensor("c_eps", [128, 1], F32))

    AR = 190 * 1024
    t_ar = es.enter_context(nc.sbuf_tensor("arena", [128, AR], mybir.dt.uint8))
    h_f32 = t_ar.bitcast(F32)
    h_bf = t_ar.bitcast(BF16)
    h_u32 = t_ar.bitcast(U32)

    def vf(off, cols, rows=128):
        assert off % 4 == 0 and off + cols * 4 <= AR, (off, cols)
        return h_f32[0:rows, off // 4: off // 4 + cols]

    def vb(off, cols, rows=128):
        assert off % 2 == 0 and off + cols * 2 <= AR, (off, cols)
        return h_bf[0:rows, off // 2: off // 2 + cols]

    def vu(off, cols, rows=128):
        assert off % 4 == 0 and off + cols * 4 <= AR, (off, cols)
        return h_u32[0:rows, off // 4: off // 4 + cols]

    psh = [es.enter_context(nc.psum_tensor(f"ps{i}", [128, 512], F32)) for i in range(8)]
    psf = [h.ap() for h in psh]
    psb = [h.bitcast(BF16).ap() for h in psh]
    bankg = [None] * 8

    def emit(eng, fn, waits=(), sem=None, inc=0):
        n = sem.inc(inc) if (sem is not None and inc) else None
        wt = tuple((s, v) for (s, v) in waits if v)

        def f(e, fn=fn, wt=wt, sem=sem, inc=inc):
            for (sh, v) in wt:
                e.wait_ge(sh.h, v)
            ins = fn(e)
            if sem is not None and inc:
                ins.then_inc(sem.h, inc)
        plan[eng].append(f)
        return n

    def dma(eng, sem, out_ap, in_ap, waits=(), slow=False):
        return emit(eng, lambda e, o=out_ap, i=in_ap, sl=slow:
                    e.dma_start(out=o, in_=i, allow_slow_non_contiguous=sl),
                    waits=waits, sem=sem, inc=16)

    def bw(b):
        gclaim = bankg[b]
        bankg[b] = None
        return [gclaim] if gclaim is not None else []

    def ser(semq):
        if "serload" in skipset[0]:
            return [(semq, semq.n)]
        return []

    def fence():
        tg = tuple((s, s.n) for s in ALL_SEMS if s.n > 0)
        for eng in ENGS:
            def f(e, tg=tg):
                for (sh, v) in tg:
                    e.wait_ge(sh.h, v)
            plan[eng].append(f)
        for i in range(8):
            bankg[i] = None

    dbg_stores = []

    def dbgstore(name, ap, shape, dtype):
        t = nc.dram_tensor("dbg_" + name, shape, dtype, kind="ExternalOutput")
        n = dma("scalar", s_scst, t[:, :], ap,
                waits=[(s_ve, s_ve.n), (s_sc, s_sc.n), (s_te, s_te.n)])
        dbg_stores.append(n)

    dma("sync", s_ldu, t_identf.ap(), d_identf[:, :])
    dma("sync", s_ldu, t_band32.ap(), d_band32[:, :])
    dma("sync", s_ldu, t_band86.ap(), d_band86[:, :])
    dma("sync", s_ldu, t_shiftv.ap(), d_shiftv[:, :])
    dma("sync", s_ldu, t_qsin[:, 0:32], d_qs.ap().rearrange("(c p) -> p c", p=128), slow=True)
    dma("sync", s_ldu, t_qsin[:, 32:40], d_ks.ap().rearrange("(c p) -> p c", p=128), slow=True)
    dma("sync", s_ldu, t_qsin[:, 40:48], d_vs.ap().rearrange("(c p) -> p c", p=128), slow=True)
    n_consts = dma("sync", s_ldu, t_qsin[:, 48:80], d_os.ap().rearrange("(c p) -> p c", p=128), slow=True)
    emit("vector", lambda e: e.tensor_copy(t_identb.ap(), t_identf.ap()),
         waits=[(s_ldu, n_consts)], sem=s_ve, inc=1)
    emit("vector", lambda e: e.memset(t_onesr_f.ap(), 1.0), sem=s_ve, inc=1)
    emit("vector", lambda e: e.memset(t_onesc_b.ap(), 1.0), sem=s_ve, inc=1)
    emit("vector", lambda e: e.memset(t_onesm.ap(), 1.0), sem=s_ve, inc=1)
    emit("vector", lambda e: e.memset(t_onesr_b.ap(), 1.0), sem=s_ve, inc=1)
    n_base_ve = emit("vector", lambda e: e.memset(t_eps.ap(), EPS), sem=s_ve, inc=1)

    DQ0 = 160 * 1024
    O_DQZ = 131072
    O_DQU = 132480
    O_DQR = 143488
    O_DQW = 154496
    ZON = [133120, 133120 + 13760]
    dqc = {"ve": 0, "scst": 0}
    dqp = {"g": 0, "guard": [None, None], "mmg": [None, None]}
    dq_mode = ["pe"]
    dqz = [{"ve": None, "st": None}, {"ve": None, "st": None}]
    dq_sems = [(s_dqld, s_dqst), (s_dqld2, s_dqst2)]
    dq_zi = [0]

    dequant_tasks = []

    def zp_task(osl, gu):
        def gen():
            if gu:
                pw, width = 344, 2752
                src_qz = d_gqz.reshape([128, 688])[:, osl * pw:(osl + 1) * pw]
                src_sc = d_gsc.reshape([128, 5504])[:, osl * width:(osl + 1) * width]
                dst = s_zpg.reshape([128, 5504])[:, osl * width:(osl + 1) * width]
            else:
                pw, width = 344, 2752
                src_qz = d_dqz.reshape([128, 344])[:, :]
                src_sc = d_dsc.reshape([128, 2752])[:, :]
                dst = s_zpd.reshape([128, 2752])[:, :]
            qz = vu(O_DQZ, pw)
            znib = vu(O_DQU, width)
            znib3 = znib.rearrange("p (c k) -> p c k", k=8)
            scv = vf(O_DQR, width)
            outb = vb(O_DQW, width)
            wz = [(s_ve, dqc["ve"]), (s_dqst, dqc["scst"])]
            n1 = dma("gpsimd", s_dqld, qz, src_qz, waits=wz)
            n2 = dma("gpsimd", s_dqld, scv, src_sc)
            for k in range(8):
                emit("vector",
                     lambda e, o=znib3[:, :, k:k + 1], i=qz.unsqueeze(2), sh=4 * k:
                     e.tensor_scalar(o, i, float(sh), 15.0,
                                     ALU.logical_shift_right, ALU.bitwise_and),
                     waits=[(s_dqld, n2), (s_ve, s_ve.n)], sem=s_ve, inc=1)
            nv = emit("vector",
                      lambda e, o=outb, i=znib, s=scv:
                      e.scalar_tensor_tensor(o, i, -7.0, s, ALU.add, ALU.mult),
                      waits=[(s_dqld, n2), (s_ve, s_ve.n), (s_dqst, dqc["scst"])],
                      sem=s_ve, inc=1)
            ns = dma("scalar", s_dqst, dst, outb, waits=[(s_ve, nv)])
            dqc["ve"] = nv
            dqc["scst"] = ns
            for z in range(2):
                dqz[z]["ve"] = (s_ve, nv)
                dqz[z]["st"] = (s_dqst, ns)
        return gen

    def hl_task(h, gu):
        def gen():
            if gu:
                src = d_gsc.reshape([128, 5504])[:, h * 2752:(h + 1) * 2752]
                dsth = d_ghl.reshape([2, 128, 5504])[0, :, h * 2752:(h + 1) * 2752]
                dstl = d_ghl.reshape([2, 128, 5504])[1, :, h * 2752:(h + 1) * 2752]
            else:
                src = d_dsc.reshape([128, 2752])[:, :]
                dsth = d_dhl.reshape([2, 128, 2752])[0, :, :]
                dstl = d_dhl.reshape([2, 128, 2752])[1, :, :]
            scf = vf(O_DQU, 2752)
            hib = vb(O_DQR, 2752)
            lob = vb(O_DQW, 2752)
            wz = [(s_ve, dqc["ve"]), (s_dqst, dqc["scst"])]
            n1 = dma("gpsimd", s_dqld, scf, src, waits=wz)
            nh = emit("vector", lambda e: e.tensor_copy(hib, scf),
                      waits=[(s_dqld, n1), (s_ve, s_ve.n), (s_dqst, dqc["scst"])],
                      sem=s_ve, inc=1)
            nl = emit("vector",
                      lambda e: e.tensor_tensor(lob, scf, hib, ALU.subtract),
                      waits=[(s_ve, s_ve.n)], sem=s_ve, inc=1)
            dma("scalar", s_dqst, dsth, hib, waits=[(s_ve, nh)])
            ns2 = dma("scalar", s_dqst, dstl, lob, waits=[(s_ve, nl)])
            dqc["ve"] = nl
            dqc["scst"] = ns2
            for z in range(2):
                dqz[z]["ve"] = (s_ve, nl)
                dqz[z]["st"] = (s_dqst, ns2)
        return gen

    def w1_task(it, osl, gu):
        def gen():
            if gu:
                width = 1376
                src_w = d_gqw[it * 16:(it + 1) * 16, osl * width:(osl + 1) * width]
                src_sc = d_gsc[it:it + 1, osl * width:(osl + 1) * width]
                dst = s_w1g[it * 128:(it + 1) * 128, osl * width:(osl + 1) * width]
                hs = it * GUC + osl * width
                hl_src = d_ghl[:, hs:hs + width]
            else:
                width = 1024
                src_w = d_dqw[it * 16:(it + 1) * 16, osl * width:(osl + 1) * width]
                src_sc = d_dsc[it:it + 1, osl * width:(osl + 1) * width]
                dst = s_w1d[it * 128:(it + 1) * 128, osl * width:(osl + 1) * width]
                hs = it * DM + osl * width
                hl_src = d_dhl[:, hs:hs + width]
            z = dq_zi[0] % 2
            dq_zi[0] += 1
            sld, sst = dq_sems[z]
            qbuf = vu(ZON[z], width)
            w1 = vb(ZON[z] + 5504, width)
            hib = vb(ZON[z] + 8256, width, 1)
            lob = vb(ZON[z] + 11008, width, 1)
            wz = [w for w in (dqz[z]["ve"], dqz[z]["st"]) if w is not None]
            prev_store = dqz[z]["st"]
            dma("gpsimd", sld, qbuf,
                src_w.unsqueeze(1).broadcast_to([16, 8, width]), waits=wz)
            dma("gpsimd", sld, hib, hl_src[0:1, :])
            n2 = dma("gpsimd", sld, lob, hl_src[1:2, :])
            emit("vector",
                 lambda e, o=qbuf, i=qbuf:
                 e.tensor_scalar(o, i, t_shiftv.ap(), 15.0,
                                 ALU.logical_shift_right, ALU.bitwise_and),
                 waits=[(sld, n2), (s_ldu, n_consts)], sem=s_ve, inc=1)
            last_ve = None
            c0 = 0
            ch = 0
            while c0 < width:
                cw = min(512, width - c0)
                sl = dqp["g"] % 2
                b = 6 + sl
                dqp["g"] += 1
                gg = dqp["guard"][sl]
                mmw = [(sld, n2), (s_ve, n_base_ve)] + ([gg] if gg else [])
                emit("tensor",
                     lambda e, b=b, c0=c0, cw=cw:
                     e.matmul(psf[b][:, 0:cw], t_onesr_b.ap(),
                              hib[0:1, c0:c0 + cw], start=True, stop=False),
                     waits=mmw, sem=s_te, inc=0)
                n_mm = emit("tensor",
                            lambda e, b=b, c0=c0, cw=cw:
                            e.matmul(psf[b][:, 0:cw], t_onesr_b.ap(),
                                     lob[0:1, c0:c0 + cw], start=False, stop=True),
                            sem=s_te, inc=1)
                stw = [(s_te, n_mm), (s_ve, s_ve.n)]
                if ch == 0 and prev_store:
                    stw.append(prev_store)
                last_ve = emit("vector",
                               lambda e, c0=c0, cw=cw, b=b:
                               e.scalar_tensor_tensor(w1[:, c0:c0 + cw],
                                                      qbuf[:, c0:c0 + cw], -8.0,
                                                      psf[b][:, 0:cw],
                                                      ALU.add, ALU.mult),
                               waits=stw, sem=s_ve, inc=1)
                dqp["guard"][sl] = (s_ve, last_ve)
                c0 += cw
                ch += 1
            ns = dma("scalar", sst, dst, w1, waits=[(s_ve, last_ve)])
            dqz[z]["ve"] = (s_ve, last_ve)
            dqz[z]["st"] = (sst, ns)
        return gen

    for i in range(2):
        dequant_tasks.append(zp_task(i, True))
    dequant_tasks.append(zp_task(0, False))
    dequant_tasks.append(hl_task(0, True))
    dequant_tasks.append(hl_task(1, True))
    dequant_tasks.append(hl_task(0, False))
    n_gu = 16
    n_dn = 4
    for it in range(IC):
        for osl in range(n_gu):
            dequant_tasks.append(w1_task(it, osl, True))
    for it in range(GJ):
        for osl in range(n_dn):
            dequant_tasks.append(w1_task(it, osl, False))
    if "dq" in skip:
        dequant_tasks.clear()
    for tok in skip:
        if tok.startswith("dqn"):
            del dequant_tasks[int(tok[3:]):]
    dq_idx = [0]

    def drain_dq(k):
        while k > 0 and dq_idx[0] < len(dequant_tasks):
            dequant_tasks[dq_idx[0]]()
            dq_idx[0] += 1
            k -= 1

    O_WBIN = 0
    O_SGN = 65536
    O_BSTG = 98304
    stg = vb(O_BSTG, IC * 512)
    stg3 = stg.rearrange("p (c n) -> p c n", c=IC)
    wspec = [(d_qw, s_binq, 32, 0, True),
             (d_kw, s_bink, 8, 32, False),
             (d_vw, s_binv, 8, 40, False),
             (d_ow, s_bino, 32, 48, False)]
    stg_guard = [None]
    sgn_guard = [None]
    n_bin_stores = 0
    if "bin" in skip:
        wspec = []
    for (dsrc, ddst, noc, svbase, isq) in wspec:
        for ocg in range(noc // 4):
            lw = []
            for k in range(4):
                oc = ocg * 4 + k
                wv = vf(O_WBIN + k * 16384, DM)
                lw.append((wb4[k], dma(["sync", "gpsimd"][k % 2], wb4[k], wv,
                                       dsrc[oc * 128:(oc + 1) * 128, :],
                                       waits=[(s_sc, s_sc.n), (s_ldu, n_consts)])))
            nsg = []
            for k in range(4):
                oc = ocg * 4 + k
                wv = vf(O_WBIN + k * 16384, DM)
                sg = vb(O_SGN + k * 8192, DM)
                sw = [lw[k]]
                if k == 0 and sgn_guard[0] is not None:
                    sw.append(sgn_guard[0])
                    sgn_guard[0] = None
                nsg.append(emit("scalar", lambda e, o=sg, i=wv: e.activation(o, i, AF.Sign),
                                waits=sw, sem=s_sc, inc=1))
                emit("vector",
                     lambda e, i=wv: e.reduce_sum(t_stat[:, 8:9], i, AX.X,
                                                  apply_absolute_value=True),
                     waits=[lw[k], (s_ve, s_ve.n)], sem=s_ve, inc=1)
                emit("vector",
                     lambda e: e.tensor_scalar(t_stat[:, 9:10], t_stat[:, 8:9], 1.0 / DM,
                                               1e-8, ALU.mult, ALU.max),
                     waits=[(s_ve, s_ve.n)], sem=s_ve, inc=1)
                emit("vector",
                     lambda e, col=svbase + oc, m=(RT128 if isq else 1.0):
                     e.tensor_scalar(t_svec[:, col:col + 1], t_stat[:, 9:10],
                                     t_qsin[:, col:col + 1], m, ALU.mult, ALU.mult),
                     waits=[(s_ve, s_ve.n)], sem=s_ve, inc=1)
            nev = None
            for icb in range(IC):
                if icb % 3 == 0:
                    drain_dq(0 if "nodq0" in skip else 1)
                b = 4 + (icb % 2)
                tws = bw(b) + [(s_sc, nsg[3]), (s_ve, n_base_ve)]
                if icb == 0 and stg_guard[0] is not None:
                    tws.append(stg_guard[0])
                    stg_guard[0] = None
                ntp = None
                for k in range(4):
                    sg = vb(O_SGN + k * 8192, DM)
                    ntp = emit("tensor",
                               lambda e, b=b, k=k, i=sg, ic=icb:
                               e.transpose(psb[b][:, k * 128:(k + 1) * 128],
                                           i[:, ic * 128:(ic + 1) * 128], t_identb.ap()),
                               waits=(tws if k == 0 else []), sem=s_te,
                               inc=(1 if k == 3 else 0))
                nev = emit("vector",
                           lambda e, b=b, ic=icb:
                           e.tensor_copy(stg[:, ic * 512:(ic + 1) * 512], psb[b][:, 0:512]),
                           waits=[(s_te, ntp)], sem=s_ve, inc=1)
                bankg[b] = (s_ve, nev)
            sgn_guard[0] = (s_te, s_te.n)
            dst = ddst[:, ocg * 512:(ocg + 1) * 512].rearrange("(c p) n -> p c n", p=128)
            n_bin_stores = dma("scalar", s_scst, dst, stg3, waits=[(s_ve, nev)])
            stg_guard[0] = (s_scst, n_bin_stores)

    if PH == 0:
        drain_dq(10 ** 9)
        return _finish(nc, plan, es, fence, dma, emit, s_ld, s_gp, dumps, locals())
    fence()
    O_XLNT = 0
    O_XBUF = 32768
    O_XLNB = 65536
    O_XSQ = 116736
    O_WRNG = 90112
    O_KSTG = 106496
    O_VSTG = 110592
    O_QSTG = 112640

    XLNT_O = [O_XLNT, 161792]
    def xlnt_t(ic, h):
        return vb(XLNT_O[h] + ic * 1024, 512)
    def xlnt3h(h):
        return vb(XLNT_O[h], IC * 512).rearrange("p (c n) -> p c n", c=IC)
    xlnt_guard = [None, None]
    wring = [vb(O_WRNG + i * 1024, 512) for i in range(8)]
    wring_g = [None] * 8
    wring_i = [0]
    kstg = [vb(O_KSTG + i * 1024, 512) for i in range(4)]
    kstg_g = [None] * 4
    vstg = [vb(O_VSTG + i * 1024, 512) for i in range(2)]
    vstg_g = [None] * 2
    qstg = [vb(O_QSTG + i * 1024, 512) for i in range(4)]
    qstg_g = [None] * 4
    xbuf_g = [None, None]
    xlnb_g = [None, None]
    n_q_store = [0]

    XLNB_O = [O_XLNB, O_XLNB + 8192, 81920, 98304]
    xlnb_g4 = [None] * 4

    def emit_ln_ve(sb):
        own = sb < NQB
        xsrc = d_xown if own else d_xoth
        r0 = (sb % NQB) * TB
        naps = []
        for tt in range(4):
            xv = vf(O_XBUF + (tt % 2) * 16384, DM)
            xlv = vb(XLNB_O[tt], DM)
            xq = vf(O_XSQ, DM)
            wts = [(s_ldu, n_consts)]
            if xbuf_g[tt % 2] is not None:
                wts.append(xbuf_g[tt % 2])
                xbuf_g[tt % 2] = None
            nldx = dma("sync", xb2[tt % 2], xv,
                       xsrc[r0 + tt * 128: r0 + (tt + 1) * 128, :], waits=wts)
            emit("scalar", lambda e, i=xv, o=xq:
                 e.activation(o, i, AF.Copy, accum_out=t_stat[:, 0:1]),
                 waits=[(xb2[tt % 2], nldx), (s_sc, s_sc.n)], sem=s_sc, inc=1)
            n_sqs = emit("scalar", lambda e, i=xv, o=xq:
                         e.activation(o, i, AF.Square, accum_out=t_stat[:, 1:2]),
                         waits=[(s_sc, s_sc.n)], sem=s_sc, inc=1)
            emit("vector", lambda e: e.tensor_scalar(t_stat[:, 2:3], t_stat[:, 0:1],
                                                     1.0 / DM, None, ALU.mult),
                 waits=[(s_sc, n_sqs), (s_ve, s_ve.n)], sem=s_ve, inc=1)
            emit("vector", lambda e: e.tensor_scalar(t_stat[:, 3:4], t_stat[:, 1:2],
                                                     1.0 / DM, None, ALU.mult),
                 waits=[(s_ve, s_ve.n)], sem=s_ve, inc=1)
            emit("vector", lambda e: e.tensor_tensor(t_stat[:, 4:5], t_stat[:, 2:3],
                                                     t_stat[:, 2:3], ALU.mult),
                 waits=[(s_ve, s_ve.n)], sem=s_ve, inc=1)
            emit("vector", lambda e: e.tensor_tensor(t_stat[:, 5:6], t_stat[:, 3:4],
                                                     t_stat[:, 4:5], ALU.subtract),
                 waits=[(s_ve, s_ve.n)], sem=s_ve, inc=1)
            nvar = emit("vector", lambda e: e.tensor_scalar(t_stat[:, 5:6], t_stat[:, 5:6],
                                                            0.0, None, ALU.max),
                        waits=[(s_ve, s_ve.n)], sem=s_ve, inc=1)
            nsq = emit("scalar", lambda e: e.activation(t_stat[:, 6:7], t_stat[:, 5:6],
                                                        AF.Sqrt, bias=t_eps[:, 0:1]),
                       waits=[(s_ve, nvar)], sem=s_sc, inc=1)
            emit("vector", lambda e: e.reciprocal(t_stat[:, 7:8], t_stat[:, 6:7]),
                 waits=[(s_sc, nsq), (s_ve, s_ve.n)], sem=s_ve, inc=1)
            emit("vector", lambda e: e.tensor_tensor(t_stat[:, 8:9], t_stat[:, 2:3],
                                                     t_stat[:, 7:8], ALU.mult),
                 waits=[(s_ve, s_ve.n)], sem=s_ve, inc=1)
            emit("vector", lambda e: e.tensor_scalar(t_stat[:, 9:10], t_stat[:, 8:9],
                                                     -1.0, None, ALU.mult),
                 waits=[(s_ve, s_ve.n)], sem=s_ve, inc=1)
            wla = []
            if xlnb_g4[tt] is not None:
                wla.append(xlnb_g4[tt])
                xlnb_g4[tt] = None
            nap = emit("vector", lambda e, i=xv, o=xlv:
                       e.tensor_scalar(o, i, t_stat[:, 7:8], t_stat[:, 9:10],
                                       ALU.mult, ALU.add),
                       waits=wla + [(s_ve, s_ve.n)], sem=s_ve, inc=1)
            xbuf_g[tt % 2] = (s_ve, nap)
            naps.append(nap)
        return naps

    def emit_ln_pe(half, naps):
        n_xlnt_ready = None
        for tt in range(4):
            xlv = vb(XLNB_O[tt], DM)
            for q4 in range(8):
                b = 4 + (q4 % 2)
                tws = bw(b) + [(s_ve, naps[tt])]
                if tt == 0 and q4 == 0:
                    tws.append((s_te, s_te.n))
                    if xlnt_guard[half] is not None:
                        tws.append(xlnt_guard[half])
                        xlnt_guard[half] = None
                ntp = None
                for j in range(4):
                    icb = q4 * 4 + j
                    ntp = emit("tensor",
                               lambda e, b=b, j=j, ic=icb, i=xlv:
                               e.transpose(psb[b][:, j * 128:(j + 1) * 128],
                                           i[:, ic * 128:(ic + 1) * 128], t_identb.ap()),
                               waits=(tws if j == 0 else []), sem=s_te,
                               inc=(1 if j == 3 else 0))
                nev = emit("vector",
                           lambda e, b=b, q4=q4, tt=tt, h=half:
                           e.tensor_copy(xlnt3h(h)[:, q4 * 4:(q4 + 1) * 4,
                                                   tt * 128:(tt + 1) * 128],
                                         psb[b][:, 0:512]),
                           waits=[(s_te, ntp)], sem=s_ve, inc=1)
                bankg[b] = (s_ve, nev)
                n_xlnt_ready = nev
            xlnb_g4[tt] = (s_te, s_te.n)
        return n_xlnt_ready

    nxr_cur = emit_ln_pe(0, emit_ln_ve(0))
    for sb in range(1 if "p1one" in skip else NSB):
        own = sb < NQB
        half = sb % 2
        drain_dq(2)

        def proj(bsrc, noc, svbase, store_fn, nxr=nxr_cur, h=half):
            for og in range(noc // 4):
                nmm = [None] * 4
                for icb in range(IC):
                    if icb % 4 == 0:
                        drain_dq(1)
                    slot = wring_i[0] % 8
                    wring_i[0] += 1
                    wv = wring[slot]
                    wwt = []
                    if wring_g[slot] is not None:
                        wwt.append(wring_g[slot])
                        wring_g[slot] = None
                    nld = dma(["sync", "gpsimd"][slot % 2], ring8[slot], wv,
                              bsrc[icb * 128:(icb + 1) * 128, og * 512:(og + 1) * 512],
                              waits=wwt)
                    for k in range(4):
                        tws = [(ring8[slot], nld)] if k == 0 else []
                        if icb == 0:
                            tws = tws + bw(k) + [(s_ve, nxr)]
                        nmm[k] = emit("tensor",
                                      lambda e, k=k, s=slot, ic=icb, h=h:
                                      e.matmul(psf[k][:, 0:512],
                                               wring[s][:, k * 128:(k + 1) * 128],
                                               xlnt_t(ic, h),
                                               start=(ic == 0), stop=(ic == IC - 1)),
                                      waits=tws, sem=s_te, inc=(1 if k == 3 else 0))
                    wring_g[slot] = (s_te, nmm[3])
                for k in range(4):
                    store_fn(og * 4 + k, k, nmm[3])

        def kvstore_mk(which):
            def f(oc, k, nmmlast):
                sslot = oc % 4
                swt = [(s_te, nmmlast)]
                if kstg_g[sslot] is not None:
                    swt.append(kstg_g[sslot])
                    kstg_g[sslot] = None
                sv = 32 if which == "k" else 40
                nev = emit("scalar",
                           lambda e, k=k, oc=oc, o=kstg[sslot], sv=sv:
                           e.activation(o, psf[k][:, 0:512], AF.Copy,
                                        scale=t_svec[:, sv + oc: sv + oc + 1]),
                           waits=swt, sem=s_sc, inc=1)
                bankg[k] = (s_sc, nev)
                if which == "k":
                    n = dma("scalar", stk4[sslot],
                            s_kscr[oc * 128:(oc + 1) * 128, sb * 512:(sb + 1) * 512],
                            kstg[sslot])
                    kstg_g[sslot] = (stk4[sslot], n)
                else:
                    b = 4 + (oc % 2)
                    tws = bw(b) + [(s_sc, nev)]
                    ntp = None
                    for j in range(4):
                        ntp = emit("tensor",
                                   lambda e, b=b, j=j, i=kstg[sslot]:
                                   e.transpose(psb[b][:, j * 128:(j + 1) * 128],
                                               i[:, j * 128:(j + 1) * 128], t_identb.ap()),
                                   waits=(tws if j == 0 else []), sem=s_te,
                                   inc=(1 if j == 3 else 0))
                    kstg_g[sslot] = (s_te, ntp)
                    vslot = oc % 2
                    vwt = [(s_te, ntp)]
                    if vstg_g[vslot] is not None:
                        vwt.append(vstg_g[vslot])
                        vstg_g[vslot] = None
                    nev2 = emit("vector",
                                lambda e, b=b, o=vstg[vslot]:
                                e.tensor_copy(o, psb[b][:, 0:512]),
                                waits=vwt, sem=s_ve, inc=1)
                    bankg[b] = (s_ve, nev2)
                    dstv = s_vscr[oc, sb * 512:(sb + 1) * 512, :]\
                        .rearrange("(j p) d -> p j d", p=128)
                    srcv = vstg[vslot].rearrange("p (j d) -> p j d", j=4)
                    n = dma("scalar", stv2[vslot], dstv, srcv, waits=[(s_ve, nev2)])
                    vstg_g[vslot] = (stv2[vslot], n)
            return f

        proj(s_bink, 8, 32, kvstore_mk("k"))
        naps_next = None
        if "p1one" not in skip and sb + 1 < NSB:
            naps_next = emit_ln_ve(sb + 1)
        proj(s_binv, 8, 40, kvstore_mk("v"))

        if own:
            qb = sb
            pend_q = []

            def qstore(oc, k, nmmlast, qb=qb, pend=pend_q):
                sslot = oc % 4
                swt = [(s_te, nmmlast)]
                if qstg_g[sslot] is not None:
                    swt.append(qstg_g[sslot])
                    qstg_g[sslot] = None
                nev = emit("scalar",
                           lambda e, k=k, oc=oc, o=qstg[sslot]:
                           e.activation(o, psf[k][:, 0:512], AF.Copy,
                                        scale=t_svec[:, oc: oc + 1]),
                           waits=swt, sem=s_sc, inc=1)
                bankg[k] = (s_sc, nev)
                pend.append((oc, sslot, nev))
                if len(pend) == 4:
                    og = pend[0][0] // 4
                    dstq = s_qscr[qb * DM + og * 512: qb * DM + (og + 1) * 512, :]\
                        .rearrange("(c p) n -> p c n", p=128)
                    srcq = vb(O_QSTG, 4 * 512).rearrange("p (c n) -> p c n", c=4)
                    n = dma("scalar", stq1, dstq, srcq, waits=[(s_sc, pend[-1][2])])
                    for (_, ss, _) in pend:
                        qstg_g[ss] = (stq1, n)
                    n_q_store[0] = n
                    pend.clear()
            proj(s_binq, 32, 0, qstore)
        if sb == 0 and "xlndbg" in skip:
            dbgstore("xlnt", vb(O_XLNT, IC * 512), [128, IC * 512], BF16)
            dbgstore("stat", t_stat[:, 0:24], [128, 24], F32)
        xlnt_guard[half] = (s_te, s_te.n)
        if naps_next is not None:
            nxr_cur = emit_ln_pe((sb + 1) % 2, naps_next)


    if PH == 1:
        drain_dq(10 ** 9)
        return _finish(nc, plan, es, fence, dma, emit, s_ld, s_gp, dumps, locals())
    drain_dq(10 ** 9)
    fence()
    O_KBUF = 0
    O_VBUF = 65536
    O_QRNG = 131072
    O_ABUF = 139264
    O_PBUF = 143360
    O_RBUF = 147456
    O_SQS = 151552
    O_KNS = 152576

    def kb_t(h):
        return vb(O_KBUF + h * 8192, SEQ)

    def vb_t(h, c):
        return vb(O_VBUF + (h * IC + c) * 256, 128)

    def qr_t(slot, hh):
        return vb(O_QRNG + slot * 4096 + hh * 1024, 512)
    qrng_g = [None] * 2
    abuf = [vb(O_ABUF + i * 1024, 512) for i in range(4)]
    abuf_g = [None] * 4
    pbuf = [vb(O_PBUF + i * 1024, 512) for i in range(4)]
    pbuf_g = [None] * 4
    rbuf = [vf(O_RBUF + i * 2048, 512) for i in range(2)]
    rbuf_g = [None] * 2
    sqs = vb(O_SQS, 512)
    kns = vb(O_KNS, DM)

    dma("sync", s_ld4a, vb(O_KBUF, NKVH * SEQ).rearrange("p (h n) -> p h n", h=NKVH),
        s_kscr.ap().rearrange("(h p) n -> p h n", p=128))
    n_vload = dma("sync", s_ld4a,
                  vb(O_VBUF, NKVH * IC * 128).rearrange("p (h c d) -> p h c d",
                                                        h=NKVH, c=IC),
                  s_vscr.ap().rearrange("h (c p) d -> p h c d", p=128))

    kns_guard = [None]
    if "nonorms" in skip:
        emit("vector", lambda e: e.memset(t_ebias.ap(), -8.0), sem=s_ve, inc=1)
    for kvh in range(NKVH) if "nonorms" not in skip else []:
        sw = [(s_ld4a, n_vload)]
        if kns_guard[0] is not None:
            sw.append(kns_guard[0])
            kns_guard[0] = None
        nqq = emit("scalar", lambda e, h=kvh: e.activation(kns, kb_t(h), AF.Square),
                   waits=sw, sem=s_sc, inc=1)
        nt = None
        for sl in range(8):
            tws = [(s_sc, nqq)] + bw(6)
            nt = emit("tensor",
                      lambda e, sl=sl: e.matmul(psf[6][:, 0:512], t_onesm.ap(),
                                                kns[:, sl * 512:(sl + 1) * 512],
                                                start=True, stop=True),
                      waits=tws, sem=s_te, inc=1)
            if sl == 0:
                nv = emit("vector",
                          lambda e, h=kvh:
                          e.reduce_max(t_mk2[0:1, h:h + 1], psf[6][0:1, 0:512], AX.X),
                          waits=[(s_te, nt), (s_ve, s_ve.n)], sem=s_ve, inc=1)
            else:
                emit("vector",
                     lambda e: e.reduce_max(t_stat[0:1, 16:17], psf[6][0:1, 0:512], AX.X),
                     waits=[(s_te, nt), (s_ve, s_ve.n)], sem=s_ve, inc=1)
                nv = emit("vector",
                          lambda e, h=kvh:
                          e.tensor_tensor(t_mk2[0:1, h:h + 1], t_mk2[0:1, h:h + 1],
                                          t_stat[0:1, 16:17], ALU.max),
                          waits=[(s_ve, s_ve.n)], sem=s_ve, inc=1)
            bankg[6] = (s_ve, nv)
        kns_guard[0] = (s_te, nt)

    n_ascr_store = [0]
    sqs_guard = [None]
    crow_guard = [None]
    ebias_guard = [None]
    for qb in range(1 if ("p4aone" in skip or "p4aqb1" in skip) else NQB):
        for kvh in range(1 if "p4aone" in skip else (2 if "p4akv2" in skip else NKVH)):
            qslot = (qb * NKVH + kvh) % 2
            qwt = [(stq1, n_q_store[0])]
            if qrng_g[qslot] is not None:
                qwt.append(qrng_g[qslot])
                qrng_g[qslot] = None
            nq = dma("sync", ring8[qslot],
                     vb(O_QRNG + qslot * 4096, 4 * 512)
                     .rearrange("p (h n) -> p h n", h=4),
                     s_qscr[qb * DM + kvh * 512: qb * DM + (kvh + 1) * 512, :]
                     .rearrange("(h p) n -> p h n", p=128), waits=qwt)
            for hh in range(4) if "nonorms" not in skip else []:
                sw = [(ring8[qslot], nq)]
                if sqs_guard[0] is not None:
                    sw.append(sqs_guard[0])
                    sqs_guard[0] = None
                nsq2 = emit("scalar",
                            lambda e, hh=hh, s=qslot:
                            e.activation(sqs, qr_t(s, hh), AF.Square),
                            waits=sw, sem=s_sc, inc=1)
                tws = [(s_sc, nsq2)] + bw(6)
                nt = emit("tensor",
                          lambda e: e.matmul(psf[6][:, 0:512], t_onesm.ap(), sqs,
                                             start=True, stop=True),
                          waits=tws, sem=s_te, inc=1)
                sqs_guard[0] = (s_te, nt)
                nv = emit("vector",
                          lambda e, hh=hh:
                          e.reduce_max(t_mq2[0:1, hh:hh + 1], psf[6][0:1, 0:512], AX.X),
                          waits=[(s_te, nt), (s_ve, s_ve.n)], sem=s_ve, inc=1)
                bankg[6] = (s_ve, nv)
            cw = []
            if crow_guard[0] is not None:
                cw.append(crow_guard[0])
                crow_guard[0] = None
            if "nonorms" in skip:
                ew = [ebias_guard[0]] if ebias_guard[0] is not None else []
                ebias_guard[0] = None
                nveb = emit("vector", lambda e: e.memset(t_ebias.ap(), -8.0),
                            waits=ew, sem=s_ve, inc=1)
            else:
                nv = emit("vector",
                          lambda e, h=kvh:
                          e.tensor_single_scalar(t_crow.ap(), t_mq2.ap(),
                                                 t_mk2[0:1, h:h + 1], ALU.mult),
                          waits=cw + [(s_ve, s_ve.n)], sem=s_ve, inc=1)
                nsq3 = emit("scalar",
                            lambda e: e.activation(t_crow.ap(), t_crow.ap(), AF.Sqrt),
                            waits=[(s_ve, nv)], sem=s_sc, inc=1)
                nv2 = emit("vector",
                           lambda e: e.tensor_scalar(t_crow.ap(), t_crow.ap(), -1.0, -1.0,
                                                     ALU.mult, ALU.add),
                           waits=[(s_sc, nsq3), (s_ve, s_ve.n)], sem=s_ve, inc=1)
                tws = [(s_ve, nv2)] + bw(6)
                if ebias_guard[0] is not None:
                    tws.append(ebias_guard[0])
                    ebias_guard[0] = None
                ntb = emit("tensor",
                           lambda e: e.matmul(psf[6][:, 0:4], t_onesr_f.ap(), t_crow.ap(),
                                              start=True, stop=True),
                           waits=tws, sem=s_te, inc=1)
                crow_guard[0] = (s_te, ntb)
                nveb = emit("vector", lambda e: e.tensor_copy(t_ebias.ap(), psf[6][:, 0:4]),
                            waits=[(s_te, ntb)], sem=s_ve, inc=1)
                bankg[6] = (s_ve, nveb)

            last_exp = None
            heads = [] if "noheads" in skip else list(range(4))
            for hh in heads:
                h = kvh * 4 + hh
                so = 2 + (h % 2)
                db = 4
                SBANKS = (0, 1, 5, 6)
                n_smm = [None] * IC
                n_exp = [None] * IC
                n_dmm = [None] * IC

                def emit_S(c):
                    sbk = SBANKS[c % 4]
                    if c >= 4:
                        tws = [(s_sc, n_exp[c - 4])]
                    else:
                        tws = bw(sbk)
                    if c == 0:
                        tws = tws + [(s_ve, nveb), (ring8[qslot], nq)]
                    n_smm[c] = emit("tensor",
                                    lambda e, c=c, sbk=sbk, h=kvh, hh=hh, s=qslot:
                                    e.matmul(psf[sbk][:, 0:512],
                                             kb_t(h)[:, c * 128:(c + 1) * 128],
                                             qr_t(s, hh),
                                             start=True, stop=True),
                                    waits=tws, sem=s_te, inc=1)

                def emit_exp(c):
                    sbk = SBANKS[c % 4]
                    tws = [(s_te, n_smm[c])]
                    pslot = c % 4
                    if pbuf_g[pslot] is not None:
                        tws.append(pbuf_g[pslot])
                        pbuf_g[pslot] = None
                    n_exp[c] = emit("scalar",
                                    lambda e, sbk=sbk, hh=hh, p=pslot:
                                    e.activation(pbuf[p], psf[sbk][:, 0:512], AF.Exp,
                                                 bias=t_ebias[:, hh:hh + 1]),
                                    waits=tws, sem=s_sc, inc=1)
                    bankg[sbk] = (s_sc, n_exp[c])

                def emit_PVD(c):
                    pslot = c % 4
                    tws = [(s_sc, n_exp[c])]
                    if c == 0:
                        tws = tws + bw(so) + bw(db)
                    if c < IC - 1:
                        emit("tensor",
                             lambda e, c=c, h=kvh, p=pslot, so=so:
                             e.matmul(psf[so][:, 0:512], vb_t(h, c), pbuf[p],
                                      start=(c == 0), stop=False),
                             waits=tws, sem=s_te, inc=0)
                        n_dmm[c] = emit("tensor",
                                        lambda e, c=c, p=pslot, db=db:
                                        e.matmul(psf[db][:, 0:512], t_onesm.ap(),
                                                 pbuf[p], start=(c == 0), stop=False),
                                        waits=[], sem=s_te, inc=1)
                    else:
                        emit("tensor",
                             lambda e, c=c, p=pslot, db=db:
                             e.matmul(psf[db][:, 0:512], t_onesm.ap(), pbuf[p],
                                      start=False, stop=True),
                             waits=tws, sem=s_te, inc=0)
                        n_dmm[c] = emit("tensor",
                                        lambda e, c=c, h=kvh, p=pslot, so=so:
                                        e.matmul(psf[so][:, 0:512], vb_t(h, c), pbuf[p],
                                                 start=False, stop=True),
                                        waits=[], sem=s_te, inc=1)
                    pbuf_g[pslot] = (s_te, n_dmm[c])

                emit_S(0)
                emit_exp(0)
                emit_S(1)
                emit_exp(1)
                emit_S(2)
                emit_exp(2)
                if "sonly" in skip:
                    for c in range(IC):
                        if c + 3 < IC:
                            emit_S(c + 3)
                            emit_exp(c + 3)
                        pbuf_g[c % 4] = (s_sc, n_exp[c])
                    last_exp = n_exp[IC - 1]
                    aslot = h % 4
                    awt = [(s_sc, last_exp)]
                    if abuf_g[aslot] is not None:
                        awt.append(abuf_g[aslot])
                        abuf_g[aslot] = None
                    nev = emit("vector",
                               lambda e, a=aslot: e.tensor_copy(abuf[a], pbuf[3]),
                               waits=awt, sem=s_ve, inc=1)
                    pbuf_g[3] = (s_ve, nev)
                    na = dma("gpsimd", s_gp,
                             s_ascr[qb * DM + h * 128: qb * DM + (h + 1) * 128, :],
                             abuf[aslot], waits=[(s_ve, nev)])
                    abuf_g[aslot] = (s_gp, na)
                    n_ascr_store[0] = na
                    continue
                for c in range(IC):
                    emit_PVD(c)
                    if c + 3 < IC:
                        emit_S(c + 3)
                        emit_exp(c + 3)
                last_exp = n_exp[IC - 1]
                if "nodiv" in skip:
                    aslot = h % 4
                    awt = [(s_te, n_dmm[IC - 1])] + bw(db)
                    if abuf_g[aslot] is not None:
                        awt.append(abuf_g[aslot])
                        abuf_g[aslot] = None
                    nev = emit("vector",
                               lambda e, a=aslot, so=so:
                               e.tensor_copy(abuf[a], psf[so][:, 0:512]),
                               waits=awt, sem=s_ve, inc=1)
                    bankg[so] = (s_ve, nev)
                    bankg[db] = (s_ve, nev)
                    na = dma("gpsimd", s_gp,
                             s_ascr[qb * DM + h * 128: qb * DM + (h + 1) * 128, :],
                             abuf[aslot], waits=[(s_ve, nev)])
                    abuf_g[aslot] = (s_gp, na)
                    n_ascr_store[0] = na
                    continue
                rslot = h % 2
                rwt = [(s_te, n_dmm[IC - 1]), (s_ve, s_ve.n)]
                if rbuf_g[rslot] is not None:
                    rwt.append(rbuf_g[rslot])
                    rbuf_g[rslot] = None
                nrb = emit("vector",
                           lambda e, r=rslot, db=db: e.reciprocal(rbuf[r], psf[db][:, 0:512]),
                           waits=rwt, sem=s_ve, inc=1)
                bankg[db] = (s_ve, nrb)
                aslot = h % 4
                awt = [(s_ve, nrb)]
                if abuf_g[aslot] is not None:
                    awt.append(abuf_g[aslot])
                    abuf_g[aslot] = None
                nev = emit("vector",
                           lambda e, a=aslot, r=rslot, so=so:
                           e.tensor_tensor(abuf[a], psf[so][:, 0:512], rbuf[r], ALU.mult),
                           waits=awt, sem=s_ve, inc=1)
                bankg[so] = (s_ve, nev)
                rbuf_g[rslot] = (s_ve, nev)
                na = dma("gpsimd", gpa4[aslot],
                         s_ascr[qb * DM + h * 128: qb * DM + (h + 1) * 128, :],
                         abuf[aslot], waits=[(s_ve, nev)])
                abuf_g[aslot] = (gpa4[aslot], na)
                n_ascr_store[0] = na
            qrng_g[qslot] = (s_te, s_te.n)
            if last_exp is not None:
                ebias_guard[0] = (s_sc, last_exp)
            drain_dq(8 if "oldpace" in skip else 16)

    drain_dq(10 ** 9)

    if PH == 2:
        return _finish(nc, plan, es, fence, dma, emit, s_ld, s_gp, dumps, locals())
    fence()
    O_X2B = 0
    O_X2LT = 32768
    O_HBUF = 65536
    O_WR4 = 153600
    O_OBF = 161792
    O_SGT = 163840
    O_DSTG = 165888
    O_OSTG = 169984
    O_NXG = 174080
    O_NXH = 175104
    O_ZG = 176128
    O_ZD = 178176
    O_X2LN = 179200

    x2t = [vb(O_X2B + j * 8192, DM) for j in range(4)]

    def x2lt_t(ic):
        return vb(O_X2LT + ic * 1024, 512)
    hbuf = [vb(O_HBUF + j * 1024, 512) for j in range(GJ)]
    wr4 = [vb(O_WR4 + i * 1024, 512) for i in range(8)]
    wr4_g = [None] * 8
    obf = vb(O_OBF, 512)
    sgt = vf(O_SGT, 512)
    dstg = [vf(O_DSTG + i * 2048, 512) for i in range(2)]
    dstg_g = [None] * 2
    ostg = [vf(O_OSTG + i * 2048, 512) for i in range(2)]
    ostg_g = [None] * 2
    nxg = vb(O_NXG, 512, 32)
    nxh = vb(O_NXH, 512, GJ)
    zgt = vb(O_ZG, 512, 32)
    zut = vb(O_ZG + 1024, 512, 32)
    zdt = vb(O_ZD, 512, GJ)
    x2ln = vb(O_X2LN, DM)

    n_out_store = [0]
    obf_g = [None]
    sgt_g = [None]
    zg_guard = [None]
    zd_guard = [None]
    x2ln_guard = [None]
    for qb in range(1 if "p4bone" in skip else NQB):
        ldq = s_ld4b[qb]
        n_x2i = []
        for j in range(4):
            n_x2i.append((gpc4[j], dma("gpsimd", gpc4[j], x2t[j],
                          d_xown[qb * 512 + j * 128: qb * 512 + (j + 1) * 128, :],
                          waits=[(s_ve, s_ve.n)] if j == 0 else [])))
        for og in range(8):
            nmm = [None] * 4
            for icb in range(IC):
                slot = icb % 4
                wwt = []
                if wr4_g[slot] is not None:
                    wwt.append(wr4_g[slot])
                    wr4_g[slot] = None
                nldw = dma("sync", ring8[slot], wr4[slot],
                           s_bino[icb * 128:(icb + 1) * 128, og * 512:(og + 1) * 512],
                           waits=wwt)
                aslot = 4 + icb % 4
                awt = []
                if wr4_g[aslot] is not None:
                    awt.append(wr4_g[aslot])
                    wr4_g[aslot] = None
                nlda = dma("sync", ring8[aslot], wr4[aslot],
                           s_ascr[qb * DM + icb * 128: qb * DM + (icb + 1) * 128, :],
                           waits=awt)
                for k in range(4):
                    tws = [(ring8[slot], nldw), (ring8[aslot], nlda)] if k == 0 else []
                    if icb == 0:
                        tws = tws + bw(k)
                    nmm[k] = emit("tensor",
                                  lambda e, k=k, s=slot, a=aslot, ic=icb:
                                  e.matmul(psf[k][:, 0:512],
                                           wr4[s][:, k * 128:(k + 1) * 128],
                                           wr4[a], start=(ic == 0), stop=(ic == IC - 1)),
                                  waits=tws, sem=s_te, inc=(1 if k == 3 else 0))
                wr4_g[slot] = (s_te, nmm[3])
                wr4_g[aslot] = (s_te, nmm[3])
            ntpl = None
            for k in range(4):
                oc = og * 4 + k
                swt = [(s_te, nmm[3])]
                if obf_g[0] is not None:
                    swt.append(obf_g[0])
                    obf_g[0] = None
                nevo = emit("scalar",
                            lambda e, k=k, oc=oc:
                            e.activation(obf, psf[k][:, 0:512], AF.Copy,
                                         scale=t_svec[:, 48 + oc: 48 + oc + 1]),
                            waits=swt, sem=s_sc, inc=1)
                bankg[k] = (s_sc, nevo)
                for j in range(4):
                    b = 4 + j
                    tws = [(s_sc, nevo)]
                    if k == 0:
                        tws = tws + bw(b)
                    ntpl = emit("tensor",
                                lambda e, b=b, j=j, k=k:
                                e.transpose(psb[b][:, k * 128:(k + 1) * 128],
                                            obf[:, j * 128:(j + 1) * 128], t_identb.ap()),
                                waits=tws, sem=s_te, inc=(1 if j == 3 else 0))
                obf_g[0] = (s_te, ntpl)
            for j in range(4):
                b = 4 + j
                awt = [(s_te, ntpl), n_x2i[j]]
                nadd = emit("vector",
                            lambda e, b=b, j=j, og=og:
                            e.tensor_tensor(x2t[j][:, og * 512:(og + 1) * 512],
                                            psb[b][:, 0:512],
                                            x2t[j][:, og * 512:(og + 1) * 512], ALU.add),
                            waits=awt, sem=s_ve, inc=1)
                bankg[b] = (s_ve, nadd)
        if qb == 0 and "x2dbg" in skip:
            for j in range(4):
                dbgstore(f"x2_{j}", x2t[j], [128, DM], BF16)
        n_x2lt_ready = None
        for j in range(4):
            lw = []
            if x2ln_guard[0] is not None:
                lw.append(x2ln_guard[0])
                x2ln_guard[0] = None
            emit("vector", lambda e, j=j: e.reduce_sum(t_stat[:, 0:1], x2t[j], AX.X),
                 waits=[(s_ve, s_ve.n)], sem=s_ve, inc=1)
            emit("vector", lambda e, j=j:
                 e.scalar_tensor_tensor(x2ln, x2t[j], 1.0, x2t[j], ALU.mult, ALU.mult,
                                        accum_out=t_stat[:, 1:2]),
                 waits=lw + [(s_ve, s_ve.n)], sem=s_ve, inc=1)
            emit("vector", lambda e: e.tensor_scalar(t_stat[:, 2:3], t_stat[:, 0:1],
                                                     1.0 / DM, None, ALU.mult),
                 waits=[(s_ve, s_ve.n)], sem=s_ve, inc=1)
            emit("vector", lambda e: e.tensor_scalar(t_stat[:, 3:4], t_stat[:, 1:2],
                                                     1.0 / DM, None, ALU.mult),
                 waits=[(s_ve, s_ve.n)], sem=s_ve, inc=1)
            emit("vector", lambda e: e.tensor_tensor(t_stat[:, 4:5], t_stat[:, 2:3],
                                                     t_stat[:, 2:3], ALU.mult),
                 waits=[(s_ve, s_ve.n)], sem=s_ve, inc=1)
            emit("vector", lambda e: e.tensor_tensor(t_stat[:, 5:6], t_stat[:, 3:4],
                                                     t_stat[:, 4:5], ALU.subtract),
                 waits=[(s_ve, s_ve.n)], sem=s_ve, inc=1)
            nvar = emit("vector", lambda e: e.tensor_scalar(t_stat[:, 5:6], t_stat[:, 5:6],
                                                            0.0, None, ALU.max),
                        waits=[(s_ve, s_ve.n)], sem=s_ve, inc=1)
            nsq = emit("scalar", lambda e: e.activation(t_stat[:, 6:7], t_stat[:, 5:6],
                                                        AF.Sqrt, bias=t_eps[:, 0:1]),
                       waits=[(s_ve, nvar)], sem=s_sc, inc=1)
            emit("vector", lambda e: e.reciprocal(t_stat[:, 7:8], t_stat[:, 6:7]),
                 waits=[(s_sc, nsq), (s_ve, s_ve.n)], sem=s_ve, inc=1)
            emit("vector", lambda e: e.tensor_tensor(t_stat[:, 8:9], t_stat[:, 2:3],
                                                     t_stat[:, 7:8], ALU.mult),
                 waits=[(s_ve, s_ve.n)], sem=s_ve, inc=1)
            emit("vector", lambda e: e.tensor_scalar(t_stat[:, 9:10], t_stat[:, 8:9],
                                                     -1.0, None, ALU.mult),
                 waits=[(s_ve, s_ve.n)], sem=s_ve, inc=1)
            nap = emit("vector", lambda e, j=j:
                       e.tensor_scalar(x2ln, x2t[j], t_stat[:, 7:8], t_stat[:, 9:10],
                                       ALU.mult, ALU.add),
                       waits=[(s_ve, s_ve.n)], sem=s_ve, inc=1)
            ntp = None
            for q4 in range(8):
                b = 4 + (q4 % 2)
                tws = bw(b) + [(s_ve, nap)]
                for jj in range(4):
                    icb = q4 * 4 + jj
                    ntp = emit("tensor",
                               lambda e, b=b, jj=jj, ic=icb:
                               e.transpose(psb[b][:, jj * 128:(jj + 1) * 128],
                                           x2ln[:, ic * 128:(ic + 1) * 128], t_identb.ap()),
                               waits=(tws if jj == 0 else []), sem=s_te,
                               inc=(1 if jj == 3 else 0))
                nev = emit("vector",
                           lambda e, b=b, q4=q4, j=j:
                           e.tensor_copy(vb(O_X2LT, IC * 512)
                                         .rearrange("p (c n) -> p c n", c=IC)
                                         [:, q4 * 4:(q4 + 1) * 4, j * 128:(j + 1) * 128],
                                         psb[b][:, 0:512]),
                           waits=[(s_te, ntp)], sem=s_ve, inc=1)
                bankg[b] = (s_ve, nev)
                n_x2lt_ready = nev
            x2ln_guard[0] = (s_te, ntp)
        if qb == 0 and "x2dbg" in skip:
            dbgstore("x2lt", vb(O_X2LT, IC * 512), [128, IC * 512], BF16)
        tws = bw(4) + [(s_ve, n_x2lt_ready), (s_ldu, n_consts)]
        ntx = None
        for icb in range(IC):
            ntx = emit("tensor",
                       lambda e, ic=icb:
                       e.matmul(psf[4][0:32, 0:512], t_band32[:, 31 - ic: 63 - ic],
                                x2lt_t(ic), start=(ic == 0), stop=(ic == IC - 1)),
                       waits=(tws if icb == 0 else []), sem=s_te,
                       inc=(1 if icb == IC - 1 else 0))
        nxg_n = emit("vector",
                     lambda e: e.tensor_scalar(nxg, psf[4][0:32, 0:512], -1.0, None,
                                               ALU.mult),
                     waits=[(s_te, ntx)], sem=s_ve, inc=1)
        bankg[4] = (s_ve, nxg_n)
        for j0 in range(0, GJ, 4):
            jn = min(4, GJ - j0)
            zw = []
            if zg_guard[0] is not None:
                zw.append(zg_guard[0])
                zg_guard[0] = None
            nzg = dma("sync", zs3[0], zgt[0:32, 0:jn * 128],
                      s_zpg[:, j0 * 128:(j0 + jn) * 128], waits=zw)
            nzu = dma("sync", zs3[1], zut[0:32, 0:jn * 128],
                      s_zpg[:, INTER + j0 * 128: INTER + (j0 + jn) * 128])
            nmm = [None] * 8
            for icb in range(IC):
                slot = (icb * 2) % 8
                slot2 = (icb * 2 + 1) % 8
                wwt = []
                if wr4_g[slot] is not None:
                    wwt.append(wr4_g[slot])
                    wr4_g[slot] = None
                nldg = dma("sync", ring8[slot], wr4[slot][:, 0:jn * 128],
                           s_w1g[icb * 128:(icb + 1) * 128, j0 * 128:(j0 + jn) * 128],
                           waits=wwt)
                wwt2 = []
                if wr4_g[slot2] is not None:
                    wwt2.append(wr4_g[slot2])
                    wr4_g[slot2] = None
                nldu = dma("sync", ring8[slot2], wr4[slot2][:, 0:jn * 128],
                           s_w1g[icb * 128:(icb + 1) * 128,
                                 INTER + j0 * 128: INTER + (j0 + jn) * 128],
                           waits=wwt2)
                nu_last = None
                for kk in range(2 * jn):
                    g_half = kk < jn
                    k = kk if g_half else kk - jn
                    sl_ = slot if g_half else slot2
                    bk = k if g_half else 4 + k
                    tws = [(ring8[slot], nldg), (ring8[slot2], nldu)] if kk == 0 else []
                    if icb == 0:
                        tws = tws + bw(bk)
                        if kk == 0:
                            tws = tws + [(s_ve, n_x2lt_ready)]
                    nu_last = emit("tensor",
                                   lambda e, bk=bk, k=k, s=sl_, ic=icb:
                                   e.matmul(psf[bk][:, 0:512],
                                            wr4[s][:, k * 128:(k + 1) * 128],
                                            x2lt_t(ic), start=(ic == 0), stop=False),
                                   waits=tws, sem=s_te,
                                   inc=(1 if kk == 2 * jn - 1 else 0))
                wr4_g[slot] = (s_te, nu_last)
                wr4_g[slot2] = (s_te, nu_last)
            for k in range(jn):
                tws = [(zs3[0], nzg), (s_ve, nxg_n)] if k == 0 else []
                nmm[k] = emit("tensor",
                              lambda e, k=k:
                              e.matmul(psf[k][:, 0:512], zgt[0:32, k * 128:(k + 1) * 128],
                                       nxg, start=False, stop=True),
                              waits=tws, sem=s_te, inc=1)
            for k in range(jn):
                tws = [(zs3[1], nzu)] if k == 0 else []
                nmm[4 + k] = emit("tensor",
                                  lambda e, k=k:
                                  e.matmul(psf[4 + k][:, 0:512],
                                           zut[0:32, k * 128:(k + 1) * 128],
                                           nxg, start=False, stop=True),
                                  waits=tws, sem=s_te, inc=1)
            zg_guard[0] = (s_te, nmm[4 + jn - 1])
            for k in range(jn):
                j = j0 + k
                sw = [(s_te, nmm[k])]
                if sgt_g[0] is not None:
                    sw.append(sgt_g[0])
                    sgt_g[0] = None
                nsg = emit("scalar",
                           lambda e, k=k: e.activation(sgt, psf[k][:, 0:512], AF.Silu),
                           waits=sw, sem=s_sc, inc=1)
                bankg[k] = (s_sc, nsg)
                nh = emit("vector",
                          lambda e, k=k, j=j:
                          e.tensor_tensor(hbuf[j], psf[4 + k][:, 0:512], sgt, ALU.mult),
                          waits=[(s_sc, nsg), (s_te, nmm[4 + k])], sem=s_ve, inc=1)
                bankg[4 + k] = (s_ve, nh)
                sgt_g[0] = (s_ve, nh)
        n_h_ready = s_ve.n
        if qb == 0 and "x2dbg" in skip:
            dbgstore("hb", vb(O_HBUF, GJ * 512), [128, GJ * 512], BF16)
            dbgstore("nxg", nxg, [32, 512], BF16)
        tws = bw(4) + [(s_ve, n_h_ready), (s_ldu, n_consts)]
        ntx = None
        for kc in range(GJ):
            ntx = emit("tensor",
                       lambda e, kc=kc:
                       e.matmul(psf[4][0:GJ, 0:512], t_band86[:, 85 - kc: 171 - kc],
                                hbuf[kc], start=(kc == 0), stop=(kc == GJ - 1)),
                       waits=(tws if kc == 0 else []), sem=s_te,
                       inc=(1 if kc == GJ - 1 else 0))
        nxh_n = emit("vector",
                     lambda e: e.tensor_scalar(nxh, psf[4][0:GJ, 0:512], -1.0, None,
                                               ALU.mult),
                     waits=[(s_te, ntx)], sem=s_ve, inc=1)
        bankg[4] = (s_ve, nxh_n)
        for og in range(8):
            zw = []
            if zd_guard[0] is not None:
                zw.append(zd_guard[0])
                zd_guard[0] = None
            nzd = dma("sync", zs3[2], zdt[0:GJ, :], s_zpd[:, og * 512:(og + 1) * 512],
                      waits=zw)
            nmm = [None] * 4
            for kc in range(GJ):
                slot = kc % 8
                wwt = []
                if wr4_g[slot] is not None:
                    wwt.append(wr4_g[slot])
                    wr4_g[slot] = None
                nld = dma("sync", ring8[slot], wr4[slot],
                          s_w1d[kc * 128:(kc + 1) * 128, og * 512:(og + 1) * 512],
                          waits=wwt)
                nd_last = None
                for k in range(4):
                    tws = [(ring8[slot], nld)] if k == 0 else []
                    if kc == 0:
                        tws = tws + bw(k) + [(s_ve, nxh_n)]
                    nd_last = emit("tensor",
                                   lambda e, k=k, s=slot, kc=kc:
                                   e.matmul(psf[k][:, 0:512],
                                            wr4[s][:, k * 128:(k + 1) * 128],
                                            hbuf[kc], start=(kc == 0), stop=False),
                                   waits=tws, sem=s_te, inc=(1 if k == 3 else 0))
                wr4_g[slot] = (s_te, nd_last)
            for k in range(4):
                tws = [(zs3[2], nzd)] if k == 0 else []
                nmm[k] = emit("tensor",
                              lambda e, k=k:
                              e.matmul(psf[k][:, 0:512], zdt[0:GJ, k * 128:(k + 1) * 128],
                                       nxh, start=False, stop=True),
                              waits=tws, sem=s_te, inc=1)
            zd_guard[0] = (s_te, nmm[3])
            ntp_fin = None
            for k in range(4):
                dslot = k % 2
                dwt = [(s_te, nmm[k])]
                if dstg_g[dslot] is not None:
                    dwt.append(dstg_g[dslot])
                    dstg_g[dslot] = None
                nevd = emit("vector",
                            lambda e, k=k, d=dslot: e.tensor_copy(dstg[d], psf[k][:, 0:512]),
                            waits=dwt, sem=s_ve, inc=1)
                bankg[k] = (s_ve, nevd)
                ntp = None
                for j in range(4):
                    b = 4 + j
                    tws = [(s_ve, nevd)]
                    if k == 0:
                        tws = tws + bw(b)
                    ntp = emit("tensor",
                               lambda e, b=b, j=j, k=k, d=dslot:
                               e.transpose(psf[b][:, k * 128:(k + 1) * 128],
                                           dstg[d][:, j * 128:(j + 1) * 128],
                                           t_identf.ap()),
                               waits=tws, sem=s_te, inc=(1 if j == 3 else 0))
                dstg_g[dslot] = (s_te, ntp)
                ntp_fin = ntp
            for j in range(4):
                b = 4 + j
                oslot = j % 2
                owt = [(s_te, ntp_fin)]
                if ostg_g[oslot] is not None:
                    owt.append(ostg_g[oslot])
                    ostg_g[oslot] = None
                nout = emit("vector",
                            lambda e, b=b, j=j, og=og, o=oslot:
                            e.tensor_tensor(ostg[o], psf[b][:, 0:512],
                                            x2t[j][:, og * 512:(og + 1) * 512], ALU.add),
                            waits=owt, sem=s_ve, inc=1)
                bankg[b] = (s_ve, nout)
                ng = dma("gpsimd", gpo2[oslot],
                         d_out[qb * 512 + j * 128: qb * 512 + (j + 1) * 128,
                               og * 512:(og + 1) * 512],
                         ostg[oslot], waits=[(s_ve, nout)])
                ostg_g[oslot] = (gpo2[oslot], ng)
                n_out_store[0] = ng

    g0, g1 = gpo2[0].n, gpo2[1].n
    plan["gpsimd"].append(lambda e, a=g0, b=g1:
                          (e.wait_ge(gpo2[0].h, a), e.wait_ge(gpo2[1].h, b)))
    if dumps:
        return _finish(nc, plan, es, fence, dma, emit, s_ld, s_gp, dumps, locals())

    with nc.Block() as block:
        @block.sync
        def _(e):
            for f in plan["sync"]:
                f(e)

        @block.scalar
        def _(e):
            for f in plan["scalar"]:
                f(e)

        @block.vector
        def _(e):
            for f in plan["vector"]:
                f(e)

        @block.tensor
        def _(e):
            for f in plan["tensor"]:
                f(e)

        @block.gpsimd
        def _(e):
            for f in plan["gpsimd"]:
                f(e)

    es.close()
    return nc


def _band(w, col):
    a = np.zeros((128, w), dtype=np.float32)
    a[:, col] = 1.0
    return a.astype(ml_dtypes.bfloat16)


_NC_CACHE = [None]


def kernel(**inputs):
    hs = np.asarray(inputs["L0_attn_hidden_states"])
    ins = {
        "qw": np.ascontiguousarray(inputs["L0_attn_q_weight"]),
        "kw": np.ascontiguousarray(inputs["L0_attn_k_weight"]),
        "vw": np.ascontiguousarray(inputs["L0_attn_v_weight"]),
        "ow": np.ascontiguousarray(inputs["L0_attn_o_weight"]),
        "qs": np.ascontiguousarray(inputs["L0_attn_q_scale"]),
        "ks": np.ascontiguousarray(inputs["L0_attn_k_scale"]),
        "vs": np.ascontiguousarray(inputs["L0_attn_v_scale"]),
        "os_": np.ascontiguousarray(inputs["L0_attn_o_scale"]),
        "gqw": np.ascontiguousarray(inputs["L0_ffn_qweight_gu"]),
        "gqz": np.ascontiguousarray(inputs["L0_ffn_qzeros_gu"]),
        "gsc": np.ascontiguousarray(inputs["L0_ffn_scales_gu"]),
        "dqw": np.ascontiguousarray(inputs["L0_ffn_qweight_down"]),
        "dqz": np.ascontiguousarray(inputs["L0_ffn_qzeros_down"]),
        "dsc": np.ascontiguousarray(inputs["L0_ffn_scales_down"]),
        "identf": np.eye(128, dtype=np.float32),
        "band32": _band(63, 31),
        "band86": _band(171, 85),
        "shiftv": ((np.arange(128, dtype=np.uint32) % 8) * 4).reshape(128, 1),
    }
    in_maps = []
    for c in range(8):
        b, h = c // 2, c % 2
        m = dict(ins)
        m["x_own"] = np.ascontiguousarray(hs[b, h * TOWN:(h + 1) * TOWN])
        m["x_oth"] = np.ascontiguousarray(hs[b, (1 - h) * TOWN:(2 - h) * TOWN])
        in_maps.append(m)
    if _NC_CACHE[0] is None:
        _NC_CACHE[0] = build_nc()
    res = run_bass_kernel_spmd(_NC_CACHE[0], in_maps, core_ids=list(range(8)))
    out = np.empty((4, SEQ, DM), dtype=np.float32)
    for c in range(8):
        b, h = c // 2, c % 2
        out[b, h * TOWN:(h + 1) * TOWN] = res.results[c]["out"]
    return out


# version 21
# speedup vs baseline: 1.0962x; 1.0536x over previous
import contextlib
import numpy as np
import ml_dtypes
import concourse.bass as bass
import concourse.mybir as mybir
from concourse.bass_utils import run_bass_kernel_spmd

F32 = mybir.dt.float32
BF16 = mybir.dt.bfloat16
U32 = mybir.dt.uint32
AF = mybir.ActivationFunctionType
ALU = mybir.AluOpType
AX = mybir.AxisListType

DM = 4096
KVD = 1024
NKVH = 8
INTER = 11008
GUC = 2 * INTER
SEQ = 4096
TOWN = 2048
TB = 512
NQB = TOWN // TB
NSB = SEQ // TB
IC = DM // 128
GJ = INTER // 128
RT128 = 1.0 / float(np.sqrt(128.0))
EPS = 1e-5

ENGS = ("sync", "scalar", "vector", "tensor", "gpsimd")


DUMPCAT = {
    "binq": ("binq_d", lambda L: L["s_binq"][0:256, 0:512], [256, 512], "bf"),
    "bino": ("bino_d", lambda L: L["s_bino"][0:256, 0:512], [256, 512], "bf"),
    "w1g": ("w1g_d", lambda L: L["s_w1g"][0:256, 0:1024], [256, 1024], "bf"),
    "w1d": ("w1d_d", lambda L: L["s_w1d"][0:256, 0:1024], [256, 1024], "bf"),
    "zpg": ("zpg_d", lambda L: L["s_zpg"][:, :], [32, GUC], "bf"),
    "zpd": ("zpd_d", lambda L: L["s_zpd"][:, :], [GJ, DM], "bf"),
    "kscr": ("kscr_d", lambda L: L["s_kscr"][:, 0:512], [KVD, 512], "bf"),
    "kscr2": ("kscr2_d", lambda L: L["s_kscr"][:, 2048:2560], [KVD, 512], "bf"),
    "vscr": ("vscr_d", lambda L: L["s_vscr"][0, 0:512, :], [512, 128], "bf"),
    "qscr": ("qscr_d", lambda L: L["s_qscr"][0:DM, :], [DM, 512], "bf"),
    "ascr": ("ascr_d", lambda L: L["s_ascr"][0:DM, :], [DM, 512], "bf"),
}


def _finish(nc, plan, es, fence, dma, emit, s_ld, s_gp, dumps, L):
    fence()
    nc._dump_handles = {}
    for name in dumps:
        (dname, get, shape, dt) = DUMPCAT[name]
        dt = BF16 if dt == "bf" else F32
        dtn = nc.dram_tensor(dname, shape, dt, kind="ExternalOutput")
        nc._dump_handles[dname] = dtn
        dma("sync", s_ld, dtn[:, :], get(L))
    plan["gpsimd"].append(lambda e, v=s_ld.n, w=s_gp.n:
                          (e.wait_ge(s_ld.h, v), e.wait_ge(s_gp.h, w)))
    with nc.Block() as block:
        @block.sync
        def _(e):
            for f in plan["sync"]:
                f(e)

        @block.scalar
        def _(e):
            for f in plan["scalar"]:
                f(e)

        @block.vector
        def _(e):
            for f in plan["vector"]:
                f(e)

        @block.tensor
        def _(e):
            for f in plan["tensor"]:
                f(e)

        @block.gpsimd
        def _(e):
            for f in plan["gpsimd"]:
                f(e)
    es.close()
    return nc


class Sem:
    def __init__(self, es, nc, name):
        self.h = es.enter_context(nc.semaphore(name))
        self.n = 0

    def inc(self, k=1):
        self.n += k
        return self.n


def build_nc(upto="p4b", dumps=(), skip=()):
    PH = {"p0": 0, "p1": 1, "p4a": 2, "p4b": 3}[upto]
    skipset = [set(skip)]
    nc = bass.Bass("TRN2", target_bir_lowering=False)
    es = contextlib.ExitStack()
    plan = {e: [] for e in ENGS}

    d_xown = nc.dram_tensor("x_own", [TOWN, DM], F32, kind="ExternalInput")
    d_xoth = nc.dram_tensor("x_oth", [TOWN, DM], F32, kind="ExternalInput")
    d_qw = nc.dram_tensor("qw", [DM, DM], F32, kind="ExternalInput")
    d_kw = nc.dram_tensor("kw", [KVD, DM], F32, kind="ExternalInput")
    d_vw = nc.dram_tensor("vw", [KVD, DM], F32, kind="ExternalInput")
    d_ow = nc.dram_tensor("ow", [DM, DM], F32, kind="ExternalInput")
    d_qs = nc.dram_tensor("qs", [DM], F32, kind="ExternalInput")
    d_ks = nc.dram_tensor("ks", [KVD], F32, kind="ExternalInput")
    d_vs = nc.dram_tensor("vs", [KVD], F32, kind="ExternalInput")
    d_os = nc.dram_tensor("os_", [DM], F32, kind="ExternalInput")
    d_gqw = nc.dram_tensor("gqw", [512, GUC], U32, kind="ExternalInput")
    d_gqz = nc.dram_tensor("gqz", [32, GUC // 8], U32, kind="ExternalInput")
    d_gsc = nc.dram_tensor("gsc", [32, GUC], F32, kind="ExternalInput")
    d_dqw = nc.dram_tensor("dqw", [INTER // 8, DM], U32, kind="ExternalInput")
    d_dqz = nc.dram_tensor("dqz", [GJ, DM // 8], U32, kind="ExternalInput")
    d_dsc = nc.dram_tensor("dsc", [GJ, DM], F32, kind="ExternalInput")
    d_identf = nc.dram_tensor("identf", [128, 128], F32, kind="ExternalInput")
    d_band32 = nc.dram_tensor("band32", [128, 63], BF16, kind="ExternalInput")
    d_band86 = nc.dram_tensor("band86", [128, 171], BF16, kind="ExternalInput")
    d_shiftv = nc.dram_tensor("shiftv", [128, 1], U32, kind="ExternalInput")

    d_out = nc.dram_tensor("out", [TOWN, DM], F32, kind="ExternalOutput")

    s_binq = nc.dram_tensor("binq", [DM, DM], BF16)
    s_bink = nc.dram_tensor("bink", [DM, KVD], BF16)
    s_binv = nc.dram_tensor("binv", [DM, KVD], BF16)
    s_bino = nc.dram_tensor("bino", [DM, DM], BF16)
    s_w1g = nc.dram_tensor("w1g", [DM, GUC], BF16)
    s_w1d = nc.dram_tensor("w1d", [INTER, DM], BF16)
    d_ghl = nc.dram_tensor("ghl", [2, 32 * GUC], BF16)
    d_dhl = nc.dram_tensor("dhl", [2, 86 * DM], BF16)
    s_zpg = nc.dram_tensor("zpg", [32, GUC], BF16)
    s_ghi = nc.dram_tensor("ghi", [32, GUC], BF16)
    s_glo = nc.dram_tensor("glo", [32, GUC], BF16)
    s_dhi = nc.dram_tensor("dhi", [GJ, DM], BF16)
    s_dlo = nc.dram_tensor("dlo", [GJ, DM], BF16)
    s_zpd = nc.dram_tensor("zpd", [GJ, DM], BF16)
    s_kscr = nc.dram_tensor("kscr", [KVD, SEQ], BF16)
    s_vscr = nc.dram_tensor("vscr", [NKVH, SEQ, 128], BF16)
    s_qscr = nc.dram_tensor("qscr", [NQB * DM, TB], BF16)
    s_ascr = nc.dram_tensor("ascr", [NQB * DM, TB], BF16)

    s_ld = Sem(es, nc, "s_ld")
    s_ld4a = Sem(es, nc, "s_ld4a")
    s_ld4b = [Sem(es, nc, f"s_ld4b{i}") for i in range(NQB)]
    s_scst = Sem(es, nc, "s_scst")
    s_gp = Sem(es, nc, "s_gp")
    s_dqld = Sem(es, nc, "s_dqld")
    s_dqld2 = Sem(es, nc, "s_dqld2")
    s_dqst = Sem(es, nc, "s_dqst")
    s_dqst2 = Sem(es, nc, "s_dqst2")
    s_ldz = Sem(es, nc, "s_ldz")
    s_ldu = Sem(es, nc, "s_ldu")
    ring8 = [Sem(es, nc, f"s_r{i}") for i in range(8)]
    xb2 = [Sem(es, nc, f"s_xb{i}") for i in range(2)]
    wb4 = [Sem(es, nc, f"s_wb{i}") for i in range(4)]
    zs3 = [Sem(es, nc, f"s_z{i}") for i in range(3)]
    stk4 = [Sem(es, nc, f"s_stk{i}") for i in range(4)]
    stv2 = [Sem(es, nc, f"s_stv{i}") for i in range(2)]
    stq1 = Sem(es, nc, "s_stq")
    gpc4 = [Sem(es, nc, f"s_gpc{i}") for i in range(4)]
    gpa4 = [Sem(es, nc, f"s_gpa{i}") for i in range(4)]
    gpo2 = [Sem(es, nc, f"s_gpo{i}") for i in range(2)]
    s_te = Sem(es, nc, "s_te")
    s_ve = Sem(es, nc, "s_ve")
    s_sc = Sem(es, nc, "s_sc")
    ALL_SEMS = ([s_ld, s_ld4a, s_scst, s_gp, s_dqld, s_dqst, s_ldz, s_ldu, s_te, s_ve, s_sc]
                + s_ld4b + ring8 + xb2 + wb4 + zs3 + stk4 + stv2 + [stq1] + gpc4 + gpa4 + gpo2
                + [s_dqld2, s_dqst2])

    t_identf = es.enter_context(nc.sbuf_tensor("c_identf", [128, 128], F32))
    t_identb = es.enter_context(nc.sbuf_tensor("c_identb", [128, 128], BF16))
    t_onesr_f = es.enter_context(nc.sbuf_tensor("c_onesr_f", [1, 128], F32))
    t_onesc_b = es.enter_context(nc.sbuf_tensor("c_onesc_b", [128, 1], BF16))
    t_onesm = es.enter_context(nc.sbuf_tensor("c_onesm", [128, 128], BF16))
    t_onesr_b = es.enter_context(nc.sbuf_tensor("c_onesr_b", [1, 128], BF16))
    t_band32 = es.enter_context(nc.sbuf_tensor("c_band32", [128, 63], BF16))
    t_band86 = es.enter_context(nc.sbuf_tensor("c_band86", [128, 171], BF16))
    t_shiftv = es.enter_context(nc.sbuf_tensor("c_shiftv", [128, 1], U32))
    t_svec = es.enter_context(nc.sbuf_tensor("c_svec", [128, 80], F32))
    t_qsin = es.enter_context(nc.sbuf_tensor("c_qsin", [128, 80], F32))
    t_ebias = es.enter_context(nc.sbuf_tensor("c_ebias", [128, 4], F32))
    t_mk2 = es.enter_context(nc.sbuf_tensor("c_mk2", [1, 8], F32))
    t_mq2 = es.enter_context(nc.sbuf_tensor("c_mq2", [1, 4], F32))
    t_crow = es.enter_context(nc.sbuf_tensor("c_crow", [1, 4], F32))
    t_stat = es.enter_context(nc.sbuf_tensor("c_stat", [128, 24], F32))
    t_rd = es.enter_context(nc.sbuf_tensor("c_rd", [1, 512], F32))
    t_rd2 = es.enter_context(nc.sbuf_tensor("c_rd2", [1, 512], F32))
    t_eps = es.enter_context(nc.sbuf_tensor("c_eps", [128, 1], F32))

    AR = 190 * 1024
    t_ar = es.enter_context(nc.sbuf_tensor("arena", [128, AR], mybir.dt.uint8))
    h_f32 = t_ar.bitcast(F32)
    h_bf = t_ar.bitcast(BF16)
    h_u32 = t_ar.bitcast(U32)

    def vf(off, cols, rows=128):
        assert off % 4 == 0 and off + cols * 4 <= AR, (off, cols)
        return h_f32[0:rows, off // 4: off // 4 + cols]

    def vb(off, cols, rows=128):
        assert off % 2 == 0 and off + cols * 2 <= AR, (off, cols)
        return h_bf[0:rows, off // 2: off // 2 + cols]

    def vu(off, cols, rows=128):
        assert off % 4 == 0 and off + cols * 4 <= AR, (off, cols)
        return h_u32[0:rows, off // 4: off // 4 + cols]

    psh = [es.enter_context(nc.psum_tensor(f"ps{i}", [128, 512], F32)) for i in range(8)]
    psf = [h.ap() for h in psh]
    psb = [h.bitcast(BF16).ap() for h in psh]
    bankg = [None] * 8

    def emit(eng, fn, waits=(), sem=None, inc=0):
        n = sem.inc(inc) if (sem is not None and inc) else None
        wt = tuple((s, v) for (s, v) in waits if v)

        def f(e, fn=fn, wt=wt, sem=sem, inc=inc):
            for (sh, v) in wt:
                e.wait_ge(sh.h, v)
            ins = fn(e)
            if sem is not None and inc:
                ins.then_inc(sem.h, inc)
        plan[eng].append(f)
        return n

    def dma(eng, sem, out_ap, in_ap, waits=(), slow=False):
        return emit(eng, lambda e, o=out_ap, i=in_ap, sl=slow:
                    e.dma_start(out=o, in_=i, allow_slow_non_contiguous=sl),
                    waits=waits, sem=sem, inc=16)

    def bw(b):
        gclaim = bankg[b]
        bankg[b] = None
        return [gclaim] if gclaim is not None else []

    def ser(semq):
        if "serload" in skipset[0]:
            return [(semq, semq.n)]
        return []

    def fence():
        tg = tuple((s, s.n) for s in ALL_SEMS if s.n > 0)
        for eng in ENGS:
            def f(e, tg=tg):
                for (sh, v) in tg:
                    e.wait_ge(sh.h, v)
            plan[eng].append(f)
        for i in range(8):
            bankg[i] = None

    dbg_stores = []

    def dbgstore(name, ap, shape, dtype):
        t = nc.dram_tensor("dbg_" + name, shape, dtype, kind="ExternalOutput")
        n = dma("scalar", s_scst, t[:, :], ap,
                waits=[(s_ve, s_ve.n), (s_sc, s_sc.n), (s_te, s_te.n)])
        dbg_stores.append(n)

    dma("sync", s_ldu, t_identf.ap(), d_identf[:, :])
    dma("sync", s_ldu, t_band32.ap(), d_band32[:, :])
    dma("sync", s_ldu, t_band86.ap(), d_band86[:, :])
    dma("sync", s_ldu, t_shiftv.ap(), d_shiftv[:, :])
    dma("sync", s_ldu, t_qsin[:, 0:32], d_qs.ap().rearrange("(c p) -> p c", p=128), slow=True)
    dma("sync", s_ldu, t_qsin[:, 32:40], d_ks.ap().rearrange("(c p) -> p c", p=128), slow=True)
    dma("sync", s_ldu, t_qsin[:, 40:48], d_vs.ap().rearrange("(c p) -> p c", p=128), slow=True)
    n_consts = dma("sync", s_ldu, t_qsin[:, 48:80], d_os.ap().rearrange("(c p) -> p c", p=128), slow=True)
    emit("vector", lambda e: e.tensor_copy(t_identb.ap(), t_identf.ap()),
         waits=[(s_ldu, n_consts)], sem=s_ve, inc=1)
    emit("vector", lambda e: e.memset(t_onesr_f.ap(), 1.0), sem=s_ve, inc=1)
    emit("vector", lambda e: e.memset(t_onesc_b.ap(), 1.0), sem=s_ve, inc=1)
    emit("vector", lambda e: e.memset(t_onesm.ap(), 1.0), sem=s_ve, inc=1)
    emit("vector", lambda e: e.memset(t_onesr_b.ap(), 1.0), sem=s_ve, inc=1)
    n_base_ve = emit("vector", lambda e: e.memset(t_eps.ap(), EPS), sem=s_ve, inc=1)

    DQ0 = 160 * 1024
    O_DQZ = 131072
    O_DQU = 132480
    O_DQR = 143488
    O_DQW = 154496
    ZON = [133120, 133120 + 13760]
    dqc = {"ve": 0, "scst": 0}
    dqp = {"g": 0, "guard": [None, None], "mmg": [None, None]}
    dq_mode = ["pe"]
    dqz = [{"ve": None, "st": None}, {"ve": None, "st": None}]
    dq_sems = [(s_dqld, s_dqst), (s_dqld2, s_dqst2)]
    dq_zi = [0]

    dequant_tasks = []

    def zp_task(osl, gu):
        def gen():
            if gu:
                pw, width = 344, 2752
                src_qz = d_gqz.reshape([128, 688])[:, osl * pw:(osl + 1) * pw]
                src_sc = d_gsc.reshape([128, 5504])[:, osl * width:(osl + 1) * width]
                dst = s_zpg.reshape([128, 5504])[:, osl * width:(osl + 1) * width]
            else:
                pw, width = 344, 2752
                src_qz = d_dqz.reshape([128, 344])[:, :]
                src_sc = d_dsc.reshape([128, 2752])[:, :]
                dst = s_zpd.reshape([128, 2752])[:, :]
            qz = vu(O_DQZ, pw)
            znib = vu(O_DQU, width)
            znib3 = znib.rearrange("p (c k) -> p c k", k=8)
            scv = vf(O_DQR, width)
            outb = vb(O_DQW, width)
            wz = [(s_ve, dqc["ve"]), (s_dqst, dqc["scst"])]
            n1 = dma("gpsimd", s_dqld, qz, src_qz, waits=wz)
            n2 = dma("gpsimd", s_dqld, scv, src_sc)
            for k in range(8):
                emit("vector",
                     lambda e, o=znib3[:, :, k:k + 1], i=qz.unsqueeze(2), sh=4 * k:
                     e.tensor_scalar(o, i, float(sh), 15.0,
                                     ALU.logical_shift_right, ALU.bitwise_and),
                     waits=[(s_dqld, n2), (s_ve, s_ve.n)], sem=s_ve, inc=1)
            nv = emit("vector",
                      lambda e, o=outb, i=znib, s=scv:
                      e.scalar_tensor_tensor(o, i, -7.0, s, ALU.add, ALU.mult),
                      waits=[(s_dqld, n2), (s_ve, s_ve.n), (s_dqst, dqc["scst"])],
                      sem=s_ve, inc=1)
            ns = dma("scalar", s_dqst, dst, outb, waits=[(s_ve, nv)])
            dqc["ve"] = nv
            dqc["scst"] = ns
            for z in range(2):
                dqz[z]["ve"] = (s_ve, nv)
                dqz[z]["st"] = (s_dqst, ns)
        return gen

    def hl_task(h, gu):
        def gen():
            if gu:
                src = d_gsc.reshape([128, 5504])[:, h * 2752:(h + 1) * 2752]
                dsth = d_ghl.reshape([2, 128, 5504])[0, :, h * 2752:(h + 1) * 2752]
                dstl = d_ghl.reshape([2, 128, 5504])[1, :, h * 2752:(h + 1) * 2752]
            else:
                src = d_dsc.reshape([128, 2752])[:, :]
                dsth = d_dhl.reshape([2, 128, 2752])[0, :, :]
                dstl = d_dhl.reshape([2, 128, 2752])[1, :, :]
            scf = vf(O_DQU, 2752)
            hib = vb(O_DQR, 2752)
            lob = vb(O_DQW, 2752)
            wz = [(s_ve, dqc["ve"]), (s_dqst, dqc["scst"])]
            n1 = dma("gpsimd", s_dqld, scf, src, waits=wz)
            nh = emit("vector", lambda e: e.tensor_copy(hib, scf),
                      waits=[(s_dqld, n1), (s_ve, s_ve.n), (s_dqst, dqc["scst"])],
                      sem=s_ve, inc=1)
            nl = emit("vector",
                      lambda e: e.tensor_tensor(lob, scf, hib, ALU.subtract),
                      waits=[(s_ve, s_ve.n)], sem=s_ve, inc=1)
            dma("scalar", s_dqst, dsth, hib, waits=[(s_ve, nh)])
            ns2 = dma("scalar", s_dqst, dstl, lob, waits=[(s_ve, nl)])
            dqc["ve"] = nl
            dqc["scst"] = ns2
            for z in range(2):
                dqz[z]["ve"] = (s_ve, nl)
                dqz[z]["st"] = (s_dqst, ns2)
        return gen

    def w1_task(it, osl, gu):
        def gen():
            if gu:
                width = 1376
                src_w = d_gqw[it * 16:(it + 1) * 16, osl * width:(osl + 1) * width]
                src_sc = d_gsc[it:it + 1, osl * width:(osl + 1) * width]
                dst = s_w1g[it * 128:(it + 1) * 128, osl * width:(osl + 1) * width]
                hs = it * GUC + osl * width
                hl_src = d_ghl[:, hs:hs + width]
            else:
                width = 1024
                src_w = d_dqw[it * 16:(it + 1) * 16, osl * width:(osl + 1) * width]
                src_sc = d_dsc[it:it + 1, osl * width:(osl + 1) * width]
                dst = s_w1d[it * 128:(it + 1) * 128, osl * width:(osl + 1) * width]
                hs = it * DM + osl * width
                hl_src = d_dhl[:, hs:hs + width]
            z = dq_zi[0] % 2
            dq_zi[0] += 1
            sld, sst = dq_sems[z]
            qbuf = vu(ZON[z], width)
            w1 = vb(ZON[z] + 5504, width)
            hib = vb(ZON[z] + 8256, width, 1)
            lob = vb(ZON[z] + 11008, width, 1)
            wz = [w for w in (dqz[z]["ve"], dqz[z]["st"]) if w is not None]
            prev_store = dqz[z]["st"]
            dma("gpsimd", sld, qbuf,
                src_w.unsqueeze(1).broadcast_to([16, 8, width]), waits=wz)
            dma("gpsimd", sld, hib, hl_src[0:1, :])
            n2 = dma("gpsimd", sld, lob, hl_src[1:2, :])
            emit("vector",
                 lambda e, o=qbuf, i=qbuf:
                 e.tensor_scalar(o, i, t_shiftv.ap(), 15.0,
                                 ALU.logical_shift_right, ALU.bitwise_and),
                 waits=[(sld, n2), (s_ldu, n_consts)], sem=s_ve, inc=1)
            last_ve = None
            c0 = 0
            ch = 0
            while c0 < width:
                cw = min(512, width - c0)
                sl = dqp["g"] % 2
                b = 6 + sl
                dqp["g"] += 1
                gg = dqp["guard"][sl]
                mmw = [(sld, n2), (s_ve, n_base_ve)] + ([gg] if gg else [])
                emit("tensor",
                     lambda e, b=b, c0=c0, cw=cw:
                     e.matmul(psf[b][:, 0:cw], t_onesr_b.ap(),
                              hib[0:1, c0:c0 + cw], start=True, stop=False),
                     waits=mmw, sem=s_te, inc=0)
                n_mm = emit("tensor",
                            lambda e, b=b, c0=c0, cw=cw:
                            e.matmul(psf[b][:, 0:cw], t_onesr_b.ap(),
                                     lob[0:1, c0:c0 + cw], start=False, stop=True),
                            sem=s_te, inc=1)
                stw = [(s_te, n_mm), (s_ve, s_ve.n)]
                if ch == 0 and prev_store:
                    stw.append(prev_store)
                last_ve = emit("vector",
                               lambda e, c0=c0, cw=cw, b=b:
                               e.scalar_tensor_tensor(w1[:, c0:c0 + cw],
                                                      qbuf[:, c0:c0 + cw], -8.0,
                                                      psf[b][:, 0:cw],
                                                      ALU.add, ALU.mult),
                               waits=stw, sem=s_ve, inc=1)
                dqp["guard"][sl] = (s_ve, last_ve)
                c0 += cw
                ch += 1
            ns = dma("scalar", sst, dst, w1, waits=[(s_ve, last_ve)])
            dqz[z]["ve"] = (s_ve, last_ve)
            dqz[z]["st"] = (sst, ns)
        return gen

    for i in range(2):
        dequant_tasks.append(zp_task(i, True))
    dequant_tasks.append(zp_task(0, False))
    dequant_tasks.append(hl_task(0, True))
    dequant_tasks.append(hl_task(1, True))
    dequant_tasks.append(hl_task(0, False))
    n_gu = 16
    n_dn = 4
    for it in range(IC):
        for osl in range(n_gu):
            dequant_tasks.append(w1_task(it, osl, True))
    for it in range(GJ):
        for osl in range(n_dn):
            dequant_tasks.append(w1_task(it, osl, False))
    if "dq" in skip:
        dequant_tasks.clear()
    for tok in skip:
        if tok.startswith("dqn"):
            del dequant_tasks[int(tok[3:]):]
    dq_idx = [0]

    def drain_dq(k):
        while k > 0 and dq_idx[0] < len(dequant_tasks):
            dequant_tasks[dq_idx[0]]()
            dq_idx[0] += 1
            k -= 1

    O_WBIN = 0
    O_SGN = 65536
    O_BSTG = 98304
    stg = vb(O_BSTG, IC * 512)
    stg3 = stg.rearrange("p (c n) -> p c n", c=IC)
    wspec = [(d_qw, s_binq, 32, 0, True),
             (d_kw, s_bink, 8, 32, False),
             (d_vw, s_binv, 8, 40, False),
             (d_ow, s_bino, 32, 48, False)]
    stg_guard = [None]
    sgn_guard = [None]
    n_bin_stores = 0
    if "bin" in skip:
        wspec = []
    for (dsrc, ddst, noc, svbase, isq) in wspec:
        for ocg in range(noc // 4):
            lw = []
            for k in range(4):
                oc = ocg * 4 + k
                wv = vf(O_WBIN + k * 16384, DM)
                lw.append((wb4[k], dma(["sync", "gpsimd"][k % 2], wb4[k], wv,
                                       dsrc[oc * 128:(oc + 1) * 128, :],
                                       waits=[(s_sc, s_sc.n), (s_ldu, n_consts)])))
            nsg = []
            for k in range(4):
                oc = ocg * 4 + k
                wv = vf(O_WBIN + k * 16384, DM)
                sg = vb(O_SGN + k * 8192, DM)
                sw = [lw[k]]
                if k == 0 and sgn_guard[0] is not None:
                    sw.append(sgn_guard[0])
                    sgn_guard[0] = None
                nsg.append(emit("scalar", lambda e, o=sg, i=wv: e.activation(o, i, AF.Sign),
                                waits=sw, sem=s_sc, inc=1))
                emit("vector",
                     lambda e, i=wv: e.reduce_sum(t_stat[:, 8:9], i, AX.X,
                                                  apply_absolute_value=True),
                     waits=[lw[k], (s_ve, s_ve.n)], sem=s_ve, inc=1)
                emit("vector",
                     lambda e: e.tensor_scalar(t_stat[:, 9:10], t_stat[:, 8:9], 1.0 / DM,
                                               1e-8, ALU.mult, ALU.max),
                     waits=[(s_ve, s_ve.n)], sem=s_ve, inc=1)
                emit("vector",
                     lambda e, col=svbase + oc, m=(RT128 if isq else 1.0):
                     e.tensor_scalar(t_svec[:, col:col + 1], t_stat[:, 9:10],
                                     t_qsin[:, col:col + 1], m, ALU.mult, ALU.mult),
                     waits=[(s_ve, s_ve.n)], sem=s_ve, inc=1)
            nev = None
            for icb in range(IC):
                if icb % 3 == 0:
                    drain_dq(0 if "nodq0" in skip else 1)
                b = 4 + (icb % 2)
                tws = bw(b) + [(s_sc, nsg[3]), (s_ve, n_base_ve)]
                if icb == 0 and stg_guard[0] is not None:
                    tws.append(stg_guard[0])
                    stg_guard[0] = None
                ntp = None
                for k in range(4):
                    sg = vb(O_SGN + k * 8192, DM)
                    ntp = emit("tensor",
                               lambda e, b=b, k=k, i=sg, ic=icb:
                               e.transpose(psb[b][:, k * 128:(k + 1) * 128],
                                           i[:, ic * 128:(ic + 1) * 128], t_identb.ap()),
                               waits=(tws if k == 0 else []), sem=s_te,
                               inc=(1 if k == 3 else 0))
                nev = emit("vector",
                           lambda e, b=b, ic=icb:
                           e.tensor_copy(stg[:, ic * 512:(ic + 1) * 512], psb[b][:, 0:512]),
                           waits=[(s_te, ntp)], sem=s_ve, inc=1)
                bankg[b] = (s_ve, nev)
            sgn_guard[0] = (s_te, s_te.n)
            dst = ddst[:, ocg * 512:(ocg + 1) * 512].rearrange("(c p) n -> p c n", p=128)
            n_bin_stores = dma("scalar", s_scst, dst, stg3, waits=[(s_ve, nev)])
            stg_guard[0] = (s_scst, n_bin_stores)

    if PH == 0:
        drain_dq(10 ** 9)
        return _finish(nc, plan, es, fence, dma, emit, s_ld, s_gp, dumps, locals())
    fence()
    O_XLNT = 0
    O_XBUF = 32768
    O_XLNB = 65536
    O_XSQ = 116736
    O_WRNG = 90112
    O_KSTG = 106496
    O_VSTG = 110592
    O_QSTG = 112640

    XLNT_O = [O_XLNT, 161792]
    def xlnt_t(ic, h):
        return vb(XLNT_O[h] + ic * 1024, 512)
    def xlnt3h(h):
        return vb(XLNT_O[h], IC * 512).rearrange("p (c n) -> p c n", c=IC)
    xlnt_guard = [None, None]
    wring = [vb(O_WRNG + i * 1024, 512) for i in range(8)]
    wring_g = [None] * 8
    wring_i = [0]
    kstg = [vb(O_KSTG + i * 1024, 512) for i in range(4)]
    kstg_g = [None] * 4
    vstg = [vb(O_VSTG + i * 1024, 512) for i in range(2)]
    vstg_g = [None] * 2
    qstg = [vb(O_QSTG + i * 1024, 512) for i in range(4)]
    qstg_g = [None] * 4
    xbuf_g = [None, None]
    xlnb_g = [None, None]
    n_q_store = [0]

    XLNB_O = [O_XLNB, O_XLNB + 8192, 81920, 98304]
    xlnb_g4 = [None] * 4

    def emit_ln_ve(sb):
        own = sb < NQB
        xsrc = d_xown if own else d_xoth
        r0 = (sb % NQB) * TB
        naps = []
        for tt in range(4):
            xv = vf(O_XBUF + (tt % 2) * 16384, DM)
            xlv = vb(XLNB_O[tt], DM)
            xq = vf(O_XSQ, DM)
            wts = [(s_ldu, n_consts)]
            if xbuf_g[tt % 2] is not None:
                wts.append(xbuf_g[tt % 2])
                xbuf_g[tt % 2] = None
            nldx = dma("sync", xb2[tt % 2], xv,
                       xsrc[r0 + tt * 128: r0 + (tt + 1) * 128, :], waits=wts)
            emit("scalar", lambda e, i=xv, o=xq:
                 e.activation(o, i, AF.Copy, accum_out=t_stat[:, 0:1]),
                 waits=[(xb2[tt % 2], nldx), (s_sc, s_sc.n)], sem=s_sc, inc=1)
            n_sqs = emit("scalar", lambda e, i=xv, o=xq:
                         e.activation(o, i, AF.Square, accum_out=t_stat[:, 1:2]),
                         waits=[(s_sc, s_sc.n)], sem=s_sc, inc=1)
            emit("vector", lambda e: e.tensor_scalar(t_stat[:, 2:3], t_stat[:, 0:1],
                                                     1.0 / DM, None, ALU.mult),
                 waits=[(s_sc, n_sqs), (s_ve, s_ve.n)], sem=s_ve, inc=1)
            emit("vector", lambda e: e.tensor_scalar(t_stat[:, 3:4], t_stat[:, 1:2],
                                                     1.0 / DM, None, ALU.mult),
                 waits=[(s_ve, s_ve.n)], sem=s_ve, inc=1)
            emit("vector", lambda e: e.tensor_tensor(t_stat[:, 4:5], t_stat[:, 2:3],
                                                     t_stat[:, 2:3], ALU.mult),
                 waits=[(s_ve, s_ve.n)], sem=s_ve, inc=1)
            emit("vector", lambda e: e.tensor_tensor(t_stat[:, 5:6], t_stat[:, 3:4],
                                                     t_stat[:, 4:5], ALU.subtract),
                 waits=[(s_ve, s_ve.n)], sem=s_ve, inc=1)
            nvar = emit("vector", lambda e: e.tensor_scalar(t_stat[:, 5:6], t_stat[:, 5:6],
                                                            0.0, None, ALU.max),
                        waits=[(s_ve, s_ve.n)], sem=s_ve, inc=1)
            nsq = emit("scalar", lambda e: e.activation(t_stat[:, 6:7], t_stat[:, 5:6],
                                                        AF.Sqrt, bias=t_eps[:, 0:1]),
                       waits=[(s_ve, nvar)], sem=s_sc, inc=1)
            emit("vector", lambda e: e.reciprocal(t_stat[:, 7:8], t_stat[:, 6:7]),
                 waits=[(s_sc, nsq), (s_ve, s_ve.n)], sem=s_ve, inc=1)
            emit("vector", lambda e: e.tensor_tensor(t_stat[:, 8:9], t_stat[:, 2:3],
                                                     t_stat[:, 7:8], ALU.mult),
                 waits=[(s_ve, s_ve.n)], sem=s_ve, inc=1)
            emit("vector", lambda e: e.tensor_scalar(t_stat[:, 9:10], t_stat[:, 8:9],
                                                     -1.0, None, ALU.mult),
                 waits=[(s_ve, s_ve.n)], sem=s_ve, inc=1)
            wla = []
            if xlnb_g4[tt] is not None:
                wla.append(xlnb_g4[tt])
                xlnb_g4[tt] = None
            nap = emit("vector", lambda e, i=xv, o=xlv:
                       e.tensor_scalar(o, i, t_stat[:, 7:8], t_stat[:, 9:10],
                                       ALU.mult, ALU.add),
                       waits=wla + [(s_ve, s_ve.n)], sem=s_ve, inc=1)
            xbuf_g[tt % 2] = (s_ve, nap)
            naps.append(nap)
        return naps

    def emit_ln_pe(half, naps):
        n_xlnt_ready = None
        for tt in range(4):
            xlv = vb(XLNB_O[tt], DM)
            for q4 in range(8):
                b = 4 + (q4 % 2)
                tws = bw(b) + [(s_ve, naps[tt])]
                if tt == 0 and q4 == 0:
                    tws.append((s_te, s_te.n))
                    if xlnt_guard[half] is not None:
                        tws.append(xlnt_guard[half])
                        xlnt_guard[half] = None
                ntp = None
                for j in range(4):
                    icb = q4 * 4 + j
                    ntp = emit("tensor",
                               lambda e, b=b, j=j, ic=icb, i=xlv:
                               e.transpose(psb[b][:, j * 128:(j + 1) * 128],
                                           i[:, ic * 128:(ic + 1) * 128], t_identb.ap()),
                               waits=(tws if j == 0 else []), sem=s_te,
                               inc=(1 if j == 3 else 0))
                nev = emit("vector",
                           lambda e, b=b, q4=q4, tt=tt, h=half:
                           e.tensor_copy(xlnt3h(h)[:, q4 * 4:(q4 + 1) * 4,
                                                   tt * 128:(tt + 1) * 128],
                                         psb[b][:, 0:512]),
                           waits=[(s_te, ntp)], sem=s_ve, inc=1)
                bankg[b] = (s_ve, nev)
                n_xlnt_ready = nev
            xlnb_g4[tt] = (s_te, s_te.n)
        return n_xlnt_ready

    nxr_cur = emit_ln_pe(0, emit_ln_ve(0))
    for sb in range(1 if "p1one" in skip else NSB):
        own = sb < NQB
        half = sb % 2
        drain_dq(2)

        def proj(bsrc, noc, svbase, store_fn, nxr=nxr_cur, h=half):
            for og in range(noc // 4):
                nmm = [None] * 4
                for icb in range(IC):
                    if icb % 4 == 0:
                        drain_dq(1)
                    slot = wring_i[0] % 8
                    wring_i[0] += 1
                    wv = wring[slot]
                    wwt = []
                    if wring_g[slot] is not None:
                        wwt.append(wring_g[slot])
                        wring_g[slot] = None
                    nld = dma(["sync", "gpsimd"][slot % 2], ring8[slot], wv,
                              bsrc[icb * 128:(icb + 1) * 128, og * 512:(og + 1) * 512],
                              waits=wwt)
                    for k in range(4):
                        tws = [(ring8[slot], nld)] if k == 0 else []
                        if icb == 0:
                            tws = tws + bw(k) + [(s_ve, nxr)]
                        nmm[k] = emit("tensor",
                                      lambda e, k=k, s=slot, ic=icb, h=h:
                                      e.matmul(psf[k][:, 0:512],
                                               wring[s][:, k * 128:(k + 1) * 128],
                                               xlnt_t(ic, h),
                                               start=(ic == 0), stop=(ic == IC - 1)),
                                      waits=tws, sem=s_te, inc=(1 if k == 3 else 0))
                    wring_g[slot] = (s_te, nmm[3])
                for k in range(4):
                    store_fn(og * 4 + k, k, nmm[3])

        def kvstore_mk(which):
            def f(oc, k, nmmlast):
                sslot = oc % 4
                swt = [(s_te, nmmlast)]
                if kstg_g[sslot] is not None:
                    swt.append(kstg_g[sslot])
                    kstg_g[sslot] = None
                sv = 32 if which == "k" else 40
                nev = emit("scalar",
                           lambda e, k=k, oc=oc, o=kstg[sslot], sv=sv:
                           e.activation(o, psf[k][:, 0:512], AF.Copy,
                                        scale=t_svec[:, sv + oc: sv + oc + 1]),
                           waits=swt, sem=s_sc, inc=1)
                bankg[k] = (s_sc, nev)
                if which == "k":
                    n = dma("scalar", stk4[sslot],
                            s_kscr[oc * 128:(oc + 1) * 128, sb * 512:(sb + 1) * 512],
                            kstg[sslot])
                    kstg_g[sslot] = (stk4[sslot], n)
                else:
                    b = 4 + (oc % 2)
                    tws = bw(b) + [(s_sc, nev)]
                    ntp = None
                    for j in range(4):
                        ntp = emit("tensor",
                                   lambda e, b=b, j=j, i=kstg[sslot]:
                                   e.transpose(psb[b][:, j * 128:(j + 1) * 128],
                                               i[:, j * 128:(j + 1) * 128], t_identb.ap()),
                                   waits=(tws if j == 0 else []), sem=s_te,
                                   inc=(1 if j == 3 else 0))
                    kstg_g[sslot] = (s_te, ntp)
                    vslot = oc % 2
                    vwt = [(s_te, ntp)]
                    if vstg_g[vslot] is not None:
                        vwt.append(vstg_g[vslot])
                        vstg_g[vslot] = None
                    nev2 = emit("vector",
                                lambda e, b=b, o=vstg[vslot]:
                                e.tensor_copy(o, psb[b][:, 0:512]),
                                waits=vwt, sem=s_ve, inc=1)
                    bankg[b] = (s_ve, nev2)
                    dstv = s_vscr[oc, sb * 512:(sb + 1) * 512, :]\
                        .rearrange("(j p) d -> p j d", p=128)
                    srcv = vstg[vslot].rearrange("p (j d) -> p j d", j=4)
                    n = dma("scalar", stv2[vslot], dstv, srcv, waits=[(s_ve, nev2)])
                    vstg_g[vslot] = (stv2[vslot], n)
            return f

        proj(s_bink, 8, 32, kvstore_mk("k"))
        naps_next = None
        if "p1one" not in skip and sb + 1 < NSB:
            naps_next = emit_ln_ve(sb + 1)
        proj(s_binv, 8, 40, kvstore_mk("v"))

        if own:
            qb = sb
            pend_q = []

            def qstore(oc, k, nmmlast, qb=qb, pend=pend_q):
                sslot = oc % 4
                swt = [(s_te, nmmlast)]
                if qstg_g[sslot] is not None:
                    swt.append(qstg_g[sslot])
                    qstg_g[sslot] = None
                nev = emit("scalar",
                           lambda e, k=k, oc=oc, o=qstg[sslot]:
                           e.activation(o, psf[k][:, 0:512], AF.Copy,
                                        scale=t_svec[:, oc: oc + 1]),
                           waits=swt, sem=s_sc, inc=1)
                bankg[k] = (s_sc, nev)
                pend.append((oc, sslot, nev))
                if len(pend) == 4:
                    og = pend[0][0] // 4
                    dstq = s_qscr[qb * DM + og * 512: qb * DM + (og + 1) * 512, :]\
                        .rearrange("(c p) n -> p c n", p=128)
                    srcq = vb(O_QSTG, 4 * 512).rearrange("p (c n) -> p c n", c=4)
                    n = dma("scalar", stq1, dstq, srcq, waits=[(s_sc, pend[-1][2])])
                    for (_, ss, _) in pend:
                        qstg_g[ss] = (stq1, n)
                    n_q_store[0] = n
                    pend.clear()
            proj(s_binq, 32, 0, qstore)
        if sb == 0 and "xlndbg" in skip:
            dbgstore("xlnt", vb(O_XLNT, IC * 512), [128, IC * 512], BF16)
            dbgstore("stat", t_stat[:, 0:24], [128, 24], F32)
        xlnt_guard[half] = (s_te, s_te.n)
        if naps_next is not None:
            nxr_cur = emit_ln_pe((sb + 1) % 2, naps_next)


    if PH == 1:
        drain_dq(10 ** 9)
        return _finish(nc, plan, es, fence, dma, emit, s_ld, s_gp, dumps, locals())
    drain_dq(10 ** 9)
    fence()
    O_KBUF = 0
    O_VBUF = 65536
    O_QRNG = 131072
    O_ABUF = 139264
    O_PBUF = 143360
    O_RBUF = 147456
    O_SQS = 151552
    O_KNS = 152576

    def kb_t(h):
        return vb(O_KBUF + h * 8192, SEQ)

    def vb_t(h, c):
        return vb(O_VBUF + (h * IC + c) * 256, 128)

    def qr_t(slot, hh):
        return vb(O_QRNG + slot * 4096 + hh * 1024, 512)
    qrng_g = [None] * 2
    abuf = [vb(O_ABUF + i * 1024, 512) for i in range(4)]
    abuf_g = [None] * 4
    pbuf = [vb(O_PBUF + i * 1024, 512) for i in range(4)]
    pbuf_g = [None] * 4
    rbuf = [vf(O_RBUF + i * 2048, 512) for i in range(2)]
    rbuf_g = [None] * 2
    sqs = vb(O_SQS, 512)
    kns = vb(O_KNS, DM)

    dma("sync", s_ld4a, vb(O_KBUF, NKVH * SEQ).rearrange("p (h n) -> p h n", h=NKVH),
        s_kscr.ap().rearrange("(h p) n -> p h n", p=128))
    vbv = vb(O_VBUF, NKVH * IC * 128).rearrange("p (h c d) -> p h c d",
                                                h=NKVH, c=IC)
    dma("scalar", s_ld4a, vbv[:, 0:3],
        s_vscr[0:3].rearrange("h (c p) d -> p h c d", p=128))
    dma("gpsimd", s_ld4a, vbv[:, 3:6],
        s_vscr[3:6].rearrange("h (c p) d -> p h c d", p=128))
    n_vload = dma("scalar", s_ld4a, vbv[:, 6:8],
                  s_vscr[6:8].rearrange("h (c p) d -> p h c d", p=128))

    kns_guard = [None]
    if "nonorms" in skip:
        emit("vector", lambda e: e.memset(t_ebias.ap(), -8.0), sem=s_ve, inc=1)
    for kvh in range(NKVH) if "nonorms" not in skip else []:
        sw = [(s_ld4a, n_vload)]
        if kns_guard[0] is not None:
            sw.append(kns_guard[0])
            kns_guard[0] = None
        nqq = emit("scalar", lambda e, h=kvh: e.activation(kns, kb_t(h), AF.Square),
                   waits=sw, sem=s_sc, inc=1)
        nt = None
        for sl in range(8):
            tws = [(s_sc, nqq)] + bw(6)
            nt = emit("tensor",
                      lambda e, sl=sl: e.matmul(psf[6][:, 0:512], t_onesm.ap(),
                                                kns[:, sl * 512:(sl + 1) * 512],
                                                start=True, stop=True),
                      waits=tws, sem=s_te, inc=1)
            if sl == 0:
                nv = emit("vector",
                          lambda e, h=kvh:
                          e.reduce_max(t_mk2[0:1, h:h + 1], psf[6][0:1, 0:512], AX.X),
                          waits=[(s_te, nt), (s_ve, s_ve.n)], sem=s_ve, inc=1)
            else:
                emit("vector",
                     lambda e: e.reduce_max(t_stat[0:1, 16:17], psf[6][0:1, 0:512], AX.X),
                     waits=[(s_te, nt), (s_ve, s_ve.n)], sem=s_ve, inc=1)
                nv = emit("vector",
                          lambda e, h=kvh:
                          e.tensor_tensor(t_mk2[0:1, h:h + 1], t_mk2[0:1, h:h + 1],
                                          t_stat[0:1, 16:17], ALU.max),
                          waits=[(s_ve, s_ve.n)], sem=s_ve, inc=1)
            bankg[6] = (s_ve, nv)
        kns_guard[0] = (s_te, nt)

    n_ascr_store = [0]
    sqs_guard = [None]
    crow_guard = [None]
    ebias_guard = [None]
    for qb in range(1 if ("p4aone" in skip or "p4aqb1" in skip) else NQB):
        for kvh in range(1 if "p4aone" in skip else (2 if "p4akv2" in skip else NKVH)):
            qslot = (qb * NKVH + kvh) % 2
            qwt = [(stq1, n_q_store[0])]
            if qrng_g[qslot] is not None:
                qwt.append(qrng_g[qslot])
                qrng_g[qslot] = None
            nq = dma("sync", ring8[qslot],
                     vb(O_QRNG + qslot * 4096, 4 * 512)
                     .rearrange("p (h n) -> p h n", h=4),
                     s_qscr[qb * DM + kvh * 512: qb * DM + (kvh + 1) * 512, :]
                     .rearrange("(h p) n -> p h n", p=128), waits=qwt)
            for hh in range(4) if "nonorms" not in skip else []:
                sw = [(ring8[qslot], nq)]
                if sqs_guard[0] is not None:
                    sw.append(sqs_guard[0])
                    sqs_guard[0] = None
                nsq2 = emit("scalar",
                            lambda e, hh=hh, s=qslot:
                            e.activation(sqs, qr_t(s, hh), AF.Square),
                            waits=sw, sem=s_sc, inc=1)
                tws = [(s_sc, nsq2)] + bw(6)
                nt = emit("tensor",
                          lambda e: e.matmul(psf[6][:, 0:512], t_onesm.ap(), sqs,
                                             start=True, stop=True),
                          waits=tws, sem=s_te, inc=1)
                sqs_guard[0] = (s_te, nt)
                nv = emit("vector",
                          lambda e, hh=hh:
                          e.reduce_max(t_mq2[0:1, hh:hh + 1], psf[6][0:1, 0:512], AX.X),
                          waits=[(s_te, nt), (s_ve, s_ve.n)], sem=s_ve, inc=1)
                bankg[6] = (s_ve, nv)
            cw = []
            if crow_guard[0] is not None:
                cw.append(crow_guard[0])
                crow_guard[0] = None
            if "nonorms" in skip:
                ew = [ebias_guard[0]] if ebias_guard[0] is not None else []
                ebias_guard[0] = None
                nveb = emit("vector", lambda e: e.memset(t_ebias.ap(), -8.0),
                            waits=ew, sem=s_ve, inc=1)
            else:
                nv = emit("vector",
                          lambda e, h=kvh:
                          e.tensor_single_scalar(t_crow.ap(), t_mq2.ap(),
                                                 t_mk2[0:1, h:h + 1], ALU.mult),
                          waits=cw + [(s_ve, s_ve.n)], sem=s_ve, inc=1)
                nsq3 = emit("scalar",
                            lambda e: e.activation(t_crow.ap(), t_crow.ap(), AF.Sqrt),
                            waits=[(s_ve, nv)], sem=s_sc, inc=1)
                nv2 = emit("vector",
                           lambda e: e.tensor_scalar(t_crow.ap(), t_crow.ap(), -1.0, -1.0,
                                                     ALU.mult, ALU.add),
                           waits=[(s_sc, nsq3), (s_ve, s_ve.n)], sem=s_ve, inc=1)
                tws = [(s_ve, nv2)] + bw(6)
                if ebias_guard[0] is not None:
                    tws.append(ebias_guard[0])
                    ebias_guard[0] = None
                ntb = emit("tensor",
                           lambda e: e.matmul(psf[6][:, 0:4], t_onesr_f.ap(), t_crow.ap(),
                                              start=True, stop=True),
                           waits=tws, sem=s_te, inc=1)
                crow_guard[0] = (s_te, ntb)
                nveb = emit("vector", lambda e: e.tensor_copy(t_ebias.ap(), psf[6][:, 0:4]),
                            waits=[(s_te, ntb)], sem=s_ve, inc=1)
                bankg[6] = (s_ve, nveb)

            last_exp = None
            heads = [] if "noheads" in skip else list(range(4))
            for hh in heads:
                h = kvh * 4 + hh
                so = 2 + (h % 2)
                db = 4
                SBANKS = (0, 1, 5, 6)
                n_smm = [None] * IC
                n_exp = [None] * IC
                n_dmm = [None] * IC

                def emit_S(c):
                    sbk = SBANKS[c % 4]
                    if c >= 4:
                        tws = [(s_sc, n_exp[c - 4])]
                    else:
                        tws = bw(sbk)
                    if c == 0:
                        tws = tws + [(s_ve, nveb), (ring8[qslot], nq)]
                    n_smm[c] = emit("tensor",
                                    lambda e, c=c, sbk=sbk, h=kvh, hh=hh, s=qslot:
                                    e.matmul(psf[sbk][:, 0:512],
                                             kb_t(h)[:, c * 128:(c + 1) * 128],
                                             qr_t(s, hh),
                                             start=True, stop=True),
                                    waits=tws, sem=s_te, inc=1)

                def emit_exp(c):
                    sbk = SBANKS[c % 4]
                    tws = [(s_te, n_smm[c])]
                    pslot = c % 4
                    if pbuf_g[pslot] is not None:
                        tws.append(pbuf_g[pslot])
                        pbuf_g[pslot] = None
                    n_exp[c] = emit("scalar",
                                    lambda e, sbk=sbk, hh=hh, p=pslot:
                                    e.activation(pbuf[p], psf[sbk][:, 0:512], AF.Exp,
                                                 bias=t_ebias[:, hh:hh + 1]),
                                    waits=tws, sem=s_sc, inc=1)
                    bankg[sbk] = (s_sc, n_exp[c])

                def emit_PVD(c):
                    pslot = c % 4
                    tws = [(s_sc, n_exp[c])]
                    if c == 0:
                        tws = tws + bw(so) + bw(db)
                    if c < IC - 1:
                        emit("tensor",
                             lambda e, c=c, h=kvh, p=pslot, so=so:
                             e.matmul(psf[so][:, 0:512], vb_t(h, c), pbuf[p],
                                      start=(c == 0), stop=False),
                             waits=tws, sem=s_te, inc=0)
                        n_dmm[c] = emit("tensor",
                                        lambda e, c=c, p=pslot, db=db:
                                        e.matmul(psf[db][:, 0:512], t_onesm.ap(),
                                                 pbuf[p], start=(c == 0), stop=False),
                                        waits=[], sem=s_te, inc=1)
                    else:
                        emit("tensor",
                             lambda e, c=c, p=pslot, db=db:
                             e.matmul(psf[db][:, 0:512], t_onesm.ap(), pbuf[p],
                                      start=False, stop=True),
                             waits=tws, sem=s_te, inc=0)
                        n_dmm[c] = emit("tensor",
                                        lambda e, c=c, h=kvh, p=pslot, so=so:
                                        e.matmul(psf[so][:, 0:512], vb_t(h, c), pbuf[p],
                                                 start=False, stop=True),
                                        waits=[], sem=s_te, inc=1)
                    pbuf_g[pslot] = (s_te, n_dmm[c])

                emit_S(0)
                emit_exp(0)
                emit_S(1)
                emit_exp(1)
                emit_S(2)
                emit_exp(2)
                if "sonly" in skip:
                    for c in range(IC):
                        if c + 3 < IC:
                            emit_S(c + 3)
                            emit_exp(c + 3)
                        pbuf_g[c % 4] = (s_sc, n_exp[c])
                    last_exp = n_exp[IC - 1]
                    aslot = h % 4
                    awt = [(s_sc, last_exp)]
                    if abuf_g[aslot] is not None:
                        awt.append(abuf_g[aslot])
                        abuf_g[aslot] = None
                    nev = emit("vector",
                               lambda e, a=aslot: e.tensor_copy(abuf[a], pbuf[3]),
                               waits=awt, sem=s_ve, inc=1)
                    pbuf_g[3] = (s_ve, nev)
                    na = dma("gpsimd", s_gp,
                             s_ascr[qb * DM + h * 128: qb * DM + (h + 1) * 128, :],
                             abuf[aslot], waits=[(s_ve, nev)])
                    abuf_g[aslot] = (s_gp, na)
                    n_ascr_store[0] = na
                    continue
                for c in range(IC):
                    emit_PVD(c)
                    if c + 3 < IC:
                        emit_S(c + 3)
                        emit_exp(c + 3)
                last_exp = n_exp[IC - 1]
                if "nodiv" in skip:
                    aslot = h % 4
                    awt = [(s_te, n_dmm[IC - 1])] + bw(db)
                    if abuf_g[aslot] is not None:
                        awt.append(abuf_g[aslot])
                        abuf_g[aslot] = None
                    nev = emit("vector",
                               lambda e, a=aslot, so=so:
                               e.tensor_copy(abuf[a], psf[so][:, 0:512]),
                               waits=awt, sem=s_ve, inc=1)
                    bankg[so] = (s_ve, nev)
                    bankg[db] = (s_ve, nev)
                    na = dma("gpsimd", s_gp,
                             s_ascr[qb * DM + h * 128: qb * DM + (h + 1) * 128, :],
                             abuf[aslot], waits=[(s_ve, nev)])
                    abuf_g[aslot] = (s_gp, na)
                    n_ascr_store[0] = na
                    continue
                rslot = h % 2
                rwt = [(s_te, n_dmm[IC - 1]), (s_ve, s_ve.n)]
                if rbuf_g[rslot] is not None:
                    rwt.append(rbuf_g[rslot])
                    rbuf_g[rslot] = None
                nrb = emit("vector",
                           lambda e, r=rslot, db=db: e.reciprocal(rbuf[r], psf[db][:, 0:512]),
                           waits=rwt, sem=s_ve, inc=1)
                bankg[db] = (s_ve, nrb)
                aslot = h % 4
                awt = [(s_ve, nrb)]
                if abuf_g[aslot] is not None:
                    awt.append(abuf_g[aslot])
                    abuf_g[aslot] = None
                nev = emit("vector",
                           lambda e, a=aslot, r=rslot, so=so:
                           e.tensor_tensor(abuf[a], psf[so][:, 0:512], rbuf[r], ALU.mult),
                           waits=awt, sem=s_ve, inc=1)
                bankg[so] = (s_ve, nev)
                rbuf_g[rslot] = (s_ve, nev)
                na = dma("gpsimd", gpa4[aslot],
                         s_ascr[qb * DM + h * 128: qb * DM + (h + 1) * 128, :],
                         abuf[aslot], waits=[(s_ve, nev)])
                abuf_g[aslot] = (gpa4[aslot], na)
                n_ascr_store[0] = na
            qrng_g[qslot] = (s_te, s_te.n)
            if last_exp is not None:
                ebias_guard[0] = (s_sc, last_exp)
            drain_dq(8 if "oldpace" in skip else 16)

    drain_dq(10 ** 9)

    if PH == 2:
        return _finish(nc, plan, es, fence, dma, emit, s_ld, s_gp, dumps, locals())
    fence()
    O_X2B = 0
    O_X2LT = 32768
    O_HBUF = 65536
    O_WR4 = 153600
    O_OBF = 161792
    O_SGT = 163840
    O_DSTG = 165888
    O_OSTG = 169984
    O_NXG = 174080
    O_NXH = 175104
    O_ZG = 176128
    O_ZD = 178176
    O_X2LN = 179200

    x2t = [vb(O_X2B + j * 8192, DM) for j in range(4)]

    def x2lt_t(ic):
        return vb(O_X2LT + ic * 1024, 512)
    hbuf = [vb(O_HBUF + j * 1024, 512) for j in range(GJ)]
    wr4 = [vb(O_WR4 + i * 1024, 512) for i in range(8)]
    wr4_g = [None] * 8
    obf = vb(O_OBF, 512)
    sgt = vf(O_SGT, 512)
    dstg = [vf(O_DSTG + i * 2048, 512) for i in range(2)]
    dstg_g = [None] * 2
    ostg = [vf(O_OSTG + i * 2048, 512) for i in range(2)]
    ostg_g = [None] * 2
    nxg = vb(O_NXG, 512, 32)
    nxh = vb(O_NXH, 512, GJ)
    zgt = vb(O_ZG, 512, 32)
    zut = vb(O_ZG + 1024, 512, 32)
    zdt = vb(O_ZD, 512, GJ)
    x2ln = vb(O_X2LN, DM)

    n_out_store = [0]
    obf_g = [None]
    sgt_g = [None]
    zg_guard = [None]
    zd_guard = [None]
    x2ln_guard = [None]
    for qb in range(1 if "p4bone" in skip else NQB):
        ldq = s_ld4b[qb]
        n_x2i = []
        for j in range(4):
            n_x2i.append((gpc4[j], dma("gpsimd", gpc4[j], x2t[j],
                          d_xown[qb * 512 + j * 128: qb * 512 + (j + 1) * 128, :],
                          waits=[(s_ve, s_ve.n)] if j == 0 else [])))
        for og in range(8):
            nmm = [None] * 4
            for icb in range(IC):
                slot = icb % 4
                wwt = []
                if wr4_g[slot] is not None:
                    wwt.append(wr4_g[slot])
                    wr4_g[slot] = None
                nldw = dma("sync", ring8[slot], wr4[slot],
                           s_bino[icb * 128:(icb + 1) * 128, og * 512:(og + 1) * 512],
                           waits=wwt)
                aslot = 4 + icb % 4
                awt = []
                if wr4_g[aslot] is not None:
                    awt.append(wr4_g[aslot])
                    wr4_g[aslot] = None
                nlda = dma("sync", ring8[aslot], wr4[aslot],
                           s_ascr[qb * DM + icb * 128: qb * DM + (icb + 1) * 128, :],
                           waits=awt)
                for k in range(4):
                    tws = [(ring8[slot], nldw), (ring8[aslot], nlda)] if k == 0 else []
                    if icb == 0:
                        tws = tws + bw(k)
                    nmm[k] = emit("tensor",
                                  lambda e, k=k, s=slot, a=aslot, ic=icb:
                                  e.matmul(psf[k][:, 0:512],
                                           wr4[s][:, k * 128:(k + 1) * 128],
                                           wr4[a], start=(ic == 0), stop=(ic == IC - 1)),
                                  waits=tws, sem=s_te, inc=(1 if k == 3 else 0))
                wr4_g[slot] = (s_te, nmm[3])
                wr4_g[aslot] = (s_te, nmm[3])
            ntpl = None
            for k in range(4):
                oc = og * 4 + k
                swt = [(s_te, nmm[3])]
                if obf_g[0] is not None:
                    swt.append(obf_g[0])
                    obf_g[0] = None
                nevo = emit("scalar",
                            lambda e, k=k, oc=oc:
                            e.activation(obf, psf[k][:, 0:512], AF.Copy,
                                         scale=t_svec[:, 48 + oc: 48 + oc + 1]),
                            waits=swt, sem=s_sc, inc=1)
                bankg[k] = (s_sc, nevo)
                for j in range(4):
                    b = 4 + j
                    tws = [(s_sc, nevo)]
                    if k == 0:
                        tws = tws + bw(b)
                    ntpl = emit("tensor",
                                lambda e, b=b, j=j, k=k:
                                e.transpose(psb[b][:, k * 128:(k + 1) * 128],
                                            obf[:, j * 128:(j + 1) * 128], t_identb.ap()),
                                waits=tws, sem=s_te, inc=(1 if j == 3 else 0))
                obf_g[0] = (s_te, ntpl)
            for j in range(4):
                b = 4 + j
                awt = [(s_te, ntpl), n_x2i[j]]
                nadd = emit("vector",
                            lambda e, b=b, j=j, og=og:
                            e.tensor_tensor(x2t[j][:, og * 512:(og + 1) * 512],
                                            psb[b][:, 0:512],
                                            x2t[j][:, og * 512:(og + 1) * 512], ALU.add),
                            waits=awt, sem=s_ve, inc=1)
                bankg[b] = (s_ve, nadd)
        if qb == 0 and "x2dbg" in skip:
            for j in range(4):
                dbgstore(f"x2_{j}", x2t[j], [128, DM], BF16)
        n_x2lt_ready = None
        for j in range(4):
            lw = []
            if x2ln_guard[0] is not None:
                lw.append(x2ln_guard[0])
                x2ln_guard[0] = None
            emit("vector", lambda e, j=j: e.reduce_sum(t_stat[:, 0:1], x2t[j], AX.X),
                 waits=[(s_ve, s_ve.n)], sem=s_ve, inc=1)
            emit("vector", lambda e, j=j:
                 e.scalar_tensor_tensor(x2ln, x2t[j], 1.0, x2t[j], ALU.mult, ALU.mult,
                                        accum_out=t_stat[:, 1:2]),
                 waits=lw + [(s_ve, s_ve.n)], sem=s_ve, inc=1)
            emit("vector", lambda e: e.tensor_scalar(t_stat[:, 2:3], t_stat[:, 0:1],
                                                     1.0 / DM, None, ALU.mult),
                 waits=[(s_ve, s_ve.n)], sem=s_ve, inc=1)
            emit("vector", lambda e: e.tensor_scalar(t_stat[:, 3:4], t_stat[:, 1:2],
                                                     1.0 / DM, None, ALU.mult),
                 waits=[(s_ve, s_ve.n)], sem=s_ve, inc=1)
            emit("vector", lambda e: e.tensor_tensor(t_stat[:, 4:5], t_stat[:, 2:3],
                                                     t_stat[:, 2:3], ALU.mult),
                 waits=[(s_ve, s_ve.n)], sem=s_ve, inc=1)
            emit("vector", lambda e: e.tensor_tensor(t_stat[:, 5:6], t_stat[:, 3:4],
                                                     t_stat[:, 4:5], ALU.subtract),
                 waits=[(s_ve, s_ve.n)], sem=s_ve, inc=1)
            nvar = emit("vector", lambda e: e.tensor_scalar(t_stat[:, 5:6], t_stat[:, 5:6],
                                                            0.0, None, ALU.max),
                        waits=[(s_ve, s_ve.n)], sem=s_ve, inc=1)
            nsq = emit("scalar", lambda e: e.activation(t_stat[:, 6:7], t_stat[:, 5:6],
                                                        AF.Sqrt, bias=t_eps[:, 0:1]),
                       waits=[(s_ve, nvar)], sem=s_sc, inc=1)
            emit("vector", lambda e: e.reciprocal(t_stat[:, 7:8], t_stat[:, 6:7]),
                 waits=[(s_sc, nsq), (s_ve, s_ve.n)], sem=s_ve, inc=1)
            emit("vector", lambda e: e.tensor_tensor(t_stat[:, 8:9], t_stat[:, 2:3],
                                                     t_stat[:, 7:8], ALU.mult),
                 waits=[(s_ve, s_ve.n)], sem=s_ve, inc=1)
            emit("vector", lambda e: e.tensor_scalar(t_stat[:, 9:10], t_stat[:, 8:9],
                                                     -1.0, None, ALU.mult),
                 waits=[(s_ve, s_ve.n)], sem=s_ve, inc=1)
            nap = emit("vector", lambda e, j=j:
                       e.tensor_scalar(x2ln, x2t[j], t_stat[:, 7:8], t_stat[:, 9:10],
                                       ALU.mult, ALU.add),
                       waits=[(s_ve, s_ve.n)], sem=s_ve, inc=1)
            ntp = None
            for q4 in range(8):
                b = 4 + (q4 % 2)
                tws = bw(b) + [(s_ve, nap)]
                for jj in range(4):
                    icb = q4 * 4 + jj
                    ntp = emit("tensor",
                               lambda e, b=b, jj=jj, ic=icb:
                               e.transpose(psb[b][:, jj * 128:(jj + 1) * 128],
                                           x2ln[:, ic * 128:(ic + 1) * 128], t_identb.ap()),
                               waits=(tws if jj == 0 else []), sem=s_te,
                               inc=(1 if jj == 3 else 0))
                nev = emit("vector",
                           lambda e, b=b, q4=q4, j=j:
                           e.tensor_copy(vb(O_X2LT, IC * 512)
                                         .rearrange("p (c n) -> p c n", c=IC)
                                         [:, q4 * 4:(q4 + 1) * 4, j * 128:(j + 1) * 128],
                                         psb[b][:, 0:512]),
                           waits=[(s_te, ntp)], sem=s_ve, inc=1)
                bankg[b] = (s_ve, nev)
                n_x2lt_ready = nev
            x2ln_guard[0] = (s_te, ntp)
        if qb == 0 and "x2dbg" in skip:
            dbgstore("x2lt", vb(O_X2LT, IC * 512), [128, IC * 512], BF16)
        tws = bw(4) + [(s_ve, n_x2lt_ready), (s_ldu, n_consts)]
        ntx = None
        for icb in range(IC):
            ntx = emit("tensor",
                       lambda e, ic=icb:
                       e.matmul(psf[4][0:32, 0:512], t_band32[:, 31 - ic: 63 - ic],
                                x2lt_t(ic), start=(ic == 0), stop=(ic == IC - 1)),
                       waits=(tws if icb == 0 else []), sem=s_te,
                       inc=(1 if icb == IC - 1 else 0))
        nxg_n = emit("vector",
                     lambda e: e.tensor_scalar(nxg, psf[4][0:32, 0:512], -1.0, None,
                                               ALU.mult),
                     waits=[(s_te, ntx)], sem=s_ve, inc=1)
        bankg[4] = (s_ve, nxg_n)
        for j0 in range(0, GJ, 4):
            jn = min(4, GJ - j0)
            zw = []
            if zg_guard[0] is not None:
                zw.append(zg_guard[0])
                zg_guard[0] = None
            nzg = dma("sync", zs3[0], zgt[0:32, 0:jn * 128],
                      s_zpg[:, j0 * 128:(j0 + jn) * 128], waits=zw)
            nzu = dma("sync", zs3[1], zut[0:32, 0:jn * 128],
                      s_zpg[:, INTER + j0 * 128: INTER + (j0 + jn) * 128])
            nmm = [None] * 8
            for icb in range(IC):
                slot = (icb * 2) % 8
                slot2 = (icb * 2 + 1) % 8
                wwt = []
                if wr4_g[slot] is not None:
                    wwt.append(wr4_g[slot])
                    wr4_g[slot] = None
                nldg = dma("sync", ring8[slot], wr4[slot][:, 0:jn * 128],
                           s_w1g[icb * 128:(icb + 1) * 128, j0 * 128:(j0 + jn) * 128],
                           waits=wwt)
                wwt2 = []
                if wr4_g[slot2] is not None:
                    wwt2.append(wr4_g[slot2])
                    wr4_g[slot2] = None
                nldu = dma("sync", ring8[slot2], wr4[slot2][:, 0:jn * 128],
                           s_w1g[icb * 128:(icb + 1) * 128,
                                 INTER + j0 * 128: INTER + (j0 + jn) * 128],
                           waits=wwt2)
                nu_last = None
                for kk in range(2 * jn):
                    g_half = kk < jn
                    k = kk if g_half else kk - jn
                    sl_ = slot if g_half else slot2
                    bk = k if g_half else 4 + k
                    tws = [(ring8[slot], nldg), (ring8[slot2], nldu)] if kk == 0 else []
                    if icb == 0:
                        tws = tws + bw(bk)
                        if kk == 0:
                            tws = tws + [(s_ve, n_x2lt_ready)]
                    nu_last = emit("tensor",
                                   lambda e, bk=bk, k=k, s=sl_, ic=icb:
                                   e.matmul(psf[bk][:, 0:512],
                                            wr4[s][:, k * 128:(k + 1) * 128],
                                            x2lt_t(ic), start=(ic == 0), stop=False),
                                   waits=tws, sem=s_te,
                                   inc=(1 if kk == 2 * jn - 1 else 0))
                wr4_g[slot] = (s_te, nu_last)
                wr4_g[slot2] = (s_te, nu_last)
            for k in range(jn):
                tws = [(zs3[0], nzg), (s_ve, nxg_n)] if k == 0 else []
                nmm[k] = emit("tensor",
                              lambda e, k=k:
                              e.matmul(psf[k][:, 0:512], zgt[0:32, k * 128:(k + 1) * 128],
                                       nxg, start=False, stop=True),
                              waits=tws, sem=s_te, inc=1)
            for k in range(jn):
                tws = [(zs3[1], nzu)] if k == 0 else []
                nmm[4 + k] = emit("tensor",
                                  lambda e, k=k:
                                  e.matmul(psf[4 + k][:, 0:512],
                                           zut[0:32, k * 128:(k + 1) * 128],
                                           nxg, start=False, stop=True),
                                  waits=tws, sem=s_te, inc=1)
            zg_guard[0] = (s_te, nmm[4 + jn - 1])
            for k in range(jn):
                j = j0 + k
                sw = [(s_te, nmm[k])]
                if sgt_g[0] is not None:
                    sw.append(sgt_g[0])
                    sgt_g[0] = None
                nsg = emit("scalar",
                           lambda e, k=k: e.activation(sgt, psf[k][:, 0:512], AF.Silu),
                           waits=sw, sem=s_sc, inc=1)
                bankg[k] = (s_sc, nsg)
                nh = emit("vector",
                          lambda e, k=k, j=j:
                          e.tensor_tensor(hbuf[j], psf[4 + k][:, 0:512], sgt, ALU.mult),
                          waits=[(s_sc, nsg), (s_te, nmm[4 + k])], sem=s_ve, inc=1)
                bankg[4 + k] = (s_ve, nh)
                sgt_g[0] = (s_ve, nh)
        n_h_ready = s_ve.n
        if qb == 0 and "x2dbg" in skip:
            dbgstore("hb", vb(O_HBUF, GJ * 512), [128, GJ * 512], BF16)
            dbgstore("nxg", nxg, [32, 512], BF16)
        tws = bw(4) + [(s_ve, n_h_ready), (s_ldu, n_consts)]
        ntx = None
        for kc in range(GJ):
            ntx = emit("tensor",
                       lambda e, kc=kc:
                       e.matmul(psf[4][0:GJ, 0:512], t_band86[:, 85 - kc: 171 - kc],
                                hbuf[kc], start=(kc == 0), stop=(kc == GJ - 1)),
                       waits=(tws if kc == 0 else []), sem=s_te,
                       inc=(1 if kc == GJ - 1 else 0))
        nxh_n = emit("vector",
                     lambda e: e.tensor_scalar(nxh, psf[4][0:GJ, 0:512], -1.0, None,
                                               ALU.mult),
                     waits=[(s_te, ntx)], sem=s_ve, inc=1)
        bankg[4] = (s_ve, nxh_n)
        for og in range(8):
            zw = []
            if zd_guard[0] is not None:
                zw.append(zd_guard[0])
                zd_guard[0] = None
            nzd = dma("sync", zs3[2], zdt[0:GJ, :], s_zpd[:, og * 512:(og + 1) * 512],
                      waits=zw)
            nmm = [None] * 4
            for kc in range(GJ):
                slot = kc % 8
                wwt = []
                if wr4_g[slot] is not None:
                    wwt.append(wr4_g[slot])
                    wr4_g[slot] = None
                nld = dma("sync", ring8[slot], wr4[slot],
                          s_w1d[kc * 128:(kc + 1) * 128, og * 512:(og + 1) * 512],
                          waits=wwt)
                nd_last = None
                for k in range(4):
                    tws = [(ring8[slot], nld)] if k == 0 else []
                    if kc == 0:
                        tws = tws + bw(k) + [(s_ve, nxh_n)]
                    nd_last = emit("tensor",
                                   lambda e, k=k, s=slot, kc=kc:
                                   e.matmul(psf[k][:, 0:512],
                                            wr4[s][:, k * 128:(k + 1) * 128],
                                            hbuf[kc], start=(kc == 0), stop=False),
                                   waits=tws, sem=s_te, inc=(1 if k == 3 else 0))
                wr4_g[slot] = (s_te, nd_last)
            for k in range(4):
                tws = [(zs3[2], nzd)] if k == 0 else []
                nmm[k] = emit("tensor",
                              lambda e, k=k:
                              e.matmul(psf[k][:, 0:512], zdt[0:GJ, k * 128:(k + 1) * 128],
                                       nxh, start=False, stop=True),
                              waits=tws, sem=s_te, inc=1)
            zd_guard[0] = (s_te, nmm[3])
            ntp_fin = None
            for k in range(4):
                dslot = k % 2
                dwt = [(s_te, nmm[k])]
                if dstg_g[dslot] is not None:
                    dwt.append(dstg_g[dslot])
                    dstg_g[dslot] = None
                nevd = emit("vector",
                            lambda e, k=k, d=dslot: e.tensor_copy(dstg[d], psf[k][:, 0:512]),
                            waits=dwt, sem=s_ve, inc=1)
                bankg[k] = (s_ve, nevd)
                ntp = None
                for j in range(4):
                    b = 4 + j
                    tws = [(s_ve, nevd)]
                    if k == 0:
                        tws = tws + bw(b)
                    ntp = emit("tensor",
                               lambda e, b=b, j=j, k=k, d=dslot:
                               e.transpose(psf[b][:, k * 128:(k + 1) * 128],
                                           dstg[d][:, j * 128:(j + 1) * 128],
                                           t_identf.ap()),
                               waits=tws, sem=s_te, inc=(1 if j == 3 else 0))
                dstg_g[dslot] = (s_te, ntp)
                ntp_fin = ntp
            for j in range(4):
                b = 4 + j
                oslot = j % 2
                owt = [(s_te, ntp_fin)]
                if ostg_g[oslot] is not None:
                    owt.append(ostg_g[oslot])
                    ostg_g[oslot] = None
                nout = emit("vector",
                            lambda e, b=b, j=j, og=og, o=oslot:
                            e.tensor_tensor(ostg[o], psf[b][:, 0:512],
                                            x2t[j][:, og * 512:(og + 1) * 512], ALU.add),
                            waits=owt, sem=s_ve, inc=1)
                bankg[b] = (s_ve, nout)
                ng = dma("gpsimd", gpo2[oslot],
                         d_out[qb * 512 + j * 128: qb * 512 + (j + 1) * 128,
                               og * 512:(og + 1) * 512],
                         ostg[oslot], waits=[(s_ve, nout)])
                ostg_g[oslot] = (gpo2[oslot], ng)
                n_out_store[0] = ng

    g0, g1 = gpo2[0].n, gpo2[1].n
    plan["gpsimd"].append(lambda e, a=g0, b=g1:
                          (e.wait_ge(gpo2[0].h, a), e.wait_ge(gpo2[1].h, b)))
    if dumps:
        return _finish(nc, plan, es, fence, dma, emit, s_ld, s_gp, dumps, locals())

    with nc.Block() as block:
        @block.sync
        def _(e):
            for f in plan["sync"]:
                f(e)

        @block.scalar
        def _(e):
            for f in plan["scalar"]:
                f(e)

        @block.vector
        def _(e):
            for f in plan["vector"]:
                f(e)

        @block.tensor
        def _(e):
            for f in plan["tensor"]:
                f(e)

        @block.gpsimd
        def _(e):
            for f in plan["gpsimd"]:
                f(e)

    es.close()
    return nc


def _band(w, col):
    a = np.zeros((128, w), dtype=np.float32)
    a[:, col] = 1.0
    return a.astype(ml_dtypes.bfloat16)


_NC_CACHE = [None]


def kernel(**inputs):
    hs = np.asarray(inputs["L0_attn_hidden_states"])
    ins = {
        "qw": np.ascontiguousarray(inputs["L0_attn_q_weight"]),
        "kw": np.ascontiguousarray(inputs["L0_attn_k_weight"]),
        "vw": np.ascontiguousarray(inputs["L0_attn_v_weight"]),
        "ow": np.ascontiguousarray(inputs["L0_attn_o_weight"]),
        "qs": np.ascontiguousarray(inputs["L0_attn_q_scale"]),
        "ks": np.ascontiguousarray(inputs["L0_attn_k_scale"]),
        "vs": np.ascontiguousarray(inputs["L0_attn_v_scale"]),
        "os_": np.ascontiguousarray(inputs["L0_attn_o_scale"]),
        "gqw": np.ascontiguousarray(inputs["L0_ffn_qweight_gu"]),
        "gqz": np.ascontiguousarray(inputs["L0_ffn_qzeros_gu"]),
        "gsc": np.ascontiguousarray(inputs["L0_ffn_scales_gu"]),
        "dqw": np.ascontiguousarray(inputs["L0_ffn_qweight_down"]),
        "dqz": np.ascontiguousarray(inputs["L0_ffn_qzeros_down"]),
        "dsc": np.ascontiguousarray(inputs["L0_ffn_scales_down"]),
        "identf": np.eye(128, dtype=np.float32),
        "band32": _band(63, 31),
        "band86": _band(171, 85),
        "shiftv": ((np.arange(128, dtype=np.uint32) % 8) * 4).reshape(128, 1),
    }
    in_maps = []
    for c in range(8):
        b, h = c // 2, c % 2
        m = dict(ins)
        m["x_own"] = np.ascontiguousarray(hs[b, h * TOWN:(h + 1) * TOWN])
        m["x_oth"] = np.ascontiguousarray(hs[b, (1 - h) * TOWN:(2 - h) * TOWN])
        in_maps.append(m)
    if _NC_CACHE[0] is None:
        _NC_CACHE[0] = build_nc()
    res = run_bass_kernel_spmd(_NC_CACHE[0], in_maps, core_ids=list(range(8)))
    out = np.empty((4, SEQ, DM), dtype=np.float32)
    for c in range(8):
        b, h = c // 2, c % 2
        out[b, h * TOWN:(h + 1) * TOWN] = res.results[c]["out"]
    return out
